# version 01
# baseline (speedup 1.0000x reference)
import numpy as np
from contextlib import ExitStack
import concourse.bass as bass
import concourse.mybir as mybir
from concourse.bass_utils import run_bass_kernel_spmd

F32 = mybir.dt.float32
BF16 = mybir.dt.bfloat16
AF = mybir.ActivationFunctionType
ALU = mybir.AluOpType
AX = mybir.AxisListType

NCORE = 8
BT = 16384
TC = BT // NCORE
D = 1024


class Emit:
    ENGS = ("sync", "vector", "scalar", "tensor", "gpsimd")

    def __init__(self, nc, sem):
        self.nc = nc
        self.sem = sem
        self.tot = 0
        self.prog = []

    def op(self, eng, f, *a, dma=False, **kw):
        w = 16 if dma else 1
        self.prog.append((eng, f, a, kw, self.tot, w))
        self.tot += w

    def dma(self, dst, src, **kw):
        self.op("sync", self.nc.sync.dma_start, dst, src, dma=True, **kw)

    def dma2(self, dst, src, **kw):
        self.op("gpsimd", self.nc.gpsimd.dma_start, dst, src, dma=True, **kw)

    def v(self, name, *a, **kw):
        self.op("vector", getattr(self.nc.vector, name), *a, **kw)

    def s(self, name, *a, **kw):
        self.op("scalar", getattr(self.nc.scalar, name), *a, **kw)

    def t(self, name, *a, **kw):
        self.op("tensor", getattr(self.nc.tensor, name), *a, **kw)

    def g(self, name, *a, **kw):
        self.op("gpsimd", getattr(self.nc.gpsimd, name), *a, **kw)

    def mm(self, out, lhsT, rhs, start, stop):
        self.op("tensor", self.nc.tensor.matmul, out, lhsT, rhs, start=start, stop=stop)

    def replay(self, block):
        for eng_name in self.ENGS:
            ops = [p for p in self.prog if p[0] == eng_name]
            if not ops:
                continue

            def body(eng, ops=ops):
                for (_, f, a, kw, wv, inc) in ops:
                    if wv > 0:
                        eng.wait_ge(self.sem, wv)
                    f(*a, **kw).then_inc(self.sem, inc)
            getattr(block, eng_name)(body)

        def tail(eng):
            eng.wait_ge(self.sem, self.tot)
        block.gpsimd(tail)


def bcast_row(E, nc, ps_pool, out_sb, row_sb, ones1):
    W = row_sb.ap[-1][1] if hasattr(row_sb, "ap") else None
    W = row_sb.ap[-1][1]
    off = 0
    while off < W:
        w = min(512, W - off)
        E.mm(ps_pool[:, 0:w], ones1, row_sb[:, off:off + w], True, True)
        E.v("tensor_copy", out_sb[:, off:off + w], ps_pool[:, 0:w])
        off += w


def build_p2():
    nc = bass.Bass("TRN2", target_bir_lowering=False)
    y_in = nc.dram_tensor("y", [TC, D], F32, kind="ExternalInput").ap()
    x_in = nc.dram_tensor("x", [TC, D], F32, kind="ExternalInput").ap()
    lna_in = nc.dram_tensor("lna", [1, D], F32, kind="ExternalInput").ap()
    lnb_in = nc.dram_tensor("lnb", [1, D], F32, kind="ExternalInput").ap()
    ones_in = nc.dram_tensor("ones1", [1, 128], F32, kind="ExternalInput").ap()
    r_out = nc.dram_tensor("r", [TC, D], F32, kind="ExternalOutput").ap()

    with ExitStack() as es:
        sb = lambda n, shp, dt=F32: es.enter_context(nc.sbuf_tensor(n, shp, dt))
        S = es.enter_context(nc.semaphore("S"))
        yt = sb("yt", [128, D])
        xt = sb("xt", [128, D])
        sq = sb("sq", [128, D])
        yc = sb("yc", [128, D])
        lnaB = sb("lnaB", [128, D])
        lnbB = sb("lnbB", [128, D])
        rows = sb("rows", [1, 2 * D])
        ones1 = sb("ones1s", [1, 128])
        scr = sb("scr", [128, 4])
        ps = es.enter_context(nc.psum_tensor("ps", [128, 512], F32))
        E = Emit(nc, S)

        E.dma(rows[:, 0:D], lna_in)
        E.dma(rows[:, D:2 * D], lnb_in)
        E.dma(ones1[:], ones_in)
        bcast_row(E, nc, ps, lnaB, rows[:, 0:D], ones1[:])
        bcast_row(E, nc, ps, lnbB, rows[:, D:2 * D], ones1[:])

        ntile = TC // 128
        for i in range(ntile):
            E.dma(yt[:], y_in[i * 128:(i + 1) * 128, :])
            E.dma(xt[:], x_in[i * 128:(i + 1) * 128, :])
            E.v("reduce_sum", scr[:, 0:1], yt[:], AX.X)
            E.s("mul", scr[:, 1:2], scr[:, 0:1], 1.0 / D)
            E.v("tensor_scalar", yc[:], yt[:], scr[:, 1:2], None, ALU.subtract)
            E.s("activation", sq[:], yc[:], AF.Square, 0.0, 1.0, accum_out=scr[:, 0:1])
            E.s("activation", scr[:, 2:3], scr[:, 0:1], AF.Sqrt, 0.0, 1.0 / (D - 1))
            E.v("tensor_scalar_add", scr[:, 2:3], scr[:, 2:3], 1e-12)
            E.v("reciprocal", scr[:, 3:4], scr[:, 2:3])
            E.v("tensor_scalar", yc[:], yc[:], scr[:, 3:4], None, ALU.mult)
            E.v("tensor_mul", yc[:], yc[:], lnaB[:])
            E.v("tensor_add", yc[:], yc[:], lnbB[:])
            E.v("tensor_add", yc[:], yc[:], xt[:])
            E.dma(r_out[i * 128:(i + 1) * 128, :], yc[:])

        blk = es.enter_context(nc.Block())
        E.replay(blk)
    return nc


def p2_inmaps(y_full, x_full, lna, lnb):
    maps = []
    for c in range(NCORE):
        sl = slice(c * TC, (c + 1) * TC)
        maps.append({
            "y": np.ascontiguousarray(y_full[sl]),
            "x": np.ascontiguousarray(x_full[sl]),
            "lna": lna.reshape(1, D),
            "lnb": lnb.reshape(1, D),
            "ones1": np.ones((1, 128), np.float32),
        })
    return maps


_P2 = None


def run_p2(y_full, x_full, lna, lnb):
    global _P2
    if _P2 is None:
        _P2 = build_p2()
    res = run_bass_kernel_spmd(_P2, p2_inmaps(y_full, x_full, lna, lnb), core_ids=list(range(NCORE)))
    return np.concatenate([r["r"] for r in res.results], axis=0)


def emit_ln_block(E, nc, src_ap, row0, xt, sq, scr, xnbf, psT, xnT_dst, identb, epscol):
    E.dma(xt[:], src_ap[row0:row0 + 128, :])
    E.v("reduce_sum", scr[:, 0:1], xt[:], AX.X)
    E.s("mul", scr[:, 1:2], scr[:, 0:1], 1.0 / D)
    E.v("tensor_scalar", xt[:], xt[:], scr[:, 1:2], None, ALU.subtract)
    E.s("activation", sq[:], xt[:], AF.Square, 0.0, 1.0, accum_out=scr[:, 0:1])
    E.s("activation", scr[:, 2:3], scr[:, 0:1], AF.Ln, epscol, 1.0 / D)
    E.s("activation", scr[:, 3:4], scr[:, 2:3], AF.Exp, 0.0, -0.5)
    E.v("tensor_scalar", xnbf[:], xt[:], scr[:, 3:4], None, ALU.mult)
    for dc in range(8):
        E.t("transpose", psT[:, dc * 128:(dc + 1) * 128], xnbf[:, dc * 128:(dc + 1) * 128], identb[:])
    E.v("tensor_copy", xnT_dst, psT[:])


def build_p1():
    nc = bass.Bass("TRN2", target_bir_lowering=False)
    L = 4096
    LH = 2048
    x_in = nc.dram_tensor("x", [L, D], F32, kind="ExternalInput").ap()
    xpwT_in = nc.dram_tensor("xpwT", [D, 96], F32, kind="ExternalInput").ap()
    dtwT_in = nc.dram_tensor("dtwT", [64, 512], F32, kind="ExternalInput").ap()
    dtb_in = nc.dram_tensor("dtb", [512], F32, kind="ExternalInput").ap()
    alog_in = nc.dram_tensor("alog", [512, 16], F32, kind="ExternalInput").ap()
    dpar_in = nc.dram_tensor("dpar", [512], F32, kind="ExternalInput").ap()
    sel_in = nc.dram_tensor("sel", [32, 32, 128], F32, kind="ExternalInput").ap()
    identb_in = nc.dram_tensor("identb", [128, 128], BF16, kind="ExternalInput").ap()
    identf_in = nc.dram_tensor("identf", [128, 128], F32, kind="ExternalInput").ap()
    y_out = nc.dram_tensor("y", [L, 512], F32, kind="ExternalOutput").ap()

    with ExitStack() as es:
        sb = lambda n, shp, dt=F32: es.enter_context(nc.sbuf_tensor(n, shp, dt))
        S = es.enter_context(nc.semaphore("S"))
        identb = sb("identb_s", [128, 128], BF16)
        identf = sb("identf_s", [128, 128], F32)
        sel = sb("sel_s", [32, 32 * 128])
        xpw_bf = sb("xpw_bf", [128, 8 * 96], BF16)
        dtw_bf = sb("dtw_bf", [64, 512], BF16)
        cols = sb("cols", [128, 16])
        A_sb = sb("A_sb", [128, 64])
        hcar = sb("hcar", [128, 64])
        kcol = sb("kcol", [128, 4])
        scr = sb("scr", [128, 8])
        xt = sb("xt", [128, D])
        sq = sb("sq", [128, D])
        xnbf = sb("xnbf", [128, D], BF16)
        xnT = sb("xnT", [128, 8 * LH], BF16)
        dr_bf = sb("dr_bf", [64, LH], BF16)
        BC32 = sb("BC32", [32, LH])
        deltaT = sb("deltaT", [128, 4 * LH])
        BmB = sb("BmB", [128, LH])
        CbB = sb("CbB", [128, LH])
        dAt = sb("dAt", [128, LH])
        BXt = sb("BXt", [128, LH])
        ht = sb("ht", [128, LH])
        ut = sb("ut", [128, LH])
        yT = sb("yT", [128, 4 * LH])
        ytile = sb("ytile", [128, 512])
        dbcT = sb("dbcT", [96, LH])
        ps = es.enter_context(nc.psum_tensor("ps", [128, 512], F32))
        psT = es.enter_context(nc.psum_tensor("psT", [128, 1024], BF16))
        psF = es.enter_context(nc.psum_tensor("psF", [128, 512], F32))

        E = Emit(nc, S)
        E.v("memset", kcol[:, 0:1], 1e-5)
        E.v("memset", kcol[:, 1:2], 1.0)
        E.dma(identb[:], identb_in)
        E.dma(identf[:], identf_in)
        E.dma(sel[:], sel_in.rearrange("k j p -> k (j p)"))
        E.dma(bass.AP(xt, 0, [[1024, 128], [96, 8], [1, 96]]),
              xpwT_in.rearrange("(c p) r -> p c r", p=128))
        E.s("copy", xpw_bf[:], xt[:, 0:8 * 96])
        E.dma(sq[0:64, 0:512], dtwT_in)
        E.s("copy", dtw_bf[:], sq[0:64, 0:512])
        E.dma(cols[:, 0:4], dtb_in.rearrange("(c p) -> p c", p=128),
              allow_slow_non_contiguous=True)
        E.dma(cols[:, 4:8], dpar_in.rearrange("(c p) -> p c", p=128),
              allow_slow_non_contiguous=True)
        E.dma(bass.AP(A_sb, 0, [[64, 128], [16, 4], [1, 16]]),
              alog_in.rearrange("(c p) n -> p c n", p=128))
        E.s("activation", A_sb[:], A_sb[:], AF.Exp, 0.0, 1.0)
        E.v("tensor_scalar_mul", A_sb[:], A_sb[:], -1.0)

        dh = None
        for lh in range(2):
            l0 = lh * LH
            for i in range(LH // 128):
                dst = bass.AP(xnT, i * 128, [[8 * LH, 128], [LH, 8], [1, 128]])
                emit_ln_block(E, nc, x_in, l0 + i * 128, xt, sq, scr, xnbf, psT, dst, identb,
                              kcol[:, 0:1])
            for lg in range(LH // 512):
                for c in range(8):
                    E.mm(ps[0:96, 0:512],
                         bass.AP(xpw_bf, c * 96, [[8 * 96, 128], [1, 96]]),
                         bass.AP(xnT, c * LH + lg * 512, [[8 * LH, 128], [1, 512]]),
                         c == 0, c == 7)
                E.v("tensor_copy", dbcT[:, lg * 512:(lg + 1) * 512], ps[0:96, 0:512])
            E.s("copy", dr_bf[:], dbcT[0:64, :])
            E.v("tensor_copy", BC32[:], dbcT[64:96, :])
            for dc in range(4):
                for lg in range(LH // 512):
                    E.mm(ps[:, 0:512], dtw_bf[:, dc * 128:(dc + 1) * 128],
                         dr_bf[:, lg * 512:(lg + 1) * 512], True, True)
                    dslice = deltaT[:, dc * LH + lg * 512: dc * LH + (lg + 1) * 512]
                    E.v("tensor_scalar", dAt[:, 0:512], ps[:, 0:512], cols[:, dc:dc + 1], None, ALU.add)
                    E.s("activation", BXt[:, 0:512], dAt[:, 0:512], AF.Abs, 0.0, 1.0)
                    E.s("activation", BXt[:, 0:512], BXt[:, 0:512], AF.Exp, 0.0, -1.0)
                    E.s("activation", BXt[:, 0:512], BXt[:, 0:512], AF.Ln, kcol[:, 1:2], 1.0)
                    E.v("tensor_scalar_max", dAt[:, 0:512], dAt[:, 0:512], 0.0)
                    E.v("tensor_add", dslice, dAt[:, 0:512], BXt[:, 0:512])
            for n in range(16):
                for (dst, j) in ((BmB, n), (CbB, 16 + n)):
                    for lg in range(LH // 512):
                        E.mm(ps[:, 0:512], bass.AP(sel, j * 128, [[32 * 128, 32], [1, 128]]),
                             BC32[:, lg * 512:(lg + 1) * 512], True, True)
                        E.v("tensor_copy", dst[:, lg * 512:(lg + 1) * 512], ps[:, 0:512])
                for dc in range(4):
                    dsl = deltaT[:, dc * LH:(dc + 1) * LH]
                    E.v("tensor_scalar", dAt[:], dsl, A_sb[:, dc * 16 + n:dc * 16 + n + 1], None, ALU.mult)
                    E.s("activation", dAt[:], dAt[:], AF.Exp, 0.0, 1.0)
                    xn_dc = bass.AP(xnT, (XNT_DC_BASE[dc]) * LH, [[8 * LH, 128], [1, LH]])
                    E.v("tensor_mul", ut[:], dsl, xn_dc)
                    E.v("tensor_mul", BXt[:], ut[:], BmB[:])
                    init = 0.0 if lh == 0 else hcar[:, n * 4 + dc:n * 4 + dc + 1]
                    E.v("tensor_tensor_scan", ht[:], dAt[:], BXt[:], init, ALU.mult, ALU.add)
                    E.v("tensor_copy", hcar[:, n * 4 + dc:n * 4 + dc + 1], ht[:, LH - 1:LH])
                    E.v("tensor_mul", ht[:], ht[:], CbB[:])
                    if n == 0:
                        E.v("tensor_copy", yT[:, dc * LH:(dc + 1) * LH], ht[:])
                    else:
                        E.v("tensor_add", yT[:, dc * LH:(dc + 1) * LH], yT[:, dc * LH:(dc + 1) * LH], ht[:])
            for dc in range(4):
                xn_dc = bass.AP(xnT, (XNT_DC_BASE[dc]) * LH, [[8 * LH, 128], [1, LH]])
                E.op("vector", nc.vector.scalar_tensor_tensor, yT[:, dc * LH:(dc + 1) * LH],
                     xn_dc, cols[:, 4 + dc:5 + dc], yT[:, dc * LH:(dc + 1) * LH], ALU.mult, ALU.add)
            for tt in range(LH // 128):
                for dc in range(4):
                    E.t("transpose", psF[:, dc * 128:(dc + 1) * 128],
                        yT[:, dc * LH + tt * 128: dc * LH + (tt + 1) * 128], identf[:])
                E.v("tensor_copy", ytile[:], psF[:])
                E.dma(y_out[l0 + tt * 128: l0 + (tt + 1) * 128, :], ytile[:])

        blk = es.enter_context(nc.Block())
        E.replay(blk)
    return nc


XNT_DC_BASE = [0, 1, 2, 3]


def p1_inmaps(x_full, xpw, dtw, dtb, alog, dpar):
    import ml_dtypes
    sel = np.zeros((32, 32, 128), np.float32)
    for j in range(32):
        sel[j, j, :] = 1.0
    identb = np.eye(128, dtype=np.float32).astype(ml_dtypes.bfloat16)
    identf = np.eye(128, dtype=np.float32)
    maps = []
    for c in range(NCORE):
        b, dh = c // 2, c % 2
        dsl = slice(dh * 512, (dh + 1) * 512)
        xb = x_full[b]
        if dh == 1:
            xb = np.concatenate([xb[:, 512:], xb[:, :512]], axis=1)
        xpw_r = xpw if dh == 0 else np.concatenate([xpw[:, 512:], xpw[:, :512]], axis=1)
        maps.append({
            "x": np.ascontiguousarray(xb),
            "xpwT": np.ascontiguousarray(xpw_r.T),
            "dtwT": np.ascontiguousarray(dtw[dsl, :].T),
            "dtb": np.ascontiguousarray(dtb[dsl]),
            "alog": np.ascontiguousarray(alog[dsl]),
            "dpar": np.ascontiguousarray(dpar[dsl]),
            "sel": sel,
            "identb": identb,
            "identf": identf,
        })
    return maps


_P1 = None


def run_p1(x_full, xpw, dtw, dtb, alog, dpar):
    global _P1
    if _P1 is None:
        _P1 = build_p1()
    res = run_bass_kernel_spmd(_P1, p1_inmaps(x_full, xpw, dtw, dtb, alog, dpar),
                               core_ids=list(range(NCORE)))
    y = np.zeros((4, 4096, 1024), np.float32)
    for c in range(NCORE):
        b, dh = c // 2, c % 2
        y[b, :, dh * 512:(dh + 1) * 512] = res.results[c]["y"]
    return y


def build_moe():
    nc = bass.Bass("TRN2", target_bir_lowering=False)
    T = TC
    x_in = nc.dram_tensor("x", [T, D], F32, kind="ExternalInput").ap()
    rwT_in = nc.dram_tensor("rwT", [D, 16], F32, kind="ExternalInput").ap()
    eb_in = nc.dram_tensor("eb", [1, 16], F32, kind="ExternalInput").ap()
    gwT_in = nc.dram_tensor("gwT", [16, D, 512], F32, kind="ExternalInput").ap()
    uwT_in = nc.dram_tensor("uwT", [16, D, 512], F32, kind="ExternalInput").ap()
    dwT_in = nc.dram_tensor("dwT", [16, 512, D], F32, kind="ExternalInput").ap()
    sgwT_in = nc.dram_tensor("sgwT", [D, D], F32, kind="ExternalInput").ap()
    suwT_in = nc.dram_tensor("suwT", [D, D], F32, kind="ExternalInput").ap()
    sdwT_in = nc.dram_tensor("sdwT", [D, D], F32, kind="ExternalInput").ap()
    eps_in = nc.dram_tensor("epsB", [128, 16], F32, kind="ExternalInput").ap()
    sel16_in = nc.dram_tensor("sel16", [16, 16, 128], F32, kind="ExternalInput").ap()
    ones_in = nc.dram_tensor("ones1", [1, 128], F32, kind="ExternalInput").ap()
    identb_in = nc.dram_tensor("identb", [128, 128], BF16, kind="ExternalInput").ap()
    identf_in = nc.dram_tensor("identf", [128, 128], F32, kind="ExternalInput").ap()
    r_out = nc.dram_tensor("r", [T, D], F32, kind="ExternalOutput").ap()

    with ExitStack() as es:
        sb = lambda n, shp, dt=F32: es.enter_context(nc.sbuf_tensor(n, shp, dt))
        S = es.enter_context(nc.semaphore("S"))
        identb = sb("identb_s", [128, 128], BF16)
        identf = sb("identf_s", [128, 128], F32)
        ones1 = sb("ones1_s", [1, 128])
        sel16 = sb("sel16_s", [16, 16 * 128], BF16)
        epsB = sb("epsB_s", [128, 16])
        ebB = sb("ebB", [128, 16])
        kcol = sb("kcol", [128, 4])
        scr = sb("scr", [128, 8])
        xt = sb("xt", [128, D])
        sq = sb("sq", [128, D])
        xnbf = sb("xnbf", [128, D], BF16)
        xnT = sb("xnT", [128, 8 * T], BF16)
        combT = sb("combT", [16, T], BF16)
        routedT = sb("routedT", [128, 8 * T])
        ssig = sb("ssig", [128, 16])
        sc_t = sb("sc_t", [128, 16])
        w32 = sb("w32", [128, 32])
        w6 = sb("w6", [128, 8])
        gs = sb("gs", [128, 4])
        smask = sb("smask", [128, 16])
        maskedp = sb("maskedp", [128, 16])
        work = sb("work", [128, 16])
        sel4 = sb("sel4", [128, 16])
        outt = sb("outt", [128, D])
        ps = es.enter_context(nc.psum_tensor("ps", [128, 512], F32))
        ps2 = es.enter_context(nc.psum_tensor("ps2", [128, 512], F32))
        psr = es.enter_context(nc.psum_tensor("psr", [128, 512], F32))
        psT = es.enter_context(nc.psum_tensor("psT", [128, 1024], BF16))
        psF = es.enter_context(nc.psum_tensor("psF", [128, 512], F32))
        psM = es.enter_context(nc.psum_tensor("psM", [128, 16], F32))

        E = Emit(nc, S)
        E.v("memset", kcol[:, 0:1], 1e-5)
        E.dma(identb[:], identb_in)
        E.dma(identf[:], identf_in)
        E.dma(ones1[:], ones_in)
        E.dma(routedT[0:16, 0:16 * 128], sel16_in.rearrange("k j p -> k (j p)"))
        E.s("copy", sel16[:], routedT[0:16, 0:16 * 128])
        E.dma(epsB[:], eps_in)
        rw = sb("rw", [128, 8 * 16])
        E.dma(bass.AP(rw, 0, [[128, 128], [16, 8], [1, 16]]),
              rwT_in.rearrange("(c p) e -> p c e", p=128))
        rwb = sb("rwb", [128, 8 * 16], BF16)
        E.s("copy", rwb[:], rw[:])
        ebrow = sb("ebrow", [1, 16])
        E.dma(ebrow[:], eb_in)
        E.mm(psM[:, 0:16], ones1[:], ebrow[:], True, True)
        E.v("tensor_copy", ebB[:], psM[:, 0:16])

        ntile = T // 128
        for i in range(ntile):
            dst = bass.AP(xnT, i * 128, [[8 * T, 128], [T, 8], [1, 128]])
            emit_ln_block(E, nc, x_in, i * 128, xt, sq, scr, xnbf, psT, dst, identb, kcol[:, 0:1])
            for c in range(8):
                E.mm(psM[:, 0:16], bass.AP(xnT, c * T + i * 128, [[8 * T, 128], [1, 128]]),
                     rwb[:, c * 16:(c + 1) * 16], c == 0, c == 7)
            E.s("activation", ssig[:], psM[:, 0:16], AF.Sigmoid, 0.0, 1.0)
            E.v("tensor_add", sc_t[:], ssig[:], ebB[:])
            pairs = [(0, 1), (0, 2), (0, 3), (1, 2), (1, 3), (2, 3)]
            for pi, (a, b) in enumerate(pairs):
                apa = bass.AP(sc_t, a, [[16, 128], [4, 4]])
                apb = bass.AP(sc_t, b, [[16, 128], [4, 4]])
                E.v("tensor_tensor", w32[:, pi * 4:(pi + 1) * 4], apa, apb, ALU.add)
            E.v("tensor_max", gs[:], w32[:, 0:4], w32[:, 4:8])
            E.v("tensor_max", gs[:], gs[:], w32[:, 8:12])
            E.v("tensor_max", gs[:], gs[:], w32[:, 12:16])
            E.v("tensor_max", gs[:], gs[:], w32[:, 16:20])
            E.v("tensor_max", gs[:], gs[:], w32[:, 20:24])
            for pi, (a, b) in enumerate(pairs):
                E.v("tensor_tensor", w6[:, pi:pi + 1], gs[:, a:a + 1], gs[:, b:b + 1], ALU.min)
            E.v("tensor_max", w6[:, 6:7], w6[:, 0:1], w6[:, 1:2])
            E.v("tensor_max", w6[:, 6:7], w6[:, 6:7], w6[:, 2:3])
            E.v("tensor_max", w6[:, 6:7], w6[:, 6:7], w6[:, 3:4])
            E.v("tensor_max", w6[:, 6:7], w6[:, 6:7], w6[:, 4:5])
            E.v("tensor_max", w6[:, 6:7], w6[:, 6:7], w6[:, 5:6])
            E.v("tensor_scalar", gs[:], gs[:], w6[:, 6:7], None, ALU.is_ge)
            E.v("tensor_copy", bass.AP(smask, 0, [[16, 128], [4, 4], [1, 4]]),
                bass.AP(gs, 0, [[4, 128], [1, 4], [0, 4]]))
            E.v("tensor_mul", maskedp[:], sc_t[:], smask[:])
            E.v("tensor_sub", maskedp[:], maskedp[:], epsB[:])
            E.v("tensor_copy", work[:], maskedp[:])
            for k in range(4):
                E.v("reduce_max", scr[:, 4:5], work[:], AX.X)
                if k < 3:
                    E.v("tensor_scalar", sel4[:], work[:], scr[:, 4:5], None, ALU.is_equal)
                    E.op("vector", nc.vector.scalar_tensor_tensor, work[:], sel4[:], -1e30,
                         work[:], ALU.mult, ALU.add)
            E.v("tensor_scalar", sel4[:], maskedp[:], scr[:, 4:5], None, ALU.is_ge)
            E.v("tensor_mul", sel4[:], sel4[:], ssig[:])
            E.v("reduce_sum", scr[:, 5:6], sel4[:], AX.X)
            E.v("tensor_scalar_add", scr[:, 5:6], scr[:, 5:6], 1e-20)
            E.v("reciprocal", scr[:, 6:7], scr[:, 5:6])
            E.v("tensor_scalar", sel4[:], sel4[:], scr[:, 6:7], 2.5, ALU.mult, op1=ALU.mult)
            E.t("transpose", psF[0:16, 0:128], sel4[:], identf[:])
            E.v("tensor_copy", combT[:, i * 128:(i + 1) * 128], psF[0:16, 0:128])

        def stage_load(E, stage, wbf, src_re, csize, ccount):
            per = 2048 // csize
            for c0 in range(0, ccount, per):
                n = min(per, ccount - c0)
                dst = bass.AP(stage, 0, [[2048, 128], [csize, n], [1, csize]])
                E.dma(dst, src_re[:, c0:c0 + n, :])
                E.s("copy", wbf[:, c0 * csize:(c0 + n) * csize], stage[:, 0:n * csize])

        with nc.sbuf_tensor("stage", [128, 2048], F32) as stage, \
             nc.sbuf_tensor("gw_bf", [128, 8 * D], BF16) as gw_bf, \
             nc.sbuf_tensor("uw_bf", [128, 8 * D], BF16) as uw_bf, \
             nc.sbuf_tensor("dw_bf", [128, 8 * D], BF16) as dw_bf, \
             nc.sbuf_tensor("gsil", [128, 8 * 512], BF16) as gsil, \
             nc.sbuf_tensor("h2", [128, 8 * 512], BF16) as h2:
            for e in range(16):
                stage_load(E, stage, gw_bf,
                           gwT_in[e].rearrange("(c p) i -> p c i", p=128), 512, 8)
                stage_load(E, stage, uw_bf,
                           uwT_in[e].rearrange("(c p) i -> p c i", p=128), 512, 8)
                stage_load(E, stage, dw_bf,
                           dwT_in[e].rearrange("(c p) h -> p c h", p=128), D, 4)
                for lg in range(T // 512):
                    tsl = slice(lg * 512, (lg + 1) * 512)
                    for ic in range(4):
                        for c in range(8):
                            E.mm(ps[:, 0:512],
                                 bass.AP(gw_bf, c * 512 + ic * 128, [[8 * D, 128], [1, 128]]),
                                 bass.AP(xnT, c * T + lg * 512, [[8 * T, 128], [1, 512]]),
                                 c == 0, c == 7)
                        E.s("activation", gsil[:, ic * 512:(ic + 1) * 512], ps[:, 0:512], AF.Silu, 0.0, 1.0)
                        for c in range(8):
                            E.mm(ps2[:, 0:512],
                                 bass.AP(uw_bf, c * 512 + ic * 128, [[8 * D, 128], [1, 128]]),
                                 bass.AP(xnT, c * T + lg * 512, [[8 * T, 128], [1, 512]]),
                                 c == 0, c == 7)
                        E.v("tensor_tensor", h2[:, ic * 512:(ic + 1) * 512],
                            gsil[:, ic * 512:(ic + 1) * 512], ps2[:, 0:512], ALU.mult)
                    E.mm(ps2[:, 0:512], bass.AP(sel16, e * 128, [[16 * 128, 16], [1, 128]]),
                         combT[:, tsl], True, True)
                    for ic in range(4):
                        E.v("tensor_tensor", h2[:, ic * 512:(ic + 1) * 512],
                            h2[:, ic * 512:(ic + 1) * 512], ps2[:, 0:512], ALU.mult)
                    for d2c in range(8):
                        for ic in range(4):
                            E.mm(psr[:, 0:512],
                                 bass.AP(dw_bf, ic * D + d2c * 128, [[8 * D, 128], [1, 128]]),
                                 h2[:, ic * 512:(ic + 1) * 512], ic == 0, ic == 3)
                        dst = routedT[:, d2c * T + lg * 512: d2c * T + (lg + 1) * 512]
                        if e == 0:
                            E.v("tensor_copy", dst, psr[:, 0:512])
                        else:
                            E.v("tensor_add", dst, dst, psr[:, 0:512])

            stage_load(E, stage, gw_bf, sgwT_in.rearrange("(c p) i -> p c i", p=128), D, 8)
            stage_load(E, stage, uw_bf, suwT_in.rearrange("(c p) i -> p c i", p=128), D, 8)
            stage_load(E, stage, dw_bf, sdwT_in.rearrange("(c p) i -> p c i", p=128), D, 8)
            for lg in range(T // 512):
                for ic in range(8):
                    for c in range(8):
                        E.mm(ps[:, 0:512],
                             bass.AP(gw_bf, c * D + ic * 128, [[8 * D, 128], [1, 128]]),
                             bass.AP(xnT, c * T + lg * 512, [[8 * T, 128], [1, 512]]),
                             c == 0, c == 7)
                    E.s("activation", gsil[:, ic * 512:(ic + 1) * 512], ps[:, 0:512], AF.Silu, 0.0, 1.0)
                    for c in range(8):
                        E.mm(ps2[:, 0:512],
                             bass.AP(uw_bf, c * D + ic * 128, [[8 * D, 128], [1, 128]]),
                             bass.AP(xnT, c * T + lg * 512, [[8 * T, 128], [1, 512]]),
                             c == 0, c == 7)
                    E.v("tensor_tensor", h2[:, ic * 512:(ic + 1) * 512],
                        gsil[:, ic * 512:(ic + 1) * 512], ps2[:, 0:512], ALU.mult)
                for d2c in range(8):
                    for ic in range(8):
                        E.mm(psr[:, 0:512],
                             bass.AP(dw_bf, ic * D + d2c * 128, [[8 * D, 128], [1, 128]]),
                             h2[:, ic * 512:(ic + 1) * 512], ic == 0, ic == 7)
                    dst = routedT[:, d2c * T + lg * 512: d2c * T + (lg + 1) * 512]
                    E.v("tensor_add", dst, dst, psr[:, 0:512])

        for tt in range(ntile):
            for half in range(2):
                for j in range(4):
                    d2c = half * 4 + j
                    E.t("transpose", psF[:, j * 128:(j + 1) * 128],
                        routedT[:, d2c * T + tt * 128: d2c * T + (tt + 1) * 128], identf[:])
                E.v("tensor_copy", outt[:, half * 512:(half + 1) * 512], psF[:, 0:512])
            E.dma(xt[:], x_in[tt * 128:(tt + 1) * 128, :])
            E.v("tensor_add", outt[:], outt[:], xt[:])
            E.dma(r_out[tt * 128:(tt + 1) * 128, :], outt[:])

        blk = es.enter_context(nc.Block())
        E.replay(blk)
    return nc


def moe_inmaps(x_full, IN):
    import ml_dtypes
    sel16 = np.zeros((16, 16, 128), np.float32)
    for j in range(16):
        sel16[j, j, :] = 1.0
    epsB = np.tile((np.arange(16, dtype=np.float32) * 1e-6)[None, :], (128, 1))
    identb = np.eye(128, dtype=np.float32).astype(ml_dtypes.bfloat16)
    identf = np.eye(128, dtype=np.float32)
    base = {
        "rwT": np.ascontiguousarray(IN["L0_ffn_router_weight"].T),
        "eb": IN["L0_ffn_e_bias"].reshape(1, 16),
        "gwT": np.ascontiguousarray(IN["L0_ffn_gate_w"].transpose(0, 2, 1)),
        "uwT": np.ascontiguousarray(IN["L0_ffn_up_w"].transpose(0, 2, 1)),
        "dwT": np.ascontiguousarray(IN["L0_ffn_down_w"].transpose(0, 2, 1)),
        "sgwT": np.ascontiguousarray(IN["L0_ffn_sh_gate_w"].T),
        "suwT": np.ascontiguousarray(IN["L0_ffn_sh_up_w"].T),
        "sdwT": np.ascontiguousarray(IN["L0_ffn_sh_down_w"].T),
        "epsB": epsB,
        "sel16": sel16,
        "ones1": np.ones((1, 128), np.float32),
        "identb": identb,
        "identf": identf,
    }
    maps = []
    for c in range(NCORE):
        m = dict(base)
        m["x"] = np.ascontiguousarray(x_full[c * TC:(c + 1) * TC])
        maps.append(m)
    return maps


_PMOE = None


def run_moe(x_full, IN):
    global _PMOE
    if _PMOE is None:
        _PMOE = build_moe()
    res = run_bass_kernel_spmd(_PMOE, moe_inmaps(x_full, IN), core_ids=list(range(NCORE)))
    return np.concatenate([r["r"] for r in res.results], axis=0)


def build_p6():
    nc = bass.Bass("TRN2", target_bir_lowering=False)
    L = 4096
    QH = 2048
    x_in = nc.dram_tensor("x", [L, D], F32, kind="ExternalInput").ap()
    wqkv_in = nc.dram_tensor("wqkv", [D, 3 * D], F32, kind="ExternalInput").ap()
    bq128_in = nc.dram_tensor("bq128", [128, 24], F32, kind="ExternalInput").ap()
    bvrow_in = nc.dram_tensor("bvrow", [1, D], F32, kind="ExternalInput").ap()
    wo_in = nc.dram_tensor("wo", [D, D], F32, kind="ExternalInput").ap()
    bout_in = nc.dram_tensor("bout", [1, D], F32, kind="ExternalInput").ap()
    cos_in = nc.dram_tensor("cosb", [128, L], BF16, kind="ExternalInput").ap()
    sin_in = nc.dram_tensor("sinb", [128, L], BF16, kind="ExternalInput").ap()
    rot_in = nc.dram_tensor("rot", [128, 128], BF16, kind="ExternalInput").ap()
    identb_in = nc.dram_tensor("identb", [128, 128], BF16, kind="ExternalInput").ap()
    ones_in = nc.dram_tensor("ones1", [1, 128], F32, kind="ExternalInput").ap()
    r_out = nc.dram_tensor("r", [QH, D], F32, kind="ExternalOutput").ap()
    QOFF = 0

    with ExitStack() as es:
        sb = lambda n, shp, dt=F32: es.enter_context(nc.sbuf_tensor(n, shp, dt))
        S = es.enter_context(nc.semaphore("S"))
        identb = sb("identb_s", [128, 128], BF16)
        rot = sb("rot_s", [128, 128], BF16)
        ones1 = sb("ones1_s", [1, 128])
        cosb = sb("cosb_s", [128, L], BF16)
        sinb = sb("sinb_s", [128, L], BF16)
        bq128 = sb("bq128_s", [128, 24])
        bvB = sb("bvB", [128, D])
        boutB = sb("boutB", [128, D])
        rows = sb("rows", [1, 2 * D])
        kcol = sb("kcol", [128, 4])
        scr = sb("scr", [128, 8])
        xt = sb("xt", [128, D])
        sq = sb("sq", [128, D])
        xnbf = sb("xnbf", [128, D], BF16)
        xnT = sb("xnT", [128, 8 * L], BF16)
        attnoutT = sb("attnoutT", [128, 8 * QH], BF16)
        wstage = sb("wstage", [128, 8 * 128])
        wq_bf = sb("wq_bf", [128, 8 * 128], BF16)
        wk_bf = sb("wk_bf", [128, 8 * 128], BF16)
        wv_bf = sb("wv_bf", [128, 8 * 128], BF16)
        wo_bf = sb("wo_bf", [128, 8 * D], BF16)
        vaug = sb("vaug", [128, 32 * 130], BF16)
        QT = sb("QT", [128, L], BF16)
        KT = sb("KT", [128, L], BF16)
        qf = sb("qf", [128, 512])
        t1 = sb("t1", [128, 512])
        t2 = sb("t2", [128, 512])
        qb = sb("qb", [128, 512], BF16)
        attnT = sb("attnT", [128, 512], BF16)
        osb = sb("osb", [64, 512])
        rrow = sb("rrow", [1, 512])
        outt = sb("outt", [128, D])
        ps = es.enter_context(nc.psum_tensor("ps", [128, 512], F32))
        ps2 = es.enter_context(nc.psum_tensor("ps2", [128, 512], F32))
        psO = es.enter_context(nc.psum_tensor("psO", [128, 512], F32))
        psT = es.enter_context(nc.psum_tensor("psT", [128, 1024], BF16))

        E = Emit(nc, S)
        E.v("memset", kcol[:, 0:1], 1e-5)
        E.v("memset", kcol[:, 1:2], -30.0)
        E.dma(identb[:], identb_in)
        E.dma(rot[:], rot_in)
        E.dma(ones1[:], ones_in)
        E.dma(cosb[:], cos_in)
        E.dma(sinb[:], sin_in)
        E.dma(bq128[:], bq128_in)
        E.dma(rows[:, 0:D], bvrow_in)
        E.dma(rows[:, D:2 * D], bout_in)
        bcast_row(E, nc, ps, bvB, rows[:, 0:D], ones1[:])
        bcast_row(E, nc, ps, boutB, rows[:, D:2 * D], ones1[:])
        for hp in range(8):
            E.dma(wstage[:, 0:D], wo_in[hp * 128:(hp + 1) * 128, :])
            E.s("copy", wo_bf[:, hp * D:(hp + 1) * D], wstage[:, 0:D])

        E.v("memset", bass.AP(vaug, 64, [[32 * 130, 128], [130, 32], [65, 2], [1, 1]]), 1.0)

        for i in range(L // 128):
            dst = bass.AP(xnT, i * 128, [[8 * L, 128], [L, 8], [1, 128]])
            emit_ln_block(E, nc, x_in, i * 128, xt, sq, scr, xnbf, psT, dst, identb, kcol[:, 0:1])

        QG = QH // 512
        for hp in range(8):
            for (wbf, sec) in ((wq_bf, 0), (wk_bf, 1), (wv_bf, 2)):
                col0 = sec * D + hp * 128
                E.dma(bass.AP(wstage, 0, [[1024, 128], [128, 8], [1, 128]]),
                      wqkv_in.rearrange("(c p) o -> p c o", p=128)[:, :, col0:col0 + 128])
                E.s("copy", wbf[:, 0:1024], wstage[:, 0:1024])
            for tt in range(L // 128):
                for c in range(8):
                    E.mm(ps[:, 0:128],
                         bass.AP(xnT, c * L + tt * 128, [[8 * L, 128], [1, 128]]),
                         wv_bf[:, c * 128:(c + 1) * 128], c == 0, c == 7)
                E.op("vector", nc.vector.tensor_tensor,
                     bass.AP(vaug, tt * 130, [[32 * 130, 128], [65, 2], [1, 64]]),
                     ps[:, 0:128], bvB[:, hp * 128:(hp + 1) * 128], ALU.add)
            for (wbf, QKT, bcol) in ((wq_bf, QT, hp), (wk_bf, KT, 8 + hp)):
                for lg in range(L // 512):
                    tsl = slice(lg * 512, (lg + 1) * 512)
                    for c in range(8):
                        E.mm(ps[:, 0:512], wbf[:, c * 128:(c + 1) * 128],
                             bass.AP(xnT, c * L + lg * 512, [[8 * L, 128], [1, 512]]),
                             c == 0, c == 7)
                    E.v("tensor_scalar", qf[:], ps[:, 0:512], bq128[:, bcol:bcol + 1], None, ALU.add)
                    E.s("copy", qb[:], qf[:])
                    E.mm(ps2[:, 0:512], rot[:], qb[:], True, True)
                    E.v("tensor_tensor", t1[:], qf[:], cosb[:, tsl], ALU.mult)
                    E.v("tensor_tensor", t2[:], ps2[:, 0:512], sinb[:, tsl], ALU.mult)
                    E.v("tensor_tensor", QKT[:, tsl], t1[:], t2[:], ALU.add)
            for h in range(2):
                hb = h * 64
                for qg in range(QG):
                    qsl = slice(QOFF + qg * 512, QOFF + (qg + 1) * 512)
                    for kc in range(32):
                        E.mm(ps[:, 0:512],
                             KT[hb:hb + 64, kc * 128:(kc + 1) * 128],
                             QT[hb:hb + 64, qsl], True, True)
                        E.s("activation", attnT[:], ps[:, 0:512], AF.Exp, kcol[:, 1:2], 0.125)
                        E.mm(psO[0:65, 0:512],
                             bass.AP(vaug, kc * 130 + h * 65, [[32 * 130, 128], [1, 65]]),
                             attnT[:], kc == 0, kc == 31)
                    E.v("reciprocal", rrow[:], psO[64:65, 0:512])
                    E.mm(ps2[0:64, 0:512], ones1[0:1, 0:64], rrow[:], True, True)
                    E.v("tensor_copy", osb[:], psO[0:64, 0:512])
                    E.v("tensor_tensor",
                        attnoutT[hb:hb + 64, hp * QH + qg * 512:hp * QH + (qg + 1) * 512],
                        osb[:], ps2[0:64, 0:512], ALU.mult)
        for tt in range(QH // 128):
            for og in range(2):
                for hp in range(8):
                    E.mm(ps[:, 0:512],
                         attnoutT[:, hp * QH + tt * 128:hp * QH + (tt + 1) * 128],
                         wo_bf[:, hp * D + og * 512:hp * D + og * 512 + 512],
                         hp == 0, hp == 7)
                E.v("tensor_copy", outt[:, og * 512:(og + 1) * 512], ps[:, 0:512])
            E.v("tensor_add", outt[:], outt[:], boutB[:])
            E.dma(xt[:], x_in[QOFF + tt * 128:QOFF + (tt + 1) * 128, :])
            E.v("tensor_add", outt[:], outt[:], xt[:])
            E.dma(r_out[tt * 128:(tt + 1) * 128, :], outt[:])

        blk = es.enter_context(nc.Block())
        E.replay(blk)
    return nc


def p6_consts():
    import ml_dtypes
    Dh = 64
    inv_freq = 1.0 / 10000.0 ** (np.arange(0, Dh, 2, dtype=np.float32) / Dh)
    freqs = np.arange(4096, dtype=np.float32)[:, None] * inv_freq[None, :]
    emb = np.concatenate((freqs, freqs), axis=-1)
    cos = np.cos(emb).T
    sin = np.sin(emb).T
    cosb = np.tile(cos, (2, 1)).astype(ml_dtypes.bfloat16)
    sinb = np.tile(sin, (2, 1)).astype(ml_dtypes.bfloat16)
    rotm = np.zeros((128, 128), np.float32)
    for hh in range(2):
        for m in range(64):
            if m < 32:
                rotm[hh * 64 + m + 32, hh * 64 + m] = -1.0
            else:
                rotm[hh * 64 + m - 32, hh * 64 + m] = 1.0
    rot = rotm.astype(ml_dtypes.bfloat16)
    identb = np.eye(128, dtype=np.float32).astype(ml_dtypes.bfloat16)
    return cosb, sinb, rot, identb


def p6_inmaps(x_full, IN):
    cosb, sinb, rot, identb = p6_consts()
    wqkv = IN["L0_attn_b_W_qkv"]
    bqkv = IN["L0_attn_b_b_qkv"]
    base = {
        "wqkv": wqkv,
        "bq128": np.ascontiguousarray(bqkv[:2048].reshape(16, 128).T),
        "bvrow": bqkv[2048:].reshape(1, D),
        "wo": IN["L0_attn_b_W_out"],
        "bout": IN["L0_attn_b_b_out"].reshape(1, D),
        "rot": rot,
        "identb": identb,
        "ones1": np.ones((1, 128), np.float32),
    }
    bq = np.zeros((128, 24), np.float32)
    bq[:, 0:8] = bqkv[0:1024].reshape(8, 128).T
    bq[:, 8:16] = bqkv[1024:2048].reshape(8, 128).T
    base["bq128"] = bq
    maps = []
    for c in range(NCORE):
        b, qh = c // 2, c % 2
        xb = x_full.reshape(4, 4096, D)[b]
        m = dict(base)
        if qh == 1:
            m["x"] = np.ascontiguousarray(np.concatenate([xb[2048:], xb[:2048]], axis=0))
            m["cosb"] = np.ascontiguousarray(np.concatenate([cosb[:, 2048:], cosb[:, :2048]], axis=1))
            m["sinb"] = np.ascontiguousarray(np.concatenate([sinb[:, 2048:], sinb[:, :2048]], axis=1))
        else:
            m["x"] = np.ascontiguousarray(xb)
            m["cosb"] = cosb
            m["sinb"] = sinb
        maps.append(m)
    return maps


_P6 = None


def run_p6(x_full, IN):
    global _P6
    if _P6 is None:
        _P6 = build_p6()
    res = run_bass_kernel_spmd(_P6, p6_inmaps(x_full, IN), core_ids=list(range(NCORE)))
    out = np.zeros((4, 4096, D), np.float32)
    for c in range(NCORE):
        b, qh = c // 2, c % 2
        out[b, qh * 2048:(qh + 1) * 2048] = res.results[c]["r"]
    return out.reshape(-1, D)


def build_p7():
    nc = bass.Bass("TRN2", target_bir_lowering=False)
    L = 4096
    LH = 2048
    x_in = nc.dram_tensor("x", [L, D], F32, kind="ExternalInput").ap()
    wx_in = nc.dram_tensor("wx", [D, 96], F32, kind="ExternalInput").ap()
    wdt_in = nc.dram_tensor("wdt", [64, 512], F32, kind="ExternalInput").ap()
    dtb_in = nc.dram_tensor("dtb", [512], F32, kind="ExternalInput").ap()
    alog_in = nc.dram_tensor("alog", [512, 16], F32, kind="ExternalInput").ap()
    dpar_in = nc.dram_tensor("dpar", [512], F32, kind="ExternalInput").ap()
    sel_in = nc.dram_tensor("sel", [32, 32, 128], F32, kind="ExternalInput").ap()
    identb_in = nc.dram_tensor("identb", [128, 128], BF16, kind="ExternalInput").ap()
    yT_out = nc.dram_tensor("yT", [512, L], BF16, kind="ExternalOutput").ap()

    with ExitStack() as es:
        sb = lambda n, shp, dt=F32: es.enter_context(nc.sbuf_tensor(n, shp, dt))
        S = es.enter_context(nc.semaphore("S"))
        identb = sb("identb_s", [128, 128], BF16)
        sel = sb("sel_s", [32, 32 * 128])
        wx_bf = sb("wx_bf", [128, 8 * 96], BF16)
        wdt_bf = sb("wdt_bf", [64, 512], BF16)
        cols = sb("cols", [128, 16])
        A_sb = sb("A_sb", [128, 64])
        numcar = sb("numcar", [128, 64])
        sumd = sb("sumd", [128, 12])
        kcol = sb("kcol", [128, 4])
        scr = sb("scr", [128, 8])
        deltaT = sb("deltaT", [128, 4 * L])
        xnhalfT = sb("xnhalfT", [128, 4 * L], BF16)
        BC32f = sb("BC32f", [32, L])
        ps = es.enter_context(nc.psum_tensor("ps", [128, 512], F32))
        psT = es.enter_context(nc.psum_tensor("psT", [128, 1024], BF16))

        E = Emit(nc, S)
        E.v("memset", kcol[:, 0:1], 1e-5)
        E.v("memset", kcol[:, 1:2], 1.0)
        E.v("memset", kcol[:, 2:3], 0.0)
        E.dma(identb[:], identb_in)
        E.dma(sel[:], sel_in.rearrange("k j p -> k (j p)"))
        E.dma(cols[:, 0:4], dtb_in.rearrange("(c p) -> p c", p=128), allow_slow_non_contiguous=True)
        E.dma(cols[:, 4:8], dpar_in.rearrange("(c p) -> p c", p=128), allow_slow_non_contiguous=True)
        E.dma(bass.AP(A_sb, 0, [[64, 128], [16, 4], [1, 16]]),
              alog_in.rearrange("(c p) n -> p c n", p=128))
        E.s("activation", A_sb[:], A_sb[:], AF.Exp, 0.0, 1.0)
        E.v("tensor_scalar_mul", A_sb[:], A_sb[:], -1.0)

        with nc.sbuf_tensor("xt", [128, D], F32) as xt, \
             nc.sbuf_tensor("sq", [128, D], F32) as sq, \
             nc.sbuf_tensor("xnbf", [128, D], BF16) as xnbf, \
             nc.sbuf_tensor("xnT", [128, 8 * LH], BF16) as xnT, \
             nc.sbuf_tensor("dbcT", [96, LH], F32) as dbcT, \
             nc.sbuf_tensor("dr_bf", [64, LH], BF16) as dr_bf, \
             nc.sbuf_tensor("wtmp", [128, 2 * 512], F32) as wtmp:
            E.dma(bass.AP(xt, 0, [[1024, 128], [96, 8], [1, 96]]),
                  wx_in.rearrange("(c p) r -> p c r", p=128))
            E.s("copy", wx_bf[:], xt[:, 0:8 * 96])
            E.dma(sq[0:64, 0:512], wdt_in)
            E.s("copy", wdt_bf[:], sq[0:64, 0:512])
            for lh in range(2):
                l0 = lh * LH
                for i in range(LH // 128):
                    dst = bass.AP(xnT, i * 128, [[8 * LH, 128], [LH, 8], [1, 128]])
                    emit_ln_block(E, nc, x_in, l0 + i * 128, xt, sq, scr, xnbf, psT, dst, identb,
                                  kcol[:, 0:1])
                    E.v("tensor_copy",
                        bass.AP(xnhalfT, l0 + i * 128, [[4 * L, 128], [L, 4], [1, 128]]),
                        psT[:, 0:512])
                for lg in range(LH // 512):
                    for c in range(8):
                        E.mm(ps[0:96, 0:512],
                             bass.AP(wx_bf, c * 96, [[8 * 96, 128], [1, 96]]),
                             bass.AP(xnT, c * LH + lg * 512, [[8 * LH, 128], [1, 512]]),
                             c == 0, c == 7)
                    E.v("tensor_copy", dbcT[:, lg * 512:(lg + 1) * 512], ps[0:96, 0:512])
                E.s("copy", dr_bf[:], dbcT[0:64, :])
                E.v("tensor_copy", BC32f[:, l0:l0 + LH], dbcT[64:96, :])
                for dc in range(4):
                    for lg in range(LH // 512):
                        E.mm(ps[:, 0:512], wdt_bf[:, dc * 128:(dc + 1) * 128],
                             dr_bf[:, lg * 512:(lg + 1) * 512], True, True)
                        dslice = deltaT[:, dc * L + l0 + lg * 512: dc * L + l0 + (lg + 1) * 512]
                        E.v("tensor_scalar", wtmp[:, 0:512], ps[:, 0:512], cols[:, dc:dc + 1], None, ALU.add)
                        E.s("activation", wtmp[:, 512:1024], wtmp[:, 0:512], AF.Abs, 0.0, 1.0)
                        E.s("activation", wtmp[:, 512:1024], wtmp[:, 512:1024], AF.Exp, 0.0, -1.0)
                        E.s("activation", wtmp[:, 512:1024], wtmp[:, 512:1024], AF.Ln, kcol[:, 1:2], 1.0)
                        E.v("tensor_scalar_max", wtmp[:, 0:512], wtmp[:, 0:512], 0.0)
                        E.v("tensor_add", dslice, wtmp[:, 0:512], wtmp[:, 512:1024])
                    E.v("reduce_sum", sumd[:, lh * 4 + dc:lh * 4 + dc + 1],
                        deltaT[:, dc * L + l0: dc * L + l0 + LH], AX.X)
            for dc in range(4):
                E.v("tensor_add", sumd[:, 8 + dc:9 + dc], sumd[:, dc:dc + 1],
                    sumd[:, 4 + dc:5 + dc])

        with nc.sbuf_tensor("BmB", [128, LH], BF16) as BmB, \
             nc.sbuf_tensor("CbB", [128, LH], BF16) as CbB, \
             nc.sbuf_tensor("w0", [128, LH], F32) as w0, \
             nc.sbuf_tensor("w1", [128, LH], F32) as w1, \
             nc.sbuf_tensor("w2", [128, LH], F32) as w2, \
             nc.sbuf_tensor("w3", [128, LH], F32) as w3, \
             nc.sbuf_tensor("yT_s", [128, 4 * LH], F32) as yT, \
             nc.sbuf_tensor("yTb", [128, LH], BF16) as yTb:
            zbc = bass.AP(kcol, 2, [[4, 128], [0, LH]])
            for lh in range(2):
                l0 = lh * LH
                for n in range(16):
                    for (dstB, j) in ((BmB, n), (CbB, 16 + n)):
                        for lg in range(LH // 512):
                            E.mm(ps[:, 0:512], bass.AP(sel, j * 128, [[32 * 128, 32], [1, 128]]),
                                 BC32f[:, l0 + lg * 512:l0 + (lg + 1) * 512], True, True)
                            E.v("tensor_copy", dstB[:, lg * 512:(lg + 1) * 512], ps[:, 0:512])
                    for dc in range(4):
                        dsl = deltaT[:, dc * L + l0: dc * L + l0 + LH]
                        acol = A_sb[:, dc * 16 + n:dc * 16 + n + 1]
                        E.v("tensor_scalar", w0[:], dsl, acol, None, ALU.mult)
                        if lh == 0:
                            E.v("tensor_tensor_scan", w1[:], w0[:], zbc, 0.0, ALU.add, ALU.add)
                        else:
                            E.v("tensor_scalar", scr[:, 4:5], sumd[:, dc:dc + 1], acol, None, ALU.mult)
                            E.v("tensor_tensor_scan", w1[:], w0[:], zbc, scr[:, 4:5], ALU.add, ALU.add)
                        E.v("tensor_scalar", scr[:, 5:6], sumd[:, 8 + dc:9 + dc], acol, None, ALU.mult)
                        E.s("activation", w2[:], w1[:], AF.Exp, scr[:, 5:6], -1.0)
                        xn_dc = bass.AP(xnhalfT, dc * L + l0, [[4 * L, 128], [1, LH]])
                        E.v("tensor_mul", w0[:], dsl, xn_dc)
                        E.v("tensor_mul", w0[:], w0[:], BmB[:])
                        E.v("tensor_mul", w0[:], w0[:], w2[:])
                        init = 0.0 if lh == 0 else numcar[:, n * 4 + dc:n * 4 + dc + 1]
                        E.v("tensor_tensor_scan", w3[:], w0[:], zbc, init, ALU.add, ALU.add)
                        E.v("tensor_copy", numcar[:, n * 4 + dc:n * 4 + dc + 1], w3[:, LH - 1:LH])
                        E.v("tensor_scalar_add", w2[:], w2[:], 1e-12)
                        E.v("reciprocal", w2[:], w2[:])
                        E.v("tensor_mul", w3[:], w3[:], w2[:])
                        E.v("tensor_mul", w3[:], w3[:], CbB[:])
                        if n == 0:
                            E.v("tensor_copy", yT[:, dc * LH:(dc + 1) * LH], w3[:])
                        else:
                            E.v("tensor_add", yT[:, dc * LH:(dc + 1) * LH],
                                yT[:, dc * LH:(dc + 1) * LH], w3[:])
                for dc in range(4):
                    xn_dc = bass.AP(xnhalfT, dc * L + l0, [[4 * L, 128], [1, LH]])
                    E.op("vector", nc.vector.scalar_tensor_tensor, yT[:, dc * LH:(dc + 1) * LH],
                         xn_dc, cols[:, 4 + dc:5 + dc], yT[:, dc * LH:(dc + 1) * LH], ALU.mult, ALU.add)
                    E.s("copy", yTb[:], yT[:, dc * LH:(dc + 1) * LH])
                    E.dma(yT_out[dc * 128:(dc + 1) * 128, l0:l0 + LH], yTb[:])

        blk = es.enter_context(nc.Block())
        E.replay(blk)
    return nc


def p7_inmaps(x_full, IN):
    import ml_dtypes
    sel = np.zeros((32, 32, 128), np.float32)
    for j in range(32):
        sel[j, j, :] = 1.0
    identb = np.eye(128, dtype=np.float32).astype(ml_dtypes.bfloat16)
    wx = IN["L1_attn_a_W_x"]
    wdt = IN["L1_attn_a_W_dt"]
    dtb = IN["L1_attn_a_b_dt"]
    alog = IN["L1_attn_a_A_log"]
    dpar = IN["L1_attn_a_D"]
    maps = []
    for c in range(NCORE):
        b, dh = c // 2, c % 2
        dsl = slice(dh * 512, (dh + 1) * 512)
        xb = x_full.reshape(4, 4096, D)[b]
        if dh == 1:
            xb = np.concatenate([xb[:, 512:], xb[:, :512]], axis=1)
            wx_r = np.concatenate([wx[512:], wx[:512]], axis=0)
        else:
            wx_r = wx
        maps.append({
            "x": np.ascontiguousarray(xb),
            "wx": np.ascontiguousarray(wx_r),
            "wdt": np.ascontiguousarray(wdt[:, dsl]),
            "dtb": np.ascontiguousarray(dtb[dsl]),
            "alog": np.ascontiguousarray(alog[dsl]),
            "dpar": np.ascontiguousarray(dpar[dsl]),
            "sel": sel,
            "identb": identb,
        })
    return maps


_P7 = None


def run_p7(x_full, IN):
    global _P7
    if _P7 is None:
        _P7 = build_p7()
    res = run_bass_kernel_spmd(_P7, p7_inmaps(x_full, IN), core_ids=list(range(NCORE)))
    import ml_dtypes
    yT = np.zeros((4, 1024, 4096), dtype=ml_dtypes.bfloat16)
    for c in range(NCORE):
        b, dh = c // 2, c % 2
        yT[b, dh * 512:(dh + 1) * 512] = res.results[c]["yT"]
    return yT


def build_p8():
    nc = bass.Bass("TRN2", target_bir_lowering=False)
    T = TC
    yT_in = nc.dram_tensor("yT", [D, T], BF16, kind="ExternalInput").ap()
    x_in = nc.dram_tensor("x", [T, D], F32, kind="ExternalInput").ap()
    wo_in = nc.dram_tensor("wo", [D, D], F32, kind="ExternalInput").ap()
    bout_in = nc.dram_tensor("bout", [1, D], F32, kind="ExternalInput").ap()
    ones_in = nc.dram_tensor("ones1", [1, 128], F32, kind="ExternalInput").ap()
    r_out = nc.dram_tensor("r", [T, D], F32, kind="ExternalOutput").ap()

    with ExitStack() as es:
        sb = lambda n, shp, dt=F32: es.enter_context(nc.sbuf_tensor(n, shp, dt))
        S = es.enter_context(nc.semaphore("S"))
        ones1 = sb("ones1_s", [1, 128])
        rows = sb("rows", [1, D])
        boutB = sb("boutB", [128, D])
        wstage = sb("wstage", [128, D])
        wo_bf = sb("wo_bf", [128, 8 * D], BF16)
        ytiles = sb("ytiles", [128, 8 * 128], BF16)
        xt = sb("xt", [128, D])
        outt = sb("outt", [128, D])
        ps = es.enter_context(nc.psum_tensor("ps", [128, 512], F32))
        E = Emit(nc, S)
        E.dma(ones1[:], ones_in)
        E.dma(rows[:], bout_in)
        bcast_row(E, nc, ps, boutB, rows[:], ones1[:])
        for c in range(8):
            E.dma(wstage[:], wo_in[c * 128:(c + 1) * 128, :])
            E.s("copy", wo_bf[:, c * D:(c + 1) * D], wstage[:])
        for tt in range(T // 128):
            for c in range(8):
                E.dma(ytiles[:, c * 128:(c + 1) * 128],
                      yT_in[c * 128:(c + 1) * 128, tt * 128:(tt + 1) * 128])
            for og in range(2):
                for c in range(8):
                    E.mm(ps[:, 0:512], ytiles[:, c * 128:(c + 1) * 128],
                         wo_bf[:, c * D + og * 512:c * D + og * 512 + 512],
                         c == 0, c == 7)
                E.v("tensor_copy", outt[:, og * 512:(og + 1) * 512], ps[:, 0:512])
            E.v("tensor_add", outt[:], outt[:], boutB[:])
            E.dma(xt[:], x_in[tt * 128:(tt + 1) * 128, :])
            E.v("tensor_add", outt[:], outt[:], xt[:])
            E.dma(r_out[tt * 128:(tt + 1) * 128, :], outt[:])
        blk = es.enter_context(nc.Block())
        E.replay(blk)
    return nc


_P8 = None


def run_p8(yT_full, x_full, IN):
    global _P8
    if _P8 is None:
        _P8 = build_p8()
    maps = []
    for c in range(NCORE):
        b, half = c // 2, c % 2
        maps.append({
            "yT": np.ascontiguousarray(yT_full[b][:, half * 2048:(half + 1) * 2048]),
            "x": np.ascontiguousarray(x_full[c * TC:(c + 1) * TC]),
            "wo": IN["L1_attn_a_W_out"],
            "bout": IN["L1_attn_a_b_out"].reshape(1, D),
            "ones1": np.ones((1, 128), np.float32),
        })
    res = run_bass_kernel_spmd(_P8, maps, core_ids=list(range(NCORE)))
    return np.concatenate([r["r"] for r in res.results], axis=0)


def build_p10():
    nc = bass.Bass("TRN2", target_bir_lowering=False)
    L = 4096
    QH = 2048
    SW = 5504 + 512
    x_in = nc.dram_tensor("x", [L, D], F32, kind="ExternalInput").ap()
    xq_in = nc.dram_tensor("xq", [QH, D], F32, kind="ExternalInput").ap()
    wqp_in = nc.dram_tensor("wqp", [8, D, 128], F32, kind="ExternalInput").ap()
    wkp_in = nc.dram_tensor("wkp", [8, D, 128], F32, kind="ExternalInput").ap()
    wvp_in = nc.dram_tensor("wvp", [8, D, 128], F32, kind="ExternalInput").ap()
    bqk_in = nc.dram_tensor("bqk", [128, 16], F32, kind="ExternalInput").ap()
    bvrow_in = nc.dram_tensor("bvrow", [1, D], F32, kind="ExternalInput").ap()
    wo_in = nc.dram_tensor("wo", [D, D], F32, kind="ExternalInput").ap()
    bout_in = nc.dram_tensor("bout", [1, D], F32, kind="ExternalInput").ap()
    stair_in = nc.dram_tensor("stair", [128, SW], BF16, kind="ExternalInput").ap()
    identb_in = nc.dram_tensor("identb", [128, 128], BF16, kind="ExternalInput").ap()
    ones_in = nc.dram_tensor("ones1", [1, 128], F32, kind="ExternalInput").ap()
    r_out = nc.dram_tensor("r", [QH, D], F32, kind="ExternalOutput").ap()

    with ExitStack() as es:
        sb = lambda n, shp, dt=F32: es.enter_context(nc.sbuf_tensor(n, shp, dt))
        S = es.enter_context(nc.semaphore("S"))
        identb = sb("identb_s", [128, 128], BF16)
        ones1 = sb("ones1_s", [1, 128])
        stair = sb("stair_s", [128, SW], BF16)
        bqk = sb("bqk_s", [128, 16])
        bvB = sb("bvB", [128, D])
        boutB = bvB
        rows = sb("rows", [1, D])
        kcol = sb("kcol", [128, 4])
        scr = sb("scr", [128, 8])
        xt = sb("xt", [128, D])
        sq = sb("sq", [128, D])
        xnbf = sb("xnbf", [128, D], BF16)
        xnT = sb("xnT", [128, 8 * L], BF16)
        xqnT = sb("xqnT", [128, 8 * QH], BF16)
        attnoutT = sb("attnoutT", [128, 8 * QH], BF16)
        wstage = sb("wstage", [128, 8 * 128])
        wq_bf = sb("wq_bf", [128, 8 * 128], BF16)
        wk_bf = sb("wk_bf", [128, 8 * 128], BF16)
        wv_bf = sb("wv_bf", [128, 8 * 128], BF16)
        wo_bf = sb("wo_bf", [128, 8 * D], BF16)
        vaug = sb("vaug", [128, 32 * 130], BF16)
        QT = sb("QT", [128, QH], BF16)
        KT = sb("KT", [128, L], BF16)
        msc = sb("msc", [128, 512])
        attnT = sb("attnT", [128, 512], BF16)
        outt = sq
        osb = msc[0:64, :]
        rrow = None
        ps = es.enter_context(nc.psum_tensor("ps", [128, 512], F32))
        ps2 = es.enter_context(nc.psum_tensor("ps2", [128, 512], F32))
        psO = es.enter_context(nc.psum_tensor("psO", [128, 512], F32))
        psT = es.enter_context(nc.psum_tensor("psT", [128, 1024], BF16))
        rrow = xt[0:1, 0:512]

        E = Emit(nc, S)
        E.v("memset", kcol[:, 0:1], 1e-5)
        E.v("memset", kcol[:, 1:2], -30.0)
        E.dma(identb[:], identb_in)
        E.dma(ones1[:], ones_in)
        E.dma(stair[:], stair_in)
        E.dma(bqk[:], bqk_in)
        E.dma(rows[:, 0:D], bvrow_in)
        bcast_row(E, nc, ps, bvB, rows[:, 0:D], ones1[:])
        for hp in range(8):
            E.dma(wstage[:, 0:D], wo_in[hp * 128:(hp + 1) * 128, :])
            E.s("copy", wo_bf[:, hp * D:(hp + 1) * D], wstage[:, 0:D])
        E.v("memset", bass.AP(vaug, 64, [[32 * 130, 128], [130, 32], [65, 2], [1, 1]]), 1.0)

        for i in range(L // 128):
            dst = bass.AP(xnT, i * 128, [[8 * L, 128], [L, 8], [1, 128]])
            emit_ln_block(E, nc, x_in, i * 128, xt, sq, scr, xnbf, psT, dst, identb, kcol[:, 0:1])
        for i in range(QH // 128):
            dst = bass.AP(xqnT, i * 128, [[8 * QH, 128], [QH, 8], [1, 128]])
            emit_ln_block(E, nc, xq_in, i * 128, xt, sq, scr, xnbf, psT, dst, identb, kcol[:, 0:1])

        QG = QH // 512
        for hp in range(8):
            for (wbf, src) in ((wq_bf, wqp_in), (wk_bf, wkp_in), (wv_bf, wvp_in)):
                E.dma(bass.AP(wstage, 0, [[1024, 128], [128, 8], [1, 128]]),
                      src[hp].rearrange("(c p) o -> p c o", p=128))
                E.s("copy", wbf[:, 0:1024], wstage[:, 0:1024])
            for tt in range(L // 128):
                for c in range(8):
                    E.mm(ps[:, 0:128],
                         bass.AP(xnT, c * L + tt * 128, [[8 * L, 128], [1, 128]]),
                         wv_bf[:, c * 128:(c + 1) * 128], c == 0, c == 7)
                E.op("vector", nc.vector.tensor_tensor,
                     bass.AP(vaug, tt * 130, [[32 * 130, 128], [65, 2], [1, 64]]),
                     ps[:, 0:128], bvB[:, hp * 128:(hp + 1) * 128], ALU.add)
            for lg in range(L // 512):
                for c in range(8):
                    E.mm(ps[:, 0:512], wk_bf[:, c * 128:(c + 1) * 128],
                         bass.AP(xnT, c * L + lg * 512, [[8 * L, 128], [1, 512]]),
                         c == 0, c == 7)
                E.v("tensor_scalar", KT[:, lg * 512:(lg + 1) * 512], ps[:, 0:512],
                    bqk[:, 8 + hp:9 + hp], None, ALU.add)
            for lg in range(QH // 512):
                for c in range(8):
                    E.mm(ps[:, 0:512], wq_bf[:, c * 128:(c + 1) * 128],
                         bass.AP(xqnT, c * QH + lg * 512, [[8 * QH, 128], [1, 512]]),
                         c == 0, c == 7)
                E.v("tensor_scalar", QT[:, lg * 512:(lg + 1) * 512], ps[:, 0:512],
                    bqk[:, hp:hp + 1], None, ALU.add)
            for h in range(2):
                hb = h * 64
                for qg in range(QG):
                    qsl = slice(qg * 512, (qg + 1) * 512)
                    for kc in range(32):
                        E.mm(ps[:, 0:512],
                             KT[hb:hb + 64, kc * 128:(kc + 1) * 128],
                             QT[hb:hb + 64, qsl], True, True)
                        soff = qg * 512 - kc * 128 + 3968
                        E.v("tensor_tensor", msc[:], ps[:, 0:512],
                            stair[:, soff:soff + 512], ALU.add)
                        E.s("activation", attnT[:], msc[:], AF.Exp, kcol[:, 1:2], 0.125)
                        E.mm(psO[0:65, 0:512],
                             bass.AP(vaug, kc * 130 + h * 65, [[32 * 130, 128], [1, 65]]),
                             attnT[:], kc == 0, kc == 31)
                    E.v("reciprocal", rrow[:], psO[64:65, 0:512])
                    E.mm(ps2[0:64, 0:512], ones1[0:1, 0:64], rrow[:], True, True)
                    E.v("tensor_copy", osb[:], psO[0:64, 0:512])
                    E.v("tensor_tensor",
                        attnoutT[hb:hb + 64, hp * QH + qg * 512:hp * QH + (qg + 1) * 512],
                        osb[:], ps2[0:64, 0:512], ALU.mult)
        E.dma(rows[:, 0:D], bout_in)
        bcast_row(E, nc, ps, boutB, rows[:, 0:D], ones1[:])
        for tt in range(QH // 128):
            for og in range(2):
                for hp in range(8):
                    E.mm(ps[:, 0:512],
                         attnoutT[:, hp * QH + tt * 128:hp * QH + (tt + 1) * 128],
                         wo_bf[:, hp * D + og * 512:hp * D + og * 512 + 512],
                         hp == 0, hp == 7)
                E.v("tensor_copy", outt[:, og * 512:(og + 1) * 512], ps[:, 0:512])
            E.v("tensor_add", outt[:], outt[:], boutB[:])
            E.dma(xt[:], xq_in[tt * 128:(tt + 1) * 128, :])
            E.v("tensor_add", outt[:], outt[:], xt[:])
            E.dma(r_out[tt * 128:(tt + 1) * 128, :], outt[:])

        blk = es.enter_context(nc.Block())
        E.replay(blk)
    return nc


def p10_inmaps(x_full, IN):
    import ml_dtypes
    identb = np.eye(128, dtype=np.float32).astype(ml_dtypes.bfloat16)
    Wq = IN["L1_attn_b_Wq"]
    Wk = IN["L1_attn_b_Wk"]
    Wv = IN["L1_attn_b_Wv"]
    pack = lambda W: np.ascontiguousarray(
        W.reshape(8, 2, D, 64).transpose(0, 2, 1, 3).reshape(8, D, 128))
    bqk = np.zeros((128, 16), np.float32)
    bqk[:, 0:8] = IN["L1_attn_b_bq"].reshape(8, 128).T
    bqk[:, 8:16] = IN["L1_attn_b_bk"].reshape(8, 128).T
    base = {
        "wqp": pack(Wq), "wkp": pack(Wk), "wvp": pack(Wv),
        "bqk": bqk,
        "bvrow": IN["L1_attn_b_bv"].reshape(1, D),
        "wo": IN["L1_attn_b_Wo"],
        "bout": IN["L1_attn_b_bo"].reshape(1, D),
        "identb": identb,
        "ones1": np.ones((1, 128), np.float32),
    }
    SW = 5504 + 512
    maps = []
    for c in range(NCORE):
        b, qh = c // 2, c % 2
        q0 = qh * 2048
        xb = x_full.reshape(4, 4096, D)[b]
        r = np.arange(128)[:, None]
        u = np.arange(SW)[None, :]
        stair = np.where(r <= (u - 3968 + q0), 0.0, -1e30).astype(np.float32)
        m = dict(base)
        m["x"] = np.ascontiguousarray(xb)
        m["xq"] = np.ascontiguousarray(xb[q0:q0 + 2048])
        m["stair"] = stair.astype(ml_dtypes.bfloat16)
        maps.append(m)
    return maps


_P10 = None


def run_p10(x_full, IN):
    global _P10
    if _P10 is None:
        _P10 = build_p10()
    res = run_bass_kernel_spmd(_P10, p10_inmaps(x_full, IN), core_ids=list(range(NCORE)))
    out = np.zeros((4, 4096, D), np.float32)
    for c in range(NCORE):
        b, qh = c // 2, c % 2
        out[b, qh * 2048:(qh + 1) * 2048] = res.results[c]["r"]
    return out.reshape(-1, D)


def build_p9a():
    nc = bass.Bass("TRN2", target_bir_lowering=False)
    WN = 2304
    OWN = 2048
    xw_in = nc.dram_tensor("xw", [WN + 1, D], F32, kind="ExternalInput").ap()
    wqkv_in = nc.dram_tensor("wqkv", [D, 3 * 1024], F32, kind="ExternalInput").ap()
    wo_in = nc.dram_tensor("wo", [1024, D], F32, kind="ExternalInput").ap()
    qs_in = nc.dram_tensor("qs", [128, 2], F32, kind="ExternalInput").ap()
    cosS_in = nc.dram_tensor("cosS", [128, WN], BF16, kind="ExternalInput").ap()
    sinS_in = nc.dram_tensor("sinS", [128, WN], BF16, kind="ExternalInput").ap()
    cosI_in = nc.dram_tensor("cosI", [128, WN], BF16, kind="ExternalInput").ap()
    sinI_in = nc.dram_tensor("sinI", [128, WN], BF16, kind="ExternalInput").ap()
    lmask_in = nc.dram_tensor("lmask", [4, 128, 512], F32, kind="ExternalInput").ap()
    rot_in = nc.dram_tensor("rot", [128, 128], BF16, kind="ExternalInput").ap()
    b2_in = nc.dram_tensor("b2sel", [128, 2], F32, kind="ExternalInput").ap()
    b2t_in = nc.dram_tensor("b2t", [2, 128], F32, kind="ExternalInput").ap()
    identb_in = nc.dram_tensor("identb", [128, 128], BF16, kind="ExternalInput").ap()
    r_out = nc.dram_tensor("r", [OWN, D], F32, kind="ExternalOutput").ap()

    with ExitStack() as es:
        sb = lambda n, shp, dt=F32: es.enter_context(nc.sbuf_tensor(n, shp, dt))
        S = es.enter_context(nc.semaphore("S"))
        identb = sb("identb_s", [128, 128], BF16)
        rot = sb("rot_s", [128, 128], BF16)
        b2 = sb("b2_s", [128, 2])
        b2t = sb("b2t_s", [2, 128])
        qs = sb("qs_s", [128, 2])
        cosS = sb("cosS_s", [128, WN], BF16)
        sinS = sb("sinS_s", [128, WN], BF16)
        cosI = sb("cosI_s", [128, WN], BF16)
        sinI = sb("sinI_s", [128, WN], BF16)
        lmask = sb("lmask_s", [128, 4 * 512])
        scr = sb("scr", [128, 8])
        xt = sb("xt", [128, D])
        shbf = sb("shbf", [128, D], BF16)
        shxT = sb("shxT", [128, 8 * WN], BF16)
        V = sb("V", [128, 18 * 1024], BF16)
        attnoutT = sb("attnoutT", [128, 8 * OWN], BF16)
        wstage = sb("wstage", [128, 1024])
        wq_bf = sb("wq_bf", [128, 8 * 128], BF16)
        wk_bf = sb("wk_bf", [128, 8 * 128], BF16)
        wv_bf = sb("wv_bf", [128, 8 * 128], BF16)
        wo_bf = sb("wo_bf", [128, 8 * D], BF16)
        QT = sb("QT", [128, WN], BF16)
        KT = sb("KT", [128, WN], BF16)
        sqv = sb("sqv", [128, 512])
        qsb = sb("qsb", [128, 512])
        t1 = sb("t1", [128, 512])
        nsb = sb("nsb", [2, 512])
        msc = sb("msc", [128, 512])
        attn = sb("attn", [128, 512], BF16)
        attnT = sb("attnT", [128, 512], BF16)
        ps = es.enter_context(nc.psum_tensor("ps", [128, 512], F32))
        ps2 = es.enter_context(nc.psum_tensor("ps2", [128, 512], F32))
        psO = es.enter_context(nc.psum_tensor("psO", [128, 512], F32))
        psT = es.enter_context(nc.psum_tensor("psT", [128, 1024], BF16))

        E = Emit(nc, S)
        E.dma(identb[:], identb_in)
        E.dma(rot[:], rot_in)
        E.dma(b2[:], b2_in)
        E.dma(b2t[:], b2t_in)
        E.dma(qs[:], qs_in)
        E.dma(cosS[:], cosS_in)
        E.dma(sinS[:], sinS_in)
        E.dma(cosI[:], cosI_in)
        E.dma(sinI[:], sinI_in)
        E.dma(bass.AP(lmask, 0, [[4 * 512, 128], [512, 4], [1, 512]]),
              lmask_in.rearrange("m p j -> p m j"))
        for hp in range(8):
            E.dma(wstage[:, 0:D], wo_in[hp * 128:(hp + 1) * 128, :])
            E.s("copy", wo_bf[:, hp * D:(hp + 1) * D], wstage[:, 0:D])

        for i in range(WN // 128):
            E.dma(xt[:, 0:512], xw_in[1 + i * 128:1 + (i + 1) * 128, 0:512])
            E.dma(xt[:, 512:1024], xw_in[i * 128:(i + 1) * 128, 512:1024])
            E.s("copy", shbf[:], xt[:])
            for dc in range(8):
                E.t("transpose", psT[:, dc * 128:(dc + 1) * 128], shbf[:, dc * 128:(dc + 1) * 128],
                    identb[:])
            E.v("tensor_copy", bass.AP(shxT, i * 128, [[8 * WN, 128], [WN, 8], [1, 128]]), psT[:])

        for hp in range(8):
            for (wbf, sec) in ((wq_bf, 0), (wk_bf, 1), (wv_bf, 2)):
                col0 = sec * 1024 + hp * 128
                E.dma(bass.AP(wstage, 0, [[1024, 128], [128, 8], [1, 128]]),
                      wqkv_in.rearrange("(c p) o -> p c o", p=128)[:, :, col0:col0 + 128])
                E.s("copy", wbf[:, 0:1024], wstage[:, 0:1024])
            for tt in range(WN // 128):
                for c in range(8):
                    E.mm(ps[:, 0:128],
                         bass.AP(shxT, c * WN + tt * 128, [[8 * WN, 128], [1, 128]]),
                         wv_bf[:, c * 128:(c + 1) * 128], c == 0, c == 7)
                E.v("tensor_copy",
                    bass.AP(V, tt * 1024 + hp * 128, [[18 * 1024, 128], [1, 128]]),
                    ps[:, 0:128])
            for (wbf, QKT, scol, cosX, sinX) in ((wq_bf, QT, 0, cosS, sinS),
                                                 (wk_bf, KT, 1, cosI, sinI)):
                for lg in range((WN + 511) // 512):
                    j0 = lg * 512
                    w = min(512, WN - j0)
                    for c in range(8):
                        E.mm(ps[:, 0:w], wbf[:, c * 128:(c + 1) * 128],
                             bass.AP(shxT, c * WN + j0, [[8 * WN, 128], [1, w]]),
                             c == 0, c == 7)
                    E.s("activation", sqv[:, 0:w], ps[:, 0:w], AF.Square, 0.0, 1.0)
                    E.mm(ps2[0:2, 0:w], b2[:], sqv[:, 0:w], True, True)
                    E.s("activation", nsb[:, 0:w], ps2[0:2, 0:w], AF.Sqrt, 0.0, 1.0)
                    E.v("tensor_scalar_max", nsb[:, 0:w], nsb[:, 0:w], 1e-12)
                    E.v("reciprocal", nsb[:, 0:w], nsb[:, 0:w])
                    E.mm(ps2[:, 0:w], b2t[:], nsb[:, 0:w], True, True)
                    E.v("tensor_copy", qsb[:, 0:w], ps[:, 0:w])
                    E.v("tensor_mul", qsb[:, 0:w], qsb[:, 0:w], ps2[:, 0:w])
                    E.v("tensor_scalar", qsb[:, 0:w], qsb[:, 0:w], qs[:, scol:scol + 1], None, ALU.mult)
                    E.s("copy", attn[:, 0:w], qsb[:, 0:w])
                    E.mm(ps2[:, 0:w], rot[:], attn[:, 0:w], True, True)
                    E.v("tensor_tensor", t1[:, 0:w], qsb[:, 0:w], cosX[:, j0:j0 + w], ALU.mult)
                    E.v("tensor_tensor", qsb[:, 0:w], ps2[:, 0:w], sinX[:, j0:j0 + w], ALU.mult)
                    E.v("tensor_tensor", QKT[:, j0:j0 + w], t1[:, 0:w], qsb[:, 0:w], ALU.add)
            for h in range(2):
                hb = h * 64
                for wdw in range(8):
                    jb0 = wdw * 256
                    qb0 = 256 + wdw * 256
                    for qt in range(2):
                        q0w = qb0 + qt * 128
                        E.mm(ps[:, 0:512],
                             QT[hb:hb + 64, q0w:q0w + 128],
                             KT[hb:hb + 64, jb0:jb0 + 512], True, True)
                        mi = (0 if (wdw == 0) else 2) + qt
                        E.v("tensor_tensor", msc[:], ps[:, 0:512],
                            lmask[:, mi * 512:(mi + 1) * 512], ALU.add)
                        E.v("reduce_max", scr[:, 0:1], msc[:], AX.X)
                        E.v("tensor_scalar_mul", scr[:, 1:2], scr[:, 0:1], -8.0)
                        E.s("activation", msc[:], msc[:], AF.Exp, scr[:, 1:2], 8.0,
                            accum_out=scr[:, 2:3])
                        E.v("reciprocal", scr[:, 3:4], scr[:, 2:3])
                        E.v("tensor_scalar", attn[:], msc[:], scr[:, 3:4], None, ALU.mult)
                        for jc in range(4):
                            E.t("transpose", psT[:, jc * 128:(jc + 1) * 128],
                                attn[:, jc * 128:(jc + 1) * 128], identb[:])
                        E.v("tensor_copy", attnT[:], psT[:, 0:512])
                        for jc in range(4):
                            vtt = 2 * wdw + jc
                            E.mm(psO[0:64, 0:128],
                                 bass.AP(V, vtt * 1024 + hp * 128 + hb, [[18 * 1024, 128], [1, 64]]),
                                 attnT[:, jc * 128:(jc + 1) * 128], jc == 0, jc == 3)
                        E.v("tensor_copy",
                            attnoutT[hb:hb + 64,
                                     hp * OWN + wdw * 256 + qt * 128: hp * OWN + wdw * 256 + (qt + 1) * 128],
                            psO[0:64, 0:128])
        for tt in range(OWN // 128):
            for og in range(2):
                for hp in range(8):
                    E.mm(ps[:, 0:512],
                         attnoutT[:, hp * OWN + tt * 128:hp * OWN + (tt + 1) * 128],
                         wo_bf[:, hp * D + og * 512:hp * D + og * 512 + 512],
                         hp == 0, hp == 7)
                E.v("tensor_copy", xt[:, og * 512:(og + 1) * 512], ps[:, 0:512])
            E.dma(t1[:, 0:512], xw_in[257 + tt * 128:257 + (tt + 1) * 128, 0:512])
            E.dma(qsb[:, 0:512], xw_in[257 + tt * 128:257 + (tt + 1) * 128, 512:1024])
            E.v("tensor_add", xt[:, 0:512], xt[:, 0:512], t1[:, 0:512])
            E.v("tensor_add", xt[:, 512:1024], xt[:, 512:1024], qsb[:, 0:512])
            E.dma(r_out[tt * 128:(tt + 1) * 128, :], xt[:])

        blk = es.enter_context(nc.Block())
        E.replay(blk)
    return nc


def build_p9b():
    nc = bass.Bass("TRN2", target_bir_lowering=False)
    T = 2048
    OC = 22
    xw_in = nc.dram_tensor("xw", [T + 1, D], F32, kind="ExternalInput").ap()
    w1p_in = nc.dram_tensor("w1p", [D, 2 * 2816], F32, kind="ExternalInput").ap()
    b1p_in = nc.dram_tensor("b1p", [128, 2 * OC], F32, kind="ExternalInput").ap()
    w2p_in = nc.dram_tensor("w2p", [2816, D], F32, kind="ExternalInput").ap()
    b2row_in = nc.dram_tensor("b2row", [1, D], F32, kind="ExternalInput").ap()
    g_in = nc.dram_tensor("lng", [1, D], F32, kind="ExternalInput").ap()
    bln_in = nc.dram_tensor("lnb", [1, D], F32, kind="ExternalInput").ap()
    identb_in = nc.dram_tensor("identb", [128, 128], BF16, kind="ExternalInput").ap()
    identf_in = nc.dram_tensor("identf", [128, 128], F32, kind="ExternalInput").ap()
    ones_in = nc.dram_tensor("ones1", [1, 128], F32, kind="ExternalInput").ap()
    r_out = nc.dram_tensor("r", [T, D], F32, kind="ExternalOutput").ap()

    with ExitStack() as es:
        sb = lambda n, shp, dt=F32: es.enter_context(nc.sbuf_tensor(n, shp, dt))
        S = es.enter_context(nc.semaphore("S"))
        identb = sb("identb_s", [128, 128], BF16)
        identf = sb("identf_s", [128, 128], F32)
        ones1 = sb("ones1_s", [1, 128])
        b1p = sb("b1p_s", [128, 2 * OC])
        kcol = sb("kcol", [128, 4])
        scr = sb("scr", [128, 8])
        rows = sb("rows", [1, D])
        gB = sb("gB", [128, D])
        blnB = sb("blnB", [128, D])
        b2B = sb("b2B", [128, D])
        xt = sb("xt", [128, D])
        sq = sb("sq", [128, D])
        xnbf = sb("xnbf", [128, D], BF16)
        xnT = sb("xnT", [128, 8 * T], BF16)
        outacc = sb("outacc", [128, 8 * T])
        w1a = sb("w1a", [128, 8 * 128], BF16)
        w1g = sb("w1g", [128, 8 * 128], BF16)
        w2c = sb("w2c", [128, D], BF16)
        wstage = sb("wstage", [128, 1024])
        asb = sb("asb", [128, 512])
        ggs = sb("ggs", [128, 512])
        h2T = sb("h2T", [128, 4 * 512], BF16)
        ps = es.enter_context(nc.psum_tensor("ps", [128, 512], F32))
        ps2 = es.enter_context(nc.psum_tensor("ps2", [128, 512], F32))
        psr = es.enter_context(nc.psum_tensor("psr", [128, 512], F32))
        psT = es.enter_context(nc.psum_tensor("psT", [128, 1024], BF16))
        psF = es.enter_context(nc.psum_tensor("psF", [128, 512], F32))

        E = Emit(nc, S)
        E.v("memset", kcol[:, 0:1], 1e-5)
        E.dma(identb[:], identb_in)
        E.dma(identf[:], identf_in)
        E.dma(ones1[:], ones_in)
        E.dma(b1p[:], b1p_in)
        E.dma(rows[:], g_in)
        bcast_row(E, nc, ps, gB, rows[:], ones1[:])
        E.dma(rows[:], bln_in)
        bcast_row(E, nc, ps, blnB, rows[:], ones1[:])
        E.dma(rows[:], b2row_in)
        bcast_row(E, nc, ps, b2B, rows[:], ones1[:])

        for i in range(T // 128):
            E.dma(xt[:, 0:512], xw_in[1 + i * 128:1 + (i + 1) * 128, 0:512])
            E.dma(xt[:, 512:1024], xw_in[i * 128:(i + 1) * 128, 512:1024])
            E.v("reduce_sum", scr[:, 0:1], xt[:], AX.X)
            E.s("mul", scr[:, 1:2], scr[:, 0:1], 1.0 / D)
            E.v("tensor_scalar", xt[:], xt[:], scr[:, 1:2], None, ALU.subtract)
            E.s("activation", sq[:], xt[:], AF.Square, 0.0, 1.0, accum_out=scr[:, 0:1])
            E.s("activation", scr[:, 2:3], scr[:, 0:1], AF.Ln, kcol[:, 0:1], 1.0 / D)
            E.s("activation", scr[:, 3:4], scr[:, 2:3], AF.Exp, 0.0, -0.5)
            E.v("tensor_scalar", xt[:], xt[:], scr[:, 3:4], None, ALU.mult)
            E.v("tensor_mul", xt[:], xt[:], gB[:])
            E.v("tensor_add", xt[:], xt[:], blnB[:])
            E.s("copy", xnbf[:], xt[:])
            for dc in range(8):
                E.t("transpose", psT[:, dc * 128:(dc + 1) * 128], xnbf[:, dc * 128:(dc + 1) * 128],
                    identb[:])
            E.v("tensor_copy", bass.AP(xnT, i * 128, [[8 * T, 128], [T, 8], [1, 128]]), psT[:])

        for oc in range(OC):
            E.dma(bass.AP(wstage, 0, [[1024, 128], [128, 8], [1, 128]]),
                  w1p_in.rearrange("(c p) o -> p c o", p=128)[:, :, oc * 128:(oc + 1) * 128])
            E.s("copy", w1a[:], wstage[:, 0:1024])
            E.dma(bass.AP(wstage, 0, [[1024, 128], [128, 8], [1, 128]]),
                  w1p_in.rearrange("(c p) o -> p c o", p=128)[:, :, 2816 + oc * 128:2816 + (oc + 1) * 128])
            E.s("copy", w1g[:], wstage[:, 0:1024])
            E.dma(wstage[:, 0:D], w2p_in[oc * 128:(oc + 1) * 128, :])
            E.s("copy", w2c[:], wstage[:, 0:D])
            for lg in range(T // 512):
                tsl = slice(lg * 512, (lg + 1) * 512)
                for c in range(8):
                    E.mm(ps[:, 0:512], w1a[:, c * 128:(c + 1) * 128],
                         bass.AP(xnT, c * T + lg * 512, [[8 * T, 128], [1, 512]]),
                         c == 0, c == 7)
                for c in range(8):
                    E.mm(ps2[:, 0:512], w1g[:, c * 128:(c + 1) * 128],
                         bass.AP(xnT, c * T + lg * 512, [[8 * T, 128], [1, 512]]),
                         c == 0, c == 7)
                E.v("tensor_scalar", asb[:], ps[:, 0:512], b1p[:, oc:oc + 1], None, ALU.add)
                E.s("activation", ggs[:], ps2[:, 0:512], AF.Gelu, b1p[:, OC + oc:OC + oc + 1], 1.0)
                E.v("tensor_tensor", h2T[:, lg * 512:(lg + 1) * 512], asb[:], ggs[:], ALU.mult)
            for lg in range(T // 512):
                for d2c in range(8):
                    E.mm(psr[:, 0:512], w2c[:, d2c * 128:(d2c + 1) * 128],
                         h2T[:, lg * 512:(lg + 1) * 512], True, True)
                    dst = outacc[:, d2c * T + lg * 512: d2c * T + (lg + 1) * 512]
                    if oc == 0:
                        E.v("tensor_copy", dst, psr[:, 0:512])
                    else:
                        E.v("tensor_add", dst, dst, psr[:, 0:512])

        for tt in range(T // 128):
            for half in range(2):
                for j in range(4):
                    d2c = half * 4 + j
                    E.t("transpose", psF[:, j * 128:(j + 1) * 128],
                        outacc[:, d2c * T + tt * 128: d2c * T + (tt + 1) * 128], identf[:])
                E.v("tensor_copy", xt[:, half * 512:(half + 1) * 512], psF[:, 0:512])
            E.v("tensor_add", xt[:], xt[:], b2B[:])
            E.dma(sq[:, 0:512], xw_in[1 + tt * 128:1 + (tt + 1) * 128, 0:512])
            E.dma(sq[:, 512:1024], xw_in[1 + tt * 128:1 + (tt + 1) * 128, 512:1024])
            E.v("tensor_add", xt[:], xt[:], sq[:])
            E.dma(r_out[tt * 128:(tt + 1) * 128, :], xt[:])

        blk = es.enter_context(nc.Block())
        E.replay(blk)
    return nc


_P9A = None
_P9B = None


def p9_consts():
    import ml_dtypes
    DIM_HEAD, WINDOW, n = 64, 256, 4096
    half = np.arange(0, DIM_HEAD, 2, dtype=np.float32)
    inv_freq = 1.0 / 10000.0 ** (half / DIM_HEAD)
    t = np.arange(-257 + 1 - 255, n, dtype=np.float32)
    identb = np.eye(128, dtype=np.float32).astype(ml_dtypes.bfloat16)
    identf = np.eye(128, dtype=np.float32)
    rotm = np.zeros((128, 128), np.float32)
    for hh in range(2):
        for m in range(64):
            if m < 32:
                rotm[hh * 64 + m + 32, hh * 64 + m] = -1.0
            else:
                rotm[hh * 64 + m - 32, hh * 64 + m] = 1.0
    rot = rotm.astype(ml_dtypes.bfloat16)
    b2 = np.zeros((128, 2), np.float32)
    b2[0:64, 0] = 1.0
    b2[64:128, 1] = 1.0
    b2t = np.ascontiguousarray(b2.T)
    masks = np.zeros((4, 128, 512), np.float32)
    for variant in range(2):
        for qt in range(2):
            i = (np.arange(128) + qt * 128)[:, None]
            j2 = np.arange(512)[None, :]
            prev_ok = (i <= j2) & (j2 < 256)
            own_ok = (j2 >= 256) & ((j2 - 256) <= i)
            ok = prev_ok | own_ok
            if variant == 0:
                ok = own_ok
            masks[variant * 2 + qt] = np.where(ok, 0.0, -np.float32(np.finfo(np.float32).max))
    return identb, identf, rot, b2, b2t, masks


def p9_rope_tables(tglob):
    import ml_dtypes
    DIM_HEAD, WINDOW, n = 64, 256, 4096
    half = np.arange(0, DIM_HEAD, 2, dtype=np.float32)
    inv_freq = (1.0 / 10000.0 ** (half / DIM_HEAD)).astype(np.float32)
    freqs = tglob[:, None].astype(np.float32) * inv_freq[None, :]
    freqs = np.concatenate([freqs, freqs], -1)
    base_scale = ((half + 0.4 * DIM_HEAD) / (1.4 * DIM_HEAD)).astype(np.float32)
    power = ((tglob - n // 2) / (WINDOW // 2)).astype(np.float32)
    scale = base_scale[None, :] ** power[:, None]
    scale = np.concatenate([scale, scale], -1)
    cos, sin = np.cos(freqs), np.sin(freqs)
    cS = (cos * scale).T
    sS = (sin * scale).T
    cI = (cos / scale).T
    sI = (sin / scale).T
    f = lambda a: np.ascontiguousarray(np.tile(a, (2, 1))).astype(ml_dtypes.bfloat16)
    return f(cS), f(sS), f(cI), f(sI)


def run_p9a(x_full, IN, layer):
    global _P9A
    if _P9A is None:
        _P9A = build_p9a()
    identb, identf, rot, b2, b2t, masks = p9_consts()
    qs2 = np.zeros((128, 2), np.float32)
    qs2[:, 0] = np.tile(IN["L1_ffn_q_scale"][layer], 2)
    qs2[:, 1] = np.tile(IN["L1_ffn_k_scale"][layer], 2)
    maps = []
    for c in range(NCORE):
        b, sh = c // 2, c % 2
        q0 = sh * 2048
        xb = x_full.reshape(4, 4096, D)[b]
        xw = np.zeros((2305, D), np.float32)
        lo = q0 - 257
        src_lo = max(lo, 0)
        xw[src_lo - lo:] = xb[src_lo:q0 + 2048]
        tglob = np.arange(q0 - 256, q0 + 2048, dtype=np.float32)
        cS, sS, cI, sI = p9_rope_tables(tglob)
        maps.append({
            "xw": xw,
            "wqkv": IN["L1_ffn_Wqkv"][layer],
            "wo": IN["L1_ffn_Wo"][layer],
            "qs": qs2,
            "cosS": cS, "sinS": sS, "cosI": cI, "sinI": sI,
            "lmask": masks if sh == 0 else np.concatenate([masks[2:], masks[2:]], axis=0),
            "rot": rot, "b2sel": b2, "b2t": b2t, "identb": identb,
        })
    res = run_bass_kernel_spmd(_P9A, maps, core_ids=list(range(NCORE)))
    out = np.zeros((4, 4096, D), np.float32)
    for c in range(NCORE):
        b, sh = c // 2, c % 2
        out[b, sh * 2048:(sh + 1) * 2048] = res.results[c]["r"]
    return out.reshape(-1, D)


def run_p9b(x_full, IN, layer):
    global _P9B
    if _P9B is None:
        _P9B = build_p9b()
    identb, identf, rot, b2, b2t, masks = p9_consts()
    W1 = IN["L1_ffn_W1"][layer]
    b1 = IN["L1_ffn_b1"][layer]
    W2 = IN["L1_ffn_W2"][layer]
    w1p = np.zeros((D, 2 * 2816), np.float32)
    w1p[:, 0:2730] = W1[:, 0:2730]
    w1p[:, 2816:2816 + 2730] = W1[:, 2730:]
    b1p_flat = np.zeros(2 * 2816, np.float32)
    b1p_flat[0:2730] = b1[0:2730]
    b1p_flat[2816:2816 + 2730] = b1[2730:]
    b1p = np.zeros((128, 44), np.float32)
    b1p[:, 0:22] = b1p_flat[0:2816].reshape(22, 128).T
    b1p[:, 22:44] = b1p_flat[2816:].reshape(22, 128).T
    w2p = np.zeros((2816, D), np.float32)
    w2p[0:2730] = W2
    base = {
        "w1p": w1p, "b1p": b1p, "w2p": w2p,
        "b2row": IN["L1_ffn_b2"][layer].reshape(1, D),
        "lng": IN["L1_ffn_ln_g"][layer].reshape(1, D),
        "lnb": IN["L1_ffn_ln_b"][layer].reshape(1, D),
        "identb": identb, "identf": identf,
        "ones1": np.ones((1, 128), np.float32),
    }
    maps = []
    for c in range(NCORE):
        b, sh = c // 2, c % 2
        q0 = sh * 2048
        xb = x_full.reshape(4, 4096, D)[b]
        xw = np.zeros((2049, D), np.float32)
        lo = q0 - 1
        src_lo = max(lo, 0)
        xw[src_lo - lo:] = xb[src_lo:q0 + 2048]
        m = dict(base)
        m["xw"] = xw
        maps.append(m)
    res = run_bass_kernel_spmd(_P9B, maps, core_ids=list(range(NCORE)))
    out = np.zeros((4, 4096, D), np.float32)
    for c in range(NCORE):
        b, sh = c // 2, c % 2
        out[b, sh * 2048:(sh + 1) * 2048] = res.results[c]["r"]
    return out.reshape(-1, D)


def build_pln():
    nc = bass.Bass("TRN2", target_bir_lowering=False)
    x_in = nc.dram_tensor("x", [TC, D], F32, kind="ExternalInput").ap()
    s_out = nc.dram_tensor("s", [TC, D], F32, kind="ExternalOutput").ap()
    with ExitStack() as es:
        sb = lambda n, shp, dt=F32: es.enter_context(nc.sbuf_tensor(n, shp, dt))
        S = es.enter_context(nc.semaphore("S"))
        kcol = sb("kcol", [128, 4])
        scr = sb("scr", [128, 8])
        xt = sb("xt", [128, D])
        sq = sb("sq", [128, D])
        E = Emit(nc, S)
        E.v("memset", kcol[:, 0:1], 1e-5)
        for i in range(TC // 128):
            E.dma(xt[:], x_in[i * 128:(i + 1) * 128, :])
            E.v("reduce_sum", scr[:, 0:1], xt[:], AX.X)
            E.s("mul", scr[:, 1:2], scr[:, 0:1], 1.0 / D)
            E.v("tensor_scalar", xt[:], xt[:], scr[:, 1:2], None, ALU.subtract)
            E.s("activation", sq[:], xt[:], AF.Square, 0.0, 1.0, accum_out=scr[:, 0:1])
            E.s("activation", scr[:, 2:3], scr[:, 0:1], AF.Ln, kcol[:, 0:1], 1.0 / D)
            E.s("activation", scr[:, 3:4], scr[:, 2:3], AF.Exp, 0.0, -0.5)
            E.v("tensor_scalar", xt[:], xt[:], scr[:, 3:4], None, ALU.mult)
            E.dma(s_out[i * 128:(i + 1) * 128, :], xt[:])
        blk = es.enter_context(nc.Block())
        E.replay(blk)
    return nc


def build_padd():
    nc = bass.Bass("TRN2", target_bir_lowering=False)
    a_in = nc.dram_tensor("a", [TC, D], F32, kind="ExternalInput").ap()
    b_in = nc.dram_tensor("b", [TC, D], F32, kind="ExternalInput").ap()
    r_out = nc.dram_tensor("r", [TC, D], F32, kind="ExternalOutput").ap()
    with ExitStack() as es:
        sb = lambda n, shp, dt=F32: es.enter_context(nc.sbuf_tensor(n, shp, dt))
        S = es.enter_context(nc.semaphore("S"))
        at = sb("at", [128, D])
        bt = sb("bt", [128, D])
        E = Emit(nc, S)
        for i in range(TC // 128):
            E.dma(at[:], a_in[i * 128:(i + 1) * 128, :])
            E.dma(bt[:], b_in[i * 128:(i + 1) * 128, :])
            E.v("tensor_add", at[:], at[:], bt[:])
            E.dma(r_out[i * 128:(i + 1) * 128, :], at[:])
        blk = es.enter_context(nc.Block())
        E.replay(blk)
    return nc


_PLN = None
_PADD = None


def run_pln(x_full):
    global _PLN
    if _PLN is None:
        _PLN = build_pln()
    maps = [{"x": np.ascontiguousarray(x_full[c * TC:(c + 1) * TC])} for c in range(NCORE)]
    res = run_bass_kernel_spmd(_PLN, maps, core_ids=list(range(NCORE)))
    return np.concatenate([r["s"] for r in res.results], axis=0)


def run_padd(a_full, b_full):
    global _PADD
    if _PADD is None:
        _PADD = build_padd()
    maps = [{"a": np.ascontiguousarray(a_full[c * TC:(c + 1) * TC]),
             "b": np.ascontiguousarray(b_full[c * TC:(c + 1) * TC])} for c in range(NCORE)]
    res = run_bass_kernel_spmd(_PADD, maps, core_ids=list(range(NCORE)))
    return np.concatenate([r["r"] for r in res.results], axis=0)


def kernel(**inputs):
    IN = inputs
    x = np.asarray(inputs["L0_attn_a_x"], np.float32)
    r = np.ascontiguousarray(x.reshape(-1, D))

    y0 = run_p1(x, IN["L0_attn_a_x_proj_w"], IN["L0_attn_a_dt_proj_w"],
                IN["L0_attn_a_dt_proj_b"], IN["L0_attn_a_A_log"], IN["L0_attn_a_D_param"])
    r = run_p2(y0.reshape(-1, D), r, IN["L0_attn_a_ln_a"], IN["L0_attn_a_ln_b"])
    r = run_moe(r, IN)
    r = run_p6(r, IN)
    r = run_moe(r, IN)

    yT = run_p7(r, IN)
    r = run_p8(yT, r, IN)
    for _rep in range(2):
        s = run_pln(r)
        for i in range(6):
            s = run_p9a(s, IN, i)
            s = run_p9b(s, IN, i)
        r = run_padd(r, s)
        if _rep == 0:
            r = run_p10(r, IN)
    return r.reshape(4, 4096, D)


# version 02
# speedup vs baseline: 1.2519x; 1.2519x over previous
import numpy as np
from contextlib import ExitStack
import concourse.bass as bass
import concourse.mybir as mybir
from concourse.bass_utils import run_bass_kernel_spmd

F32 = mybir.dt.float32
BF16 = mybir.dt.bfloat16
AF = mybir.ActivationFunctionType
ALU = mybir.AluOpType
AX = mybir.AxisListType

NCORE = 8
BT = 16384
TC = BT // NCORE
D = 1024


class Emit:
    ENGS = ("sync", "vector", "scalar", "tensor", "gpsimd")

    def __init__(self, nc, sems):
        self.nc = nc
        self.sem = sems["sync"] if isinstance(sems, dict) else sems
        self.tot = 0
        self.prog = []

    def op(self, eng, f, *a, dma=False, wait=True, **kw):
        w = 16 if dma else 1
        self.prog.append((eng, f, a, kw, self.tot if wait else -1, w))
        self.tot += w

    def dma(self, dst, src, **kw):
        self.op("sync", self.nc.sync.dma_start, dst, src, dma=True, **kw)

    def dma2(self, dst, src, **kw):
        self.op("gpsimd", self.nc.gpsimd.dma_start, dst, src, dma=True, **kw)

    def v(self, name, *a, **kw):
        self.op("vector", getattr(self.nc.vector, name), *a, **kw)

    def s(self, name, *a, **kw):
        self.op("scalar", getattr(self.nc.scalar, name), *a, **kw)

    def t(self, name, *a, **kw):
        self.op("tensor", getattr(self.nc.tensor, name), *a, **kw)

    def g(self, name, *a, **kw):
        self.op("gpsimd", getattr(self.nc.gpsimd, name), *a, **kw)

    def mm(self, out, lhsT, rhs, start, stop, wait=None):
        if wait is None:
            wait = bool(start)
        self.op("tensor", self.nc.tensor.matmul, out, lhsT, rhs,
                wait=wait, start=start, stop=stop)

    def replay(self, block):
        for eng_name in self.ENGS:
            ops = [p for p in self.prog if p[0] == eng_name]
            if not ops:
                continue

            def body(eng, ops=ops):
                for (_, f, a, kw, wv, inc) in ops:
                    if wv > 0:
                        eng.wait_ge(self.sem, wv)
                    f(*a, **kw).then_inc(self.sem, inc)
            getattr(block, eng_name)(body)

        def tail(eng):
            eng.wait_ge(self.sem, self.tot)
        block.gpsimd(tail)


def mk_sems(nc, es):
    return {e: es.enter_context(nc.semaphore("S_" + e)) for e in Emit.ENGS}


def bcast_row(E, nc, ps_pool, out_sb, row_sb, ones1):
    W = row_sb.ap[-1][1] if hasattr(row_sb, "ap") else None
    W = row_sb.ap[-1][1]
    off = 0
    while off < W:
        w = min(512, W - off)
        E.mm(ps_pool[:, 0:w], ones1, row_sb[:, off:off + w], True, True)
        E.v("tensor_copy", out_sb[:, off:off + w], ps_pool[:, 0:w])
        off += w


def build_p2():
    nc = bass.Bass("TRN2", target_bir_lowering=False)
    y_in = nc.dram_tensor("y", [TC, D], F32, kind="ExternalInput").ap()
    x_in = nc.dram_tensor("x", [TC, D], F32, kind="ExternalInput").ap()
    lna_in = nc.dram_tensor("lna", [1, D], F32, kind="ExternalInput").ap()
    lnb_in = nc.dram_tensor("lnb", [1, D], F32, kind="ExternalInput").ap()
    ones_in = nc.dram_tensor("ones1", [1, 128], F32, kind="ExternalInput").ap()
    r_out = nc.dram_tensor("r", [TC, D], F32, kind="ExternalOutput").ap()

    with ExitStack() as es:
        sb = lambda n, shp, dt=F32: es.enter_context(nc.sbuf_tensor(n, shp, dt))
        S = mk_sems(nc, es)
        yt = sb("yt", [128, D])
        xt = sb("xt", [128, D])
        sq = sb("sq", [128, D])
        yc = sb("yc", [128, D])
        lnaB = sb("lnaB", [128, D])
        lnbB = sb("lnbB", [128, D])
        rows = sb("rows", [1, 2 * D])
        ones1 = sb("ones1s", [1, 128])
        scr = sb("scr", [128, 4])
        ps = es.enter_context(nc.psum_tensor("ps", [128, 512], F32))
        E = Emit(nc, S)

        E.dma(rows[:, 0:D], lna_in)
        E.dma(rows[:, D:2 * D], lnb_in)
        E.dma(ones1[:], ones_in)
        bcast_row(E, nc, ps, lnaB, rows[:, 0:D], ones1[:])
        bcast_row(E, nc, ps, lnbB, rows[:, D:2 * D], ones1[:])

        ntile = TC // 128
        for i in range(ntile):
            E.dma(yt[:], y_in[i * 128:(i + 1) * 128, :])
            E.dma(xt[:], x_in[i * 128:(i + 1) * 128, :])
            E.v("reduce_sum", scr[:, 0:1], yt[:], AX.X)
            E.s("mul", scr[:, 1:2], scr[:, 0:1], 1.0 / D)
            E.v("tensor_scalar", yc[:], yt[:], scr[:, 1:2], None, ALU.subtract)
            E.s("activation", sq[:], yc[:], AF.Square, 0.0, 1.0, accum_out=scr[:, 0:1])
            E.s("activation", scr[:, 2:3], scr[:, 0:1], AF.Sqrt, 0.0, 1.0 / (D - 1))
            E.v("tensor_scalar_add", scr[:, 2:3], scr[:, 2:3], 1e-12)
            E.v("reciprocal", scr[:, 3:4], scr[:, 2:3])
            E.v("tensor_scalar", yc[:], yc[:], scr[:, 3:4], None, ALU.mult)
            E.v("tensor_mul", yc[:], yc[:], lnaB[:])
            E.v("tensor_add", yc[:], yc[:], lnbB[:])
            E.v("tensor_add", yc[:], yc[:], xt[:])
            E.dma(r_out[i * 128:(i + 1) * 128, :], yc[:])

        blk = es.enter_context(nc.Block())
        E.replay(blk)
    return nc


def p2_inmaps(y_full, x_full, lna, lnb):
    maps = []
    for c in range(NCORE):
        sl = slice(c * TC, (c + 1) * TC)
        maps.append({
            "y": np.ascontiguousarray(y_full[sl]),
            "x": np.ascontiguousarray(x_full[sl]),
            "lna": lna.reshape(1, D),
            "lnb": lnb.reshape(1, D),
            "ones1": np.ones((1, 128), np.float32),
        })
    return maps


_P2 = None


def run_p2(y_full, x_full, lna, lnb):
    global _P2
    if _P2 is None:
        _P2 = build_p2()
    res = run_bass_kernel_spmd(_P2, p2_inmaps(y_full, x_full, lna, lnb), core_ids=list(range(NCORE)))
    return np.concatenate([r["r"] for r in res.results], axis=0)


def emit_ln_block(E, nc, src_ap, row0, xt, sq, scr, xnbf, psT, xnT_dst, identb, epscol):
    E.dma(xt[:], src_ap[row0:row0 + 128, :])
    E.v("reduce_sum", scr[:, 0:1], xt[:], AX.X)
    E.s("mul", scr[:, 1:2], scr[:, 0:1], 1.0 / D)
    E.v("tensor_scalar", xt[:], xt[:], scr[:, 1:2], None, ALU.subtract)
    E.s("activation", sq[:], xt[:], AF.Square, 0.0, 1.0, accum_out=scr[:, 0:1])
    E.s("activation", scr[:, 2:3], scr[:, 0:1], AF.Ln, epscol, 1.0 / D)
    E.s("activation", scr[:, 3:4], scr[:, 2:3], AF.Exp, 0.0, -0.5)
    E.v("tensor_scalar", xnbf[:], xt[:], scr[:, 3:4], None, ALU.mult)
    for dc in range(8):
        E.t("transpose", psT[:, dc * 128:(dc + 1) * 128], xnbf[:, dc * 128:(dc + 1) * 128], identb[:],
            wait=(dc == 0))
    E.v("tensor_copy", xnT_dst, psT[:])


def build_p1():
    nc = bass.Bass("TRN2", target_bir_lowering=False)
    L = 4096
    LH = 2048
    x_in = nc.dram_tensor("x", [L, D], F32, kind="ExternalInput").ap()
    xpwT_in = nc.dram_tensor("xpwT", [D, 96], F32, kind="ExternalInput").ap()
    dtwT_in = nc.dram_tensor("dtwT", [64, 512], F32, kind="ExternalInput").ap()
    dtb_in = nc.dram_tensor("dtb", [512], F32, kind="ExternalInput").ap()
    alog_in = nc.dram_tensor("alog", [512, 16], F32, kind="ExternalInput").ap()
    dpar_in = nc.dram_tensor("dpar", [512], F32, kind="ExternalInput").ap()
    sel_in = nc.dram_tensor("sel", [32, 32, 128], F32, kind="ExternalInput").ap()
    identb_in = nc.dram_tensor("identb", [128, 128], BF16, kind="ExternalInput").ap()
    identf_in = nc.dram_tensor("identf", [128, 128], F32, kind="ExternalInput").ap()
    y_out = nc.dram_tensor("y", [L, 512], F32, kind="ExternalOutput").ap()

    with ExitStack() as es:
        sb = lambda n, shp, dt=F32: es.enter_context(nc.sbuf_tensor(n, shp, dt))
        S = mk_sems(nc, es)
        identb = sb("identb_s", [128, 128], BF16)
        identf = sb("identf_s", [128, 128], F32)
        sel = sb("sel_s", [32, 32 * 128])
        xpw_bf = sb("xpw_bf", [128, 8 * 96], BF16)
        dtw_bf = sb("dtw_bf", [64, 512], BF16)
        cols = sb("cols", [128, 16])
        A_sb = sb("A_sb", [128, 64])
        hcar = sb("hcar", [128, 64])
        kcol = sb("kcol", [128, 4])
        scr = sb("scr", [128, 8])
        xt = sb("xt", [128, D])
        sq = sb("sq", [128, D])
        xnbf = sb("xnbf", [128, D], BF16)
        xnT = sb("xnT", [128, 8 * LH], BF16)
        dr_bf = sb("dr_bf", [64, LH], BF16)
        BC32 = sb("BC32", [32, LH])
        deltaT = sb("deltaT", [128, 4 * LH])
        BmB = sb("BmB", [128, LH])
        CbB = sb("CbB", [128, LH])
        dAt = sb("dAt", [128, LH])
        BXt = sb("BXt", [128, LH])
        ht = sb("ht", [128, LH])
        ut = sb("ut", [128, LH])
        yT = sb("yT", [128, 4 * LH])
        ytile = sb("ytile", [128, 512])
        dbcT = sb("dbcT", [96, LH])
        ps = es.enter_context(nc.psum_tensor("ps", [128, 512], F32))
        psT = es.enter_context(nc.psum_tensor("psT", [128, 1024], BF16))
        psF = es.enter_context(nc.psum_tensor("psF", [128, 512], F32))

        E = Emit(nc, S)
        E.v("memset", kcol[:, 0:1], 1e-5)
        E.v("memset", kcol[:, 1:2], 1.0)
        E.dma(identb[:], identb_in)
        E.dma(identf[:], identf_in)
        E.dma(sel[:], sel_in.rearrange("k j p -> k (j p)"))
        E.dma(bass.AP(xt, 0, [[1024, 128], [96, 8], [1, 96]]),
              xpwT_in.rearrange("(c p) r -> p c r", p=128))
        E.s("copy", xpw_bf[:], xt[:, 0:8 * 96])
        E.dma(sq[0:64, 0:512], dtwT_in)
        E.s("copy", dtw_bf[:], sq[0:64, 0:512])
        E.dma(cols[:, 0:4], dtb_in.rearrange("(c p) -> p c", p=128),
              allow_slow_non_contiguous=True)
        E.dma(cols[:, 4:8], dpar_in.rearrange("(c p) -> p c", p=128),
              allow_slow_non_contiguous=True)
        E.dma(bass.AP(A_sb, 0, [[64, 128], [16, 4], [1, 16]]),
              alog_in.rearrange("(c p) n -> p c n", p=128))
        E.s("activation", A_sb[:], A_sb[:], AF.Exp, 0.0, 1.0)
        E.v("tensor_scalar_mul", A_sb[:], A_sb[:], -1.0)

        dh = None
        for lh in range(2):
            l0 = lh * LH
            for i in range(LH // 128):
                dst = bass.AP(xnT, i * 128, [[8 * LH, 128], [LH, 8], [1, 128]])
                emit_ln_block(E, nc, x_in, l0 + i * 128, xt, sq, scr, xnbf, psT, dst, identb,
                              kcol[:, 0:1])
            for lg in range(LH // 512):
                for c in range(8):
                    E.mm(ps[0:96, 0:512],
                         bass.AP(xpw_bf, c * 96, [[8 * 96, 128], [1, 96]]),
                         bass.AP(xnT, c * LH + lg * 512, [[8 * LH, 128], [1, 512]]),
                         c == 0, c == 7)
                E.v("tensor_copy", dbcT[:, lg * 512:(lg + 1) * 512], ps[0:96, 0:512])
            E.s("copy", dr_bf[:], dbcT[0:64, :])
            E.v("tensor_copy", BC32[:], dbcT[64:96, :])
            for dc in range(4):
                for lg in range(LH // 512):
                    E.mm(ps[:, 0:512], dtw_bf[:, dc * 128:(dc + 1) * 128],
                         dr_bf[:, lg * 512:(lg + 1) * 512], True, True)
                    dslice = deltaT[:, dc * LH + lg * 512: dc * LH + (lg + 1) * 512]
                    E.v("tensor_scalar", dAt[:, 0:512], ps[:, 0:512], cols[:, dc:dc + 1], None, ALU.add)
                    E.s("activation", BXt[:, 0:512], dAt[:, 0:512], AF.Abs, 0.0, 1.0)
                    E.s("activation", BXt[:, 0:512], BXt[:, 0:512], AF.Exp, 0.0, -1.0)
                    E.s("activation", BXt[:, 0:512], BXt[:, 0:512], AF.Ln, kcol[:, 1:2], 1.0)
                    E.v("tensor_scalar_max", dAt[:, 0:512], dAt[:, 0:512], 0.0)
                    E.v("tensor_add", dslice, dAt[:, 0:512], BXt[:, 0:512])
            for n in range(16):
                for (dst, j) in ((BmB, n), (CbB, 16 + n)):
                    for lg in range(LH // 512):
                        E.mm(ps[:, 0:512], bass.AP(sel, j * 128, [[32 * 128, 32], [1, 128]]),
                             BC32[:, lg * 512:(lg + 1) * 512], True, True)
                        E.v("tensor_copy", dst[:, lg * 512:(lg + 1) * 512], ps[:, 0:512])
                for dc in range(4):
                    dsl = deltaT[:, dc * LH:(dc + 1) * LH]
                    E.v("tensor_scalar", dAt[:], dsl, A_sb[:, dc * 16 + n:dc * 16 + n + 1], None, ALU.mult)
                    E.s("activation", dAt[:], dAt[:], AF.Exp, 0.0, 1.0)
                    xn_dc = bass.AP(xnT, (XNT_DC_BASE[dc]) * LH, [[8 * LH, 128], [1, LH]])
                    E.v("tensor_mul", ut[:], dsl, xn_dc)
                    E.v("tensor_mul", BXt[:], ut[:], BmB[:])
                    init = 0.0 if lh == 0 else hcar[:, n * 4 + dc:n * 4 + dc + 1]
                    E.v("tensor_tensor_scan", ht[:], dAt[:], BXt[:], init, ALU.mult, ALU.add)
                    E.v("tensor_copy", hcar[:, n * 4 + dc:n * 4 + dc + 1], ht[:, LH - 1:LH])
                    E.v("tensor_mul", ht[:], ht[:], CbB[:])
                    if n == 0:
                        E.v("tensor_copy", yT[:, dc * LH:(dc + 1) * LH], ht[:])
                    else:
                        E.v("tensor_add", yT[:, dc * LH:(dc + 1) * LH], yT[:, dc * LH:(dc + 1) * LH], ht[:])
            for dc in range(4):
                xn_dc = bass.AP(xnT, (XNT_DC_BASE[dc]) * LH, [[8 * LH, 128], [1, LH]])
                E.op("vector", nc.vector.scalar_tensor_tensor, yT[:, dc * LH:(dc + 1) * LH],
                     xn_dc, cols[:, 4 + dc:5 + dc], yT[:, dc * LH:(dc + 1) * LH], ALU.mult, ALU.add)
            for tt in range(LH // 128):
                for dc in range(4):
                    E.t("transpose", psF[:, dc * 128:(dc + 1) * 128],
                        yT[:, dc * LH + tt * 128: dc * LH + (tt + 1) * 128], identf[:],
                        wait=(dc == 0))
                E.v("tensor_copy", ytile[:], psF[:])
                E.dma(y_out[l0 + tt * 128: l0 + (tt + 1) * 128, :], ytile[:])

        blk = es.enter_context(nc.Block())
        E.replay(blk)
    return nc


XNT_DC_BASE = [0, 1, 2, 3]


def p1_inmaps(x_full, xpw, dtw, dtb, alog, dpar):
    import ml_dtypes
    sel = np.zeros((32, 32, 128), np.float32)
    for j in range(32):
        sel[j, j, :] = 1.0
    identb = np.eye(128, dtype=np.float32).astype(ml_dtypes.bfloat16)
    identf = np.eye(128, dtype=np.float32)
    maps = []
    for c in range(NCORE):
        b, dh = c // 2, c % 2
        dsl = slice(dh * 512, (dh + 1) * 512)
        xb = x_full[b]
        if dh == 1:
            xb = np.concatenate([xb[:, 512:], xb[:, :512]], axis=1)
        xpw_r = xpw if dh == 0 else np.concatenate([xpw[:, 512:], xpw[:, :512]], axis=1)
        maps.append({
            "x": np.ascontiguousarray(xb),
            "xpwT": np.ascontiguousarray(xpw_r.T),
            "dtwT": np.ascontiguousarray(dtw[dsl, :].T),
            "dtb": np.ascontiguousarray(dtb[dsl]),
            "alog": np.ascontiguousarray(alog[dsl]),
            "dpar": np.ascontiguousarray(dpar[dsl]),
            "sel": sel,
            "identb": identb,
            "identf": identf,
        })
    return maps


_P1 = None


def run_p1(x_full, xpw, dtw, dtb, alog, dpar):
    global _P1
    if _P1 is None:
        _P1 = build_p1()
    res = run_bass_kernel_spmd(_P1, p1_inmaps(x_full, xpw, dtw, dtb, alog, dpar),
                               core_ids=list(range(NCORE)))
    y = np.zeros((4, 4096, 1024), np.float32)
    for c in range(NCORE):
        b, dh = c // 2, c % 2
        y[b, :, dh * 512:(dh + 1) * 512] = res.results[c]["y"]
    return y


def build_moe():
    nc = bass.Bass("TRN2", target_bir_lowering=False)
    T = TC
    x_in = nc.dram_tensor("x", [T, D], F32, kind="ExternalInput").ap()
    rwT_in = nc.dram_tensor("rwT", [D, 16], F32, kind="ExternalInput").ap()
    eb_in = nc.dram_tensor("eb", [1, 16], F32, kind="ExternalInput").ap()
    gwT_in = nc.dram_tensor("gwT", [16, D, 512], F32, kind="ExternalInput").ap()
    uwT_in = nc.dram_tensor("uwT", [16, D, 512], F32, kind="ExternalInput").ap()
    dwT_in = nc.dram_tensor("dwT", [16, 512, D], F32, kind="ExternalInput").ap()
    sgwT_in = nc.dram_tensor("sgwT", [D, D], F32, kind="ExternalInput").ap()
    suwT_in = nc.dram_tensor("suwT", [D, D], F32, kind="ExternalInput").ap()
    sdwT_in = nc.dram_tensor("sdwT", [D, D], F32, kind="ExternalInput").ap()
    eps_in = nc.dram_tensor("epsB", [128, 16], F32, kind="ExternalInput").ap()
    sel16_in = nc.dram_tensor("sel16", [16, 16, 128], F32, kind="ExternalInput").ap()
    ones_in = nc.dram_tensor("ones1", [1, 128], F32, kind="ExternalInput").ap()
    identb_in = nc.dram_tensor("identb", [128, 128], BF16, kind="ExternalInput").ap()
    identf_in = nc.dram_tensor("identf", [128, 128], F32, kind="ExternalInput").ap()
    r_out = nc.dram_tensor("r", [T, D], F32, kind="ExternalOutput").ap()

    with ExitStack() as es:
        sb = lambda n, shp, dt=F32: es.enter_context(nc.sbuf_tensor(n, shp, dt))
        S = mk_sems(nc, es)
        identb = sb("identb_s", [128, 128], BF16)
        identf = sb("identf_s", [128, 128], F32)
        ones1 = sb("ones1_s", [1, 128])
        sel16 = sb("sel16_s", [16, 16 * 128], BF16)
        epsB = sb("epsB_s", [128, 16])
        ebB = sb("ebB", [128, 16])
        kcol = sb("kcol", [128, 4])
        scr = sb("scr", [128, 8])
        xt = sb("xt", [128, D])
        sq = sb("sq", [128, D])
        xnbf = sb("xnbf", [128, D], BF16)
        xnT = sb("xnT", [128, 8 * T], BF16)
        combT = sb("combT", [16, T], BF16)
        routedT = sb("routedT", [128, 8 * T])
        ssig = sb("ssig", [128, 16])
        sc_t = sb("sc_t", [128, 16])
        w32 = sb("w32", [128, 32])
        w6 = sb("w6", [128, 8])
        gs = sb("gs", [128, 4])
        smask = sb("smask", [128, 16])
        maskedp = sb("maskedp", [128, 16])
        work = sb("work", [128, 16])
        sel4 = sb("sel4", [128, 16])
        outt = sb("outt", [128, D])
        ps = es.enter_context(nc.psum_tensor("ps", [128, 512], F32))
        ps2 = es.enter_context(nc.psum_tensor("ps2", [128, 512], F32))
        psr = es.enter_context(nc.psum_tensor("psr", [128, 512], F32))
        psT = es.enter_context(nc.psum_tensor("psT", [128, 1024], BF16))
        psF = es.enter_context(nc.psum_tensor("psF", [128, 512], F32))
        psM = es.enter_context(nc.psum_tensor("psM", [128, 16], F32))

        E = Emit(nc, S)
        E.v("memset", kcol[:, 0:1], 1e-5)
        E.dma(identb[:], identb_in)
        E.dma(identf[:], identf_in)
        E.dma(ones1[:], ones_in)
        E.dma(routedT[0:16, 0:16 * 128], sel16_in.rearrange("k j p -> k (j p)"))
        E.s("copy", sel16[:], routedT[0:16, 0:16 * 128])
        E.dma(epsB[:], eps_in)
        rw = sb("rw", [128, 8 * 16])
        E.dma(bass.AP(rw, 0, [[128, 128], [16, 8], [1, 16]]),
              rwT_in.rearrange("(c p) e -> p c e", p=128))
        rwb = sb("rwb", [128, 8 * 16], BF16)
        E.s("copy", rwb[:], rw[:])
        ebrow = sb("ebrow", [1, 16])
        E.dma(ebrow[:], eb_in)
        E.mm(psM[:, 0:16], ones1[:], ebrow[:], True, True)
        E.v("tensor_copy", ebB[:], psM[:, 0:16])

        ntile = T // 128
        for i in range(ntile):
            dst = bass.AP(xnT, i * 128, [[8 * T, 128], [T, 8], [1, 128]])
            emit_ln_block(E, nc, x_in, i * 128, xt, sq, scr, xnbf, psT, dst, identb, kcol[:, 0:1])
            for c in range(8):
                E.mm(psM[:, 0:16], bass.AP(xnT, c * T + i * 128, [[8 * T, 128], [1, 128]]),
                     rwb[:, c * 16:(c + 1) * 16], c == 0, c == 7)
            E.s("activation", ssig[:], psM[:, 0:16], AF.Sigmoid, 0.0, 1.0)
            E.v("tensor_add", sc_t[:], ssig[:], ebB[:])
            pairs = [(0, 1), (0, 2), (0, 3), (1, 2), (1, 3), (2, 3)]
            for pi, (a, b) in enumerate(pairs):
                apa = bass.AP(sc_t, a, [[16, 128], [4, 4]])
                apb = bass.AP(sc_t, b, [[16, 128], [4, 4]])
                E.v("tensor_tensor", w32[:, pi * 4:(pi + 1) * 4], apa, apb, ALU.add)
            E.v("tensor_max", gs[:], w32[:, 0:4], w32[:, 4:8])
            E.v("tensor_max", gs[:], gs[:], w32[:, 8:12])
            E.v("tensor_max", gs[:], gs[:], w32[:, 12:16])
            E.v("tensor_max", gs[:], gs[:], w32[:, 16:20])
            E.v("tensor_max", gs[:], gs[:], w32[:, 20:24])
            for pi, (a, b) in enumerate(pairs):
                E.v("tensor_tensor", w6[:, pi:pi + 1], gs[:, a:a + 1], gs[:, b:b + 1], ALU.min)
            E.v("tensor_max", w6[:, 6:7], w6[:, 0:1], w6[:, 1:2])
            E.v("tensor_max", w6[:, 6:7], w6[:, 6:7], w6[:, 2:3])
            E.v("tensor_max", w6[:, 6:7], w6[:, 6:7], w6[:, 3:4])
            E.v("tensor_max", w6[:, 6:7], w6[:, 6:7], w6[:, 4:5])
            E.v("tensor_max", w6[:, 6:7], w6[:, 6:7], w6[:, 5:6])
            E.v("tensor_scalar", gs[:], gs[:], w6[:, 6:7], None, ALU.is_ge)
            E.v("tensor_copy", bass.AP(smask, 0, [[16, 128], [4, 4], [1, 4]]),
                bass.AP(gs, 0, [[4, 128], [1, 4], [0, 4]]))
            E.v("tensor_mul", maskedp[:], sc_t[:], smask[:])
            E.v("tensor_sub", maskedp[:], maskedp[:], epsB[:])
            E.v("tensor_copy", work[:], maskedp[:])
            for k in range(4):
                E.v("reduce_max", scr[:, 4:5], work[:], AX.X)
                if k < 3:
                    E.v("tensor_scalar", sel4[:], work[:], scr[:, 4:5], None, ALU.is_equal)
                    E.op("vector", nc.vector.scalar_tensor_tensor, work[:], sel4[:], -1e30,
                         work[:], ALU.mult, ALU.add)
            E.v("tensor_scalar", sel4[:], maskedp[:], scr[:, 4:5], None, ALU.is_ge)
            E.v("tensor_mul", sel4[:], sel4[:], ssig[:])
            E.v("reduce_sum", scr[:, 5:6], sel4[:], AX.X)
            E.v("tensor_scalar_add", scr[:, 5:6], scr[:, 5:6], 1e-20)
            E.v("reciprocal", scr[:, 6:7], scr[:, 5:6])
            E.v("tensor_scalar", sel4[:], sel4[:], scr[:, 6:7], 2.5, ALU.mult, op1=ALU.mult)
            E.t("transpose", psF[0:16, 0:128], sel4[:], identf[:])
            E.v("tensor_copy", combT[:, i * 128:(i + 1) * 128], psF[0:16, 0:128])

        def stage_load(E, stage, wbf, src_re, csize, ccount):
            per = 2048 // csize
            for c0 in range(0, ccount, per):
                n = min(per, ccount - c0)
                dst = bass.AP(stage, 0, [[2048, 128], [csize, n], [1, csize]])
                E.dma(dst, src_re[:, c0:c0 + n, :])
                E.s("copy", wbf[:, c0 * csize:(c0 + n) * csize], stage[:, 0:n * csize])

        with nc.sbuf_tensor("stage", [128, 2048], F32) as stage, \
             nc.sbuf_tensor("gw_bf", [128, 8 * D], BF16) as gw_bf, \
             nc.sbuf_tensor("uw_bf", [128, 8 * D], BF16) as uw_bf, \
             nc.sbuf_tensor("dw_bf", [128, 8 * D], BF16) as dw_bf, \
             nc.sbuf_tensor("gsil", [128, 8 * 512], BF16) as gsil, \
             nc.sbuf_tensor("h2", [128, 8 * 512], BF16) as h2:
            for e in range(16):
                stage_load(E, stage, gw_bf,
                           gwT_in[e].rearrange("(c p) i -> p c i", p=128), 512, 8)
                stage_load(E, stage, uw_bf,
                           uwT_in[e].rearrange("(c p) i -> p c i", p=128), 512, 8)
                stage_load(E, stage, dw_bf,
                           dwT_in[e].rearrange("(c p) h -> p c h", p=128), D, 4)
                for lg in range(T // 512):
                    tsl = slice(lg * 512, (lg + 1) * 512)
                    for ic in range(4):
                        for c in range(8):
                            E.mm(ps[:, 0:512],
                                 bass.AP(gw_bf, c * 512 + ic * 128, [[8 * D, 128], [1, 128]]),
                                 bass.AP(xnT, c * T + lg * 512, [[8 * T, 128], [1, 512]]),
                                 c == 0, c == 7)
                        E.s("activation", gsil[:, ic * 512:(ic + 1) * 512], ps[:, 0:512], AF.Silu, 0.0, 1.0)
                        for c in range(8):
                            E.mm(ps2[:, 0:512],
                                 bass.AP(uw_bf, c * 512 + ic * 128, [[8 * D, 128], [1, 128]]),
                                 bass.AP(xnT, c * T + lg * 512, [[8 * T, 128], [1, 512]]),
                                 c == 0, c == 7)
                        E.v("tensor_tensor", h2[:, ic * 512:(ic + 1) * 512],
                            gsil[:, ic * 512:(ic + 1) * 512], ps2[:, 0:512], ALU.mult)
                    E.mm(ps2[:, 0:512], bass.AP(sel16, e * 128, [[16 * 128, 16], [1, 128]]),
                         combT[:, tsl], True, True)
                    for ic in range(4):
                        E.v("tensor_tensor", h2[:, ic * 512:(ic + 1) * 512],
                            h2[:, ic * 512:(ic + 1) * 512], ps2[:, 0:512], ALU.mult)
                    for d2c in range(8):
                        for ic in range(4):
                            E.mm(psr[:, 0:512],
                                 bass.AP(dw_bf, ic * D + d2c * 128, [[8 * D, 128], [1, 128]]),
                                 h2[:, ic * 512:(ic + 1) * 512], ic == 0, ic == 3)
                        dst = routedT[:, d2c * T + lg * 512: d2c * T + (lg + 1) * 512]
                        if e == 0:
                            E.v("tensor_copy", dst, psr[:, 0:512])
                        else:
                            E.v("tensor_add", dst, dst, psr[:, 0:512])

            stage_load(E, stage, gw_bf, sgwT_in.rearrange("(c p) i -> p c i", p=128), D, 8)
            stage_load(E, stage, uw_bf, suwT_in.rearrange("(c p) i -> p c i", p=128), D, 8)
            stage_load(E, stage, dw_bf, sdwT_in.rearrange("(c p) i -> p c i", p=128), D, 8)
            for lg in range(T // 512):
                for ic in range(8):
                    for c in range(8):
                        E.mm(ps[:, 0:512],
                             bass.AP(gw_bf, c * D + ic * 128, [[8 * D, 128], [1, 128]]),
                             bass.AP(xnT, c * T + lg * 512, [[8 * T, 128], [1, 512]]),
                             c == 0, c == 7)
                    E.s("activation", gsil[:, ic * 512:(ic + 1) * 512], ps[:, 0:512], AF.Silu, 0.0, 1.0)
                    for c in range(8):
                        E.mm(ps2[:, 0:512],
                             bass.AP(uw_bf, c * D + ic * 128, [[8 * D, 128], [1, 128]]),
                             bass.AP(xnT, c * T + lg * 512, [[8 * T, 128], [1, 512]]),
                             c == 0, c == 7)
                    E.v("tensor_tensor", h2[:, ic * 512:(ic + 1) * 512],
                        gsil[:, ic * 512:(ic + 1) * 512], ps2[:, 0:512], ALU.mult)
                for d2c in range(8):
                    for ic in range(8):
                        E.mm(psr[:, 0:512],
                             bass.AP(dw_bf, ic * D + d2c * 128, [[8 * D, 128], [1, 128]]),
                             h2[:, ic * 512:(ic + 1) * 512], ic == 0, ic == 7)
                    dst = routedT[:, d2c * T + lg * 512: d2c * T + (lg + 1) * 512]
                    E.v("tensor_add", dst, dst, psr[:, 0:512])

        for tt in range(ntile):
            for half in range(2):
                for j in range(4):
                    d2c = half * 4 + j
                    E.t("transpose", psF[:, j * 128:(j + 1) * 128],
                        routedT[:, d2c * T + tt * 128: d2c * T + (tt + 1) * 128], identf[:],
                        wait=(j == 0))
                E.v("tensor_copy", outt[:, half * 512:(half + 1) * 512], psF[:, 0:512])
            E.dma(xt[:], x_in[tt * 128:(tt + 1) * 128, :])
            E.v("tensor_add", outt[:], outt[:], xt[:])
            E.dma(r_out[tt * 128:(tt + 1) * 128, :], outt[:])

        blk = es.enter_context(nc.Block())
        E.replay(blk)
    return nc


def moe_inmaps(x_full, IN):
    import ml_dtypes
    sel16 = np.zeros((16, 16, 128), np.float32)
    for j in range(16):
        sel16[j, j, :] = 1.0
    epsB = np.tile((np.arange(16, dtype=np.float32) * 1e-6)[None, :], (128, 1))
    identb = np.eye(128, dtype=np.float32).astype(ml_dtypes.bfloat16)
    identf = np.eye(128, dtype=np.float32)
    base = {
        "rwT": np.ascontiguousarray(IN["L0_ffn_router_weight"].T),
        "eb": IN["L0_ffn_e_bias"].reshape(1, 16),
        "gwT": np.ascontiguousarray(IN["L0_ffn_gate_w"].transpose(0, 2, 1)),
        "uwT": np.ascontiguousarray(IN["L0_ffn_up_w"].transpose(0, 2, 1)),
        "dwT": np.ascontiguousarray(IN["L0_ffn_down_w"].transpose(0, 2, 1)),
        "sgwT": np.ascontiguousarray(IN["L0_ffn_sh_gate_w"].T),
        "suwT": np.ascontiguousarray(IN["L0_ffn_sh_up_w"].T),
        "sdwT": np.ascontiguousarray(IN["L0_ffn_sh_down_w"].T),
        "epsB": epsB,
        "sel16": sel16,
        "ones1": np.ones((1, 128), np.float32),
        "identb": identb,
        "identf": identf,
    }
    maps = []
    for c in range(NCORE):
        m = dict(base)
        m["x"] = np.ascontiguousarray(x_full[c * TC:(c + 1) * TC])
        maps.append(m)
    return maps


_PMOE = None


def run_moe(x_full, IN):
    global _PMOE
    if _PMOE is None:
        _PMOE = build_moe()
    res = run_bass_kernel_spmd(_PMOE, moe_inmaps(x_full, IN), core_ids=list(range(NCORE)))
    return np.concatenate([r["r"] for r in res.results], axis=0)


def build_p6():
    nc = bass.Bass("TRN2", target_bir_lowering=False)
    L = 4096
    QH = 2048
    x_in = nc.dram_tensor("x", [L, D], F32, kind="ExternalInput").ap()
    wqkv_in = nc.dram_tensor("wqkv", [D, 3 * D], F32, kind="ExternalInput").ap()
    bq128_in = nc.dram_tensor("bq128", [128, 24], F32, kind="ExternalInput").ap()
    bvrow_in = nc.dram_tensor("bvrow", [1, D], F32, kind="ExternalInput").ap()
    wo_in = nc.dram_tensor("wo", [D, D], F32, kind="ExternalInput").ap()
    bout_in = nc.dram_tensor("bout", [1, D], F32, kind="ExternalInput").ap()
    cos_in = nc.dram_tensor("cosb", [128, L], BF16, kind="ExternalInput").ap()
    sin_in = nc.dram_tensor("sinb", [128, L], BF16, kind="ExternalInput").ap()
    rot_in = nc.dram_tensor("rot", [128, 128], BF16, kind="ExternalInput").ap()
    identb_in = nc.dram_tensor("identb", [128, 128], BF16, kind="ExternalInput").ap()
    ones_in = nc.dram_tensor("ones1", [1, 128], F32, kind="ExternalInput").ap()
    r_out = nc.dram_tensor("r", [QH, D], F32, kind="ExternalOutput").ap()
    QOFF = 0

    with ExitStack() as es:
        sb = lambda n, shp, dt=F32: es.enter_context(nc.sbuf_tensor(n, shp, dt))
        S = mk_sems(nc, es)
        identb = sb("identb_s", [128, 128], BF16)
        rot = sb("rot_s", [128, 128], BF16)
        ones1 = sb("ones1_s", [1, 128])
        cosb = sb("cosb_s", [128, L], BF16)
        sinb = sb("sinb_s", [128, L], BF16)
        bq128 = sb("bq128_s", [128, 24])
        bvB = sb("bvB", [128, D])
        boutB = sb("boutB", [128, D])
        rows = sb("rows", [1, 2 * D])
        kcol = sb("kcol", [128, 4])
        scr = sb("scr", [128, 8])
        xt = sb("xt", [128, D])
        sq = sb("sq", [128, D])
        xnbf = sb("xnbf", [128, D], BF16)
        xnT = sb("xnT", [128, 8 * L], BF16)
        attnoutT = sb("attnoutT", [128, 8 * QH], BF16)
        wstage = sb("wstage", [128, 8 * 128])
        wq_bf = sb("wq_bf", [128, 8 * 128], BF16)
        wk_bf = sb("wk_bf", [128, 8 * 128], BF16)
        wv_bf = sb("wv_bf", [128, 8 * 128], BF16)
        wo_bf = sb("wo_bf", [128, 8 * D], BF16)
        vaug = sb("vaug", [128, 32 * 130], BF16)
        QT = sb("QT", [128, L], BF16)
        KT = sb("KT", [128, L], BF16)
        qf = sb("qf", [128, 512])
        t1 = sb("t1", [128, 512])
        t2 = sb("t2", [128, 512])
        qb = sb("qb", [128, 512], BF16)
        attnT = sb("attnT", [128, 512], BF16)
        osb = sb("osb", [64, 512])
        rrow = sb("rrow", [1, 512])
        outt = sb("outt", [128, D])
        ps = es.enter_context(nc.psum_tensor("ps", [128, 512], F32))
        ps2 = es.enter_context(nc.psum_tensor("ps2", [128, 512], F32))
        psO = es.enter_context(nc.psum_tensor("psO", [128, 512], F32))
        psT = es.enter_context(nc.psum_tensor("psT", [128, 1024], BF16))

        E = Emit(nc, S)
        E.v("memset", kcol[:, 0:1], 1e-5)
        E.v("memset", kcol[:, 1:2], -30.0)
        E.dma(identb[:], identb_in)
        E.dma(rot[:], rot_in)
        E.dma(ones1[:], ones_in)
        E.dma(cosb[:], cos_in)
        E.dma(sinb[:], sin_in)
        E.dma(bq128[:], bq128_in)
        E.dma(rows[:, 0:D], bvrow_in)
        E.dma(rows[:, D:2 * D], bout_in)
        bcast_row(E, nc, ps, bvB, rows[:, 0:D], ones1[:])
        bcast_row(E, nc, ps, boutB, rows[:, D:2 * D], ones1[:])
        for hp in range(8):
            E.dma(wstage[:, 0:D], wo_in[hp * 128:(hp + 1) * 128, :])
            E.s("copy", wo_bf[:, hp * D:(hp + 1) * D], wstage[:, 0:D])

        E.v("memset", bass.AP(vaug, 64, [[32 * 130, 128], [130, 32], [65, 2], [1, 1]]), 1.0)

        for i in range(L // 128):
            dst = bass.AP(xnT, i * 128, [[8 * L, 128], [L, 8], [1, 128]])
            emit_ln_block(E, nc, x_in, i * 128, xt, sq, scr, xnbf, psT, dst, identb, kcol[:, 0:1])

        QG = QH // 512
        for hp in range(8):
            for (wbf, sec) in ((wq_bf, 0), (wk_bf, 1), (wv_bf, 2)):
                col0 = sec * D + hp * 128
                E.dma(bass.AP(wstage, 0, [[1024, 128], [128, 8], [1, 128]]),
                      wqkv_in.rearrange("(c p) o -> p c o", p=128)[:, :, col0:col0 + 128])
                E.s("copy", wbf[:, 0:1024], wstage[:, 0:1024])
            for tt in range(L // 128):
                for c in range(8):
                    E.mm(ps[:, 0:128],
                         bass.AP(xnT, c * L + tt * 128, [[8 * L, 128], [1, 128]]),
                         wv_bf[:, c * 128:(c + 1) * 128], c == 0, c == 7)
                E.op("vector", nc.vector.tensor_tensor,
                     bass.AP(vaug, tt * 130, [[32 * 130, 128], [65, 2], [1, 64]]),
                     ps[:, 0:128], bvB[:, hp * 128:(hp + 1) * 128], ALU.add)
            for (wbf, QKT, bcol) in ((wq_bf, QT, hp), (wk_bf, KT, 8 + hp)):
                for lg in range(L // 512):
                    tsl = slice(lg * 512, (lg + 1) * 512)
                    for c in range(8):
                        E.mm(ps[:, 0:512], wbf[:, c * 128:(c + 1) * 128],
                             bass.AP(xnT, c * L + lg * 512, [[8 * L, 128], [1, 512]]),
                             c == 0, c == 7)
                    E.v("tensor_scalar", qf[:], ps[:, 0:512], bq128[:, bcol:bcol + 1], None, ALU.add)
                    E.s("copy", qb[:], qf[:])
                    E.mm(ps2[:, 0:512], rot[:], qb[:], True, True)
                    E.v("tensor_tensor", t1[:], qf[:], cosb[:, tsl], ALU.mult)
                    E.v("tensor_tensor", t2[:], ps2[:, 0:512], sinb[:, tsl], ALU.mult)
                    E.v("tensor_tensor", QKT[:, tsl], t1[:], t2[:], ALU.add)
            for h in range(2):
                hb = h * 64
                for qg in range(QG):
                    qsl = slice(QOFF + qg * 512, QOFF + (qg + 1) * 512)
                    for kc in range(32):
                        E.mm(ps[:, 0:512],
                             KT[hb:hb + 64, kc * 128:(kc + 1) * 128],
                             QT[hb:hb + 64, qsl], True, True)
                        E.s("activation", attnT[:], ps[:, 0:512], AF.Exp, kcol[:, 1:2], 0.125)
                        E.mm(psO[0:65, 0:512],
                             bass.AP(vaug, kc * 130 + h * 65, [[32 * 130, 128], [1, 65]]),
                             attnT[:], kc == 0, kc == 31, wait=True)
                    E.v("reciprocal", rrow[:], psO[64:65, 0:512])
                    E.mm(ps2[0:64, 0:512], ones1[0:1, 0:64], rrow[:], True, True)
                    E.v("tensor_copy", osb[:], psO[0:64, 0:512])
                    E.v("tensor_tensor",
                        attnoutT[hb:hb + 64, hp * QH + qg * 512:hp * QH + (qg + 1) * 512],
                        osb[:], ps2[0:64, 0:512], ALU.mult)
        for tt in range(QH // 128):
            for og in range(2):
                for hp in range(8):
                    E.mm(ps[:, 0:512],
                         attnoutT[:, hp * QH + tt * 128:hp * QH + (tt + 1) * 128],
                         wo_bf[:, hp * D + og * 512:hp * D + og * 512 + 512],
                         hp == 0, hp == 7)
                E.v("tensor_copy", outt[:, og * 512:(og + 1) * 512], ps[:, 0:512])
            E.v("tensor_add", outt[:], outt[:], boutB[:])
            E.dma(xt[:], x_in[QOFF + tt * 128:QOFF + (tt + 1) * 128, :])
            E.v("tensor_add", outt[:], outt[:], xt[:])
            E.dma(r_out[tt * 128:(tt + 1) * 128, :], outt[:])

        blk = es.enter_context(nc.Block())
        E.replay(blk)
    return nc


def p6_consts():
    import ml_dtypes
    Dh = 64
    inv_freq = 1.0 / 10000.0 ** (np.arange(0, Dh, 2, dtype=np.float32) / Dh)
    freqs = np.arange(4096, dtype=np.float32)[:, None] * inv_freq[None, :]
    emb = np.concatenate((freqs, freqs), axis=-1)
    cos = np.cos(emb).T
    sin = np.sin(emb).T
    cosb = np.tile(cos, (2, 1)).astype(ml_dtypes.bfloat16)
    sinb = np.tile(sin, (2, 1)).astype(ml_dtypes.bfloat16)
    rotm = np.zeros((128, 128), np.float32)
    for hh in range(2):
        for m in range(64):
            if m < 32:
                rotm[hh * 64 + m + 32, hh * 64 + m] = -1.0
            else:
                rotm[hh * 64 + m - 32, hh * 64 + m] = 1.0
    rot = rotm.astype(ml_dtypes.bfloat16)
    identb = np.eye(128, dtype=np.float32).astype(ml_dtypes.bfloat16)
    return cosb, sinb, rot, identb


def p6_inmaps(x_full, IN):
    cosb, sinb, rot, identb = p6_consts()
    wqkv = IN["L0_attn_b_W_qkv"]
    bqkv = IN["L0_attn_b_b_qkv"]
    base = {
        "wqkv": wqkv,
        "bq128": np.ascontiguousarray(bqkv[:2048].reshape(16, 128).T),
        "bvrow": bqkv[2048:].reshape(1, D),
        "wo": IN["L0_attn_b_W_out"],
        "bout": IN["L0_attn_b_b_out"].reshape(1, D),
        "rot": rot,
        "identb": identb,
        "ones1": np.ones((1, 128), np.float32),
    }
    bq = np.zeros((128, 24), np.float32)
    bq[:, 0:8] = bqkv[0:1024].reshape(8, 128).T
    bq[:, 8:16] = bqkv[1024:2048].reshape(8, 128).T
    base["bq128"] = bq
    maps = []
    for c in range(NCORE):
        b, qh = c // 2, c % 2
        xb = x_full.reshape(4, 4096, D)[b]
        m = dict(base)
        if qh == 1:
            m["x"] = np.ascontiguousarray(np.concatenate([xb[2048:], xb[:2048]], axis=0))
            m["cosb"] = np.ascontiguousarray(np.concatenate([cosb[:, 2048:], cosb[:, :2048]], axis=1))
            m["sinb"] = np.ascontiguousarray(np.concatenate([sinb[:, 2048:], sinb[:, :2048]], axis=1))
        else:
            m["x"] = np.ascontiguousarray(xb)
            m["cosb"] = cosb
            m["sinb"] = sinb
        maps.append(m)
    return maps


_P6 = None


def run_p6(x_full, IN):
    global _P6
    if _P6 is None:
        _P6 = build_p6()
    res = run_bass_kernel_spmd(_P6, p6_inmaps(x_full, IN), core_ids=list(range(NCORE)))
    out = np.zeros((4, 4096, D), np.float32)
    for c in range(NCORE):
        b, qh = c // 2, c % 2
        out[b, qh * 2048:(qh + 1) * 2048] = res.results[c]["r"]
    return out.reshape(-1, D)


def build_p7():
    nc = bass.Bass("TRN2", target_bir_lowering=False)
    L = 4096
    LH = 2048
    x_in = nc.dram_tensor("x", [L, D], F32, kind="ExternalInput").ap()
    wx_in = nc.dram_tensor("wx", [D, 96], F32, kind="ExternalInput").ap()
    wdt_in = nc.dram_tensor("wdt", [64, 512], F32, kind="ExternalInput").ap()
    dtb_in = nc.dram_tensor("dtb", [512], F32, kind="ExternalInput").ap()
    alog_in = nc.dram_tensor("alog", [512, 16], F32, kind="ExternalInput").ap()
    dpar_in = nc.dram_tensor("dpar", [512], F32, kind="ExternalInput").ap()
    sel_in = nc.dram_tensor("sel", [32, 32, 128], F32, kind="ExternalInput").ap()
    identb_in = nc.dram_tensor("identb", [128, 128], BF16, kind="ExternalInput").ap()
    yT_out = nc.dram_tensor("yT", [512, L], BF16, kind="ExternalOutput").ap()

    with ExitStack() as es:
        sb = lambda n, shp, dt=F32: es.enter_context(nc.sbuf_tensor(n, shp, dt))
        S = mk_sems(nc, es)
        identb = sb("identb_s", [128, 128], BF16)
        sel = sb("sel_s", [32, 32 * 128])
        wx_bf = sb("wx_bf", [128, 8 * 96], BF16)
        wdt_bf = sb("wdt_bf", [64, 512], BF16)
        cols = sb("cols", [128, 16])
        A_sb = sb("A_sb", [128, 64])
        numcar = sb("numcar", [128, 64])
        sumd = sb("sumd", [128, 12])
        kcol = sb("kcol", [128, 4])
        scr = sb("scr", [128, 8])
        deltaT = sb("deltaT", [128, 4 * L])
        xnhalfT = sb("xnhalfT", [128, 4 * L], BF16)
        BC32f = sb("BC32f", [32, L])
        ps = es.enter_context(nc.psum_tensor("ps", [128, 512], F32))
        psT = es.enter_context(nc.psum_tensor("psT", [128, 1024], BF16))

        E = Emit(nc, S)
        E.v("memset", kcol[:, 0:1], 1e-5)
        E.v("memset", kcol[:, 1:2], 1.0)
        E.v("memset", kcol[:, 2:3], 0.0)
        E.dma(identb[:], identb_in)
        E.dma(sel[:], sel_in.rearrange("k j p -> k (j p)"))
        E.dma(cols[:, 0:4], dtb_in.rearrange("(c p) -> p c", p=128), allow_slow_non_contiguous=True)
        E.dma(cols[:, 4:8], dpar_in.rearrange("(c p) -> p c", p=128), allow_slow_non_contiguous=True)
        E.dma(bass.AP(A_sb, 0, [[64, 128], [16, 4], [1, 16]]),
              alog_in.rearrange("(c p) n -> p c n", p=128))
        E.s("activation", A_sb[:], A_sb[:], AF.Exp, 0.0, 1.0)
        E.v("tensor_scalar_mul", A_sb[:], A_sb[:], -1.0)

        with nc.sbuf_tensor("xt", [128, D], F32) as xt, \
             nc.sbuf_tensor("sq", [128, D], F32) as sq, \
             nc.sbuf_tensor("xnbf", [128, D], BF16) as xnbf, \
             nc.sbuf_tensor("xnT", [128, 8 * LH], BF16) as xnT, \
             nc.sbuf_tensor("dbcT", [96, LH], F32) as dbcT, \
             nc.sbuf_tensor("dr_bf", [64, LH], BF16) as dr_bf, \
             nc.sbuf_tensor("wtmp", [128, 2 * 512], F32) as wtmp:
            E.dma(bass.AP(xt, 0, [[1024, 128], [96, 8], [1, 96]]),
                  wx_in.rearrange("(c p) r -> p c r", p=128))
            E.s("copy", wx_bf[:], xt[:, 0:8 * 96])
            E.dma(sq[0:64, 0:512], wdt_in)
            E.s("copy", wdt_bf[:], sq[0:64, 0:512])
            for lh in range(2):
                l0 = lh * LH
                for i in range(LH // 128):
                    dst = bass.AP(xnT, i * 128, [[8 * LH, 128], [LH, 8], [1, 128]])
                    emit_ln_block(E, nc, x_in, l0 + i * 128, xt, sq, scr, xnbf, psT, dst, identb,
                                  kcol[:, 0:1])
                    E.v("tensor_copy",
                        bass.AP(xnhalfT, l0 + i * 128, [[4 * L, 128], [L, 4], [1, 128]]),
                        psT[:, 0:512])
                for lg in range(LH // 512):
                    for c in range(8):
                        E.mm(ps[0:96, 0:512],
                             bass.AP(wx_bf, c * 96, [[8 * 96, 128], [1, 96]]),
                             bass.AP(xnT, c * LH + lg * 512, [[8 * LH, 128], [1, 512]]),
                             c == 0, c == 7)
                    E.v("tensor_copy", dbcT[:, lg * 512:(lg + 1) * 512], ps[0:96, 0:512])
                E.s("copy", dr_bf[:], dbcT[0:64, :])
                E.v("tensor_copy", BC32f[:, l0:l0 + LH], dbcT[64:96, :])
                for dc in range(4):
                    for lg in range(LH // 512):
                        E.mm(ps[:, 0:512], wdt_bf[:, dc * 128:(dc + 1) * 128],
                             dr_bf[:, lg * 512:(lg + 1) * 512], True, True)
                        dslice = deltaT[:, dc * L + l0 + lg * 512: dc * L + l0 + (lg + 1) * 512]
                        E.v("tensor_scalar", wtmp[:, 0:512], ps[:, 0:512], cols[:, dc:dc + 1], None, ALU.add)
                        E.s("activation", wtmp[:, 512:1024], wtmp[:, 0:512], AF.Abs, 0.0, 1.0)
                        E.s("activation", wtmp[:, 512:1024], wtmp[:, 512:1024], AF.Exp, 0.0, -1.0)
                        E.s("activation", wtmp[:, 512:1024], wtmp[:, 512:1024], AF.Ln, kcol[:, 1:2], 1.0)
                        E.v("tensor_scalar_max", wtmp[:, 0:512], wtmp[:, 0:512], 0.0)
                        E.v("tensor_add", dslice, wtmp[:, 0:512], wtmp[:, 512:1024])
                    E.v("reduce_sum", sumd[:, lh * 4 + dc:lh * 4 + dc + 1],
                        deltaT[:, dc * L + l0: dc * L + l0 + LH], AX.X)
            for dc in range(4):
                E.v("tensor_add", sumd[:, 8 + dc:9 + dc], sumd[:, dc:dc + 1],
                    sumd[:, 4 + dc:5 + dc])

        with nc.sbuf_tensor("BmB", [128, LH], BF16) as BmB, \
             nc.sbuf_tensor("CbB", [128, LH], BF16) as CbB, \
             nc.sbuf_tensor("w0", [128, LH], F32) as w0, \
             nc.sbuf_tensor("w1", [128, LH], F32) as w1, \
             nc.sbuf_tensor("w2", [128, LH], F32) as w2, \
             nc.sbuf_tensor("w3", [128, LH], F32) as w3, \
             nc.sbuf_tensor("yT_s", [128, 4 * LH], F32) as yT, \
             nc.sbuf_tensor("yTb", [128, LH], BF16) as yTb:
            zbc = bass.AP(kcol, 2, [[4, 128], [0, LH]])
            for lh in range(2):
                l0 = lh * LH
                for n in range(16):
                    for (dstB, j) in ((BmB, n), (CbB, 16 + n)):
                        for lg in range(LH // 512):
                            E.mm(ps[:, 0:512], bass.AP(sel, j * 128, [[32 * 128, 32], [1, 128]]),
                                 BC32f[:, l0 + lg * 512:l0 + (lg + 1) * 512], True, True)
                            E.v("tensor_copy", dstB[:, lg * 512:(lg + 1) * 512], ps[:, 0:512])
                    for dc in range(4):
                        dsl = deltaT[:, dc * L + l0: dc * L + l0 + LH]
                        acol = A_sb[:, dc * 16 + n:dc * 16 + n + 1]
                        E.v("tensor_scalar", w0[:], dsl, acol, None, ALU.mult)
                        if lh == 0:
                            E.v("tensor_tensor_scan", w1[:], w0[:], zbc, 0.0, ALU.add, ALU.add)
                        else:
                            E.v("tensor_scalar", scr[:, 4:5], sumd[:, dc:dc + 1], acol, None, ALU.mult)
                            E.v("tensor_tensor_scan", w1[:], w0[:], zbc, scr[:, 4:5], ALU.add, ALU.add)
                        E.v("tensor_scalar", scr[:, 5:6], sumd[:, 8 + dc:9 + dc], acol, None, ALU.mult)
                        E.s("activation", w2[:], w1[:], AF.Exp, scr[:, 5:6], -1.0)
                        xn_dc = bass.AP(xnhalfT, dc * L + l0, [[4 * L, 128], [1, LH]])
                        E.v("tensor_mul", w0[:], dsl, xn_dc)
                        E.v("tensor_mul", w0[:], w0[:], BmB[:])
                        E.v("tensor_mul", w0[:], w0[:], w2[:])
                        init = 0.0 if lh == 0 else numcar[:, n * 4 + dc:n * 4 + dc + 1]
                        E.v("tensor_tensor_scan", w3[:], w0[:], zbc, init, ALU.add, ALU.add)
                        E.v("tensor_copy", numcar[:, n * 4 + dc:n * 4 + dc + 1], w3[:, LH - 1:LH])
                        E.v("tensor_scalar_add", w2[:], w2[:], 1e-12)
                        E.v("reciprocal", w2[:], w2[:])
                        E.v("tensor_mul", w3[:], w3[:], w2[:])
                        E.v("tensor_mul", w3[:], w3[:], CbB[:])
                        if n == 0:
                            E.v("tensor_copy", yT[:, dc * LH:(dc + 1) * LH], w3[:])
                        else:
                            E.v("tensor_add", yT[:, dc * LH:(dc + 1) * LH],
                                yT[:, dc * LH:(dc + 1) * LH], w3[:])
                for dc in range(4):
                    xn_dc = bass.AP(xnhalfT, dc * L + l0, [[4 * L, 128], [1, LH]])
                    E.op("vector", nc.vector.scalar_tensor_tensor, yT[:, dc * LH:(dc + 1) * LH],
                         xn_dc, cols[:, 4 + dc:5 + dc], yT[:, dc * LH:(dc + 1) * LH], ALU.mult, ALU.add)
                    E.s("copy", yTb[:], yT[:, dc * LH:(dc + 1) * LH])
                    E.dma(yT_out[dc * 128:(dc + 1) * 128, l0:l0 + LH], yTb[:])

        blk = es.enter_context(nc.Block())
        E.replay(blk)
    return nc


def p7_inmaps(x_full, IN):
    import ml_dtypes
    sel = np.zeros((32, 32, 128), np.float32)
    for j in range(32):
        sel[j, j, :] = 1.0
    identb = np.eye(128, dtype=np.float32).astype(ml_dtypes.bfloat16)
    wx = IN["L1_attn_a_W_x"]
    wdt = IN["L1_attn_a_W_dt"]
    dtb = IN["L1_attn_a_b_dt"]
    alog = IN["L1_attn_a_A_log"]
    dpar = IN["L1_attn_a_D"]
    maps = []
    for c in range(NCORE):
        b, dh = c // 2, c % 2
        dsl = slice(dh * 512, (dh + 1) * 512)
        xb = x_full.reshape(4, 4096, D)[b]
        if dh == 1:
            xb = np.concatenate([xb[:, 512:], xb[:, :512]], axis=1)
            wx_r = np.concatenate([wx[512:], wx[:512]], axis=0)
        else:
            wx_r = wx
        maps.append({
            "x": np.ascontiguousarray(xb),
            "wx": np.ascontiguousarray(wx_r),
            "wdt": np.ascontiguousarray(wdt[:, dsl]),
            "dtb": np.ascontiguousarray(dtb[dsl]),
            "alog": np.ascontiguousarray(alog[dsl]),
            "dpar": np.ascontiguousarray(dpar[dsl]),
            "sel": sel,
            "identb": identb,
        })
    return maps


_P7 = None


def run_p7(x_full, IN):
    global _P7
    if _P7 is None:
        _P7 = build_p7()
    res = run_bass_kernel_spmd(_P7, p7_inmaps(x_full, IN), core_ids=list(range(NCORE)))
    import ml_dtypes
    yT = np.zeros((4, 1024, 4096), dtype=ml_dtypes.bfloat16)
    for c in range(NCORE):
        b, dh = c // 2, c % 2
        yT[b, dh * 512:(dh + 1) * 512] = res.results[c]["yT"]
    return yT


def build_p8():
    nc = bass.Bass("TRN2", target_bir_lowering=False)
    T = TC
    yT_in = nc.dram_tensor("yT", [D, T], BF16, kind="ExternalInput").ap()
    x_in = nc.dram_tensor("x", [T, D], F32, kind="ExternalInput").ap()
    wo_in = nc.dram_tensor("wo", [D, D], F32, kind="ExternalInput").ap()
    bout_in = nc.dram_tensor("bout", [1, D], F32, kind="ExternalInput").ap()
    ones_in = nc.dram_tensor("ones1", [1, 128], F32, kind="ExternalInput").ap()
    r_out = nc.dram_tensor("r", [T, D], F32, kind="ExternalOutput").ap()

    with ExitStack() as es:
        sb = lambda n, shp, dt=F32: es.enter_context(nc.sbuf_tensor(n, shp, dt))
        S = mk_sems(nc, es)
        ones1 = sb("ones1_s", [1, 128])
        rows = sb("rows", [1, D])
        boutB = sb("boutB", [128, D])
        wstage = sb("wstage", [128, D])
        wo_bf = sb("wo_bf", [128, 8 * D], BF16)
        ytiles = sb("ytiles", [128, 8 * 128], BF16)
        xt = sb("xt", [128, D])
        outt = sb("outt", [128, D])
        ps = es.enter_context(nc.psum_tensor("ps", [128, 512], F32))
        E = Emit(nc, S)
        E.dma(ones1[:], ones_in)
        E.dma(rows[:], bout_in)
        bcast_row(E, nc, ps, boutB, rows[:], ones1[:])
        for c in range(8):
            E.dma(wstage[:], wo_in[c * 128:(c + 1) * 128, :])
            E.s("copy", wo_bf[:, c * D:(c + 1) * D], wstage[:])
        for tt in range(T // 128):
            for c in range(8):
                E.dma(ytiles[:, c * 128:(c + 1) * 128],
                      yT_in[c * 128:(c + 1) * 128, tt * 128:(tt + 1) * 128])
            for og in range(2):
                for c in range(8):
                    E.mm(ps[:, 0:512], ytiles[:, c * 128:(c + 1) * 128],
                         wo_bf[:, c * D + og * 512:c * D + og * 512 + 512],
                         c == 0, c == 7)
                E.v("tensor_copy", outt[:, og * 512:(og + 1) * 512], ps[:, 0:512])
            E.v("tensor_add", outt[:], outt[:], boutB[:])
            E.dma(xt[:], x_in[tt * 128:(tt + 1) * 128, :])
            E.v("tensor_add", outt[:], outt[:], xt[:])
            E.dma(r_out[tt * 128:(tt + 1) * 128, :], outt[:])
        blk = es.enter_context(nc.Block())
        E.replay(blk)
    return nc


_P8 = None


def run_p8(yT_full, x_full, IN):
    global _P8
    if _P8 is None:
        _P8 = build_p8()
    maps = []
    for c in range(NCORE):
        b, half = c // 2, c % 2
        maps.append({
            "yT": np.ascontiguousarray(yT_full[b][:, half * 2048:(half + 1) * 2048]),
            "x": np.ascontiguousarray(x_full[c * TC:(c + 1) * TC]),
            "wo": IN["L1_attn_a_W_out"],
            "bout": IN["L1_attn_a_b_out"].reshape(1, D),
            "ones1": np.ones((1, 128), np.float32),
        })
    res = run_bass_kernel_spmd(_P8, maps, core_ids=list(range(NCORE)))
    return np.concatenate([r["r"] for r in res.results], axis=0)


def build_p10():
    nc = bass.Bass("TRN2", target_bir_lowering=False)
    L = 4096
    QH = 2048
    SW = 5504 + 512
    x_in = nc.dram_tensor("x", [L, D], F32, kind="ExternalInput").ap()
    xq_in = nc.dram_tensor("xq", [QH, D], F32, kind="ExternalInput").ap()
    wqp_in = nc.dram_tensor("wqp", [8, D, 128], F32, kind="ExternalInput").ap()
    wkp_in = nc.dram_tensor("wkp", [8, D, 128], F32, kind="ExternalInput").ap()
    wvp_in = nc.dram_tensor("wvp", [8, D, 128], F32, kind="ExternalInput").ap()
    bqk_in = nc.dram_tensor("bqk", [128, 16], F32, kind="ExternalInput").ap()
    bvrow_in = nc.dram_tensor("bvrow", [1, D], F32, kind="ExternalInput").ap()
    wo_in = nc.dram_tensor("wo", [D, D], F32, kind="ExternalInput").ap()
    bout_in = nc.dram_tensor("bout", [1, D], F32, kind="ExternalInput").ap()
    stair_in = nc.dram_tensor("stair", [128, SW], BF16, kind="ExternalInput").ap()
    identb_in = nc.dram_tensor("identb", [128, 128], BF16, kind="ExternalInput").ap()
    ones_in = nc.dram_tensor("ones1", [1, 128], F32, kind="ExternalInput").ap()
    r_out = nc.dram_tensor("r", [QH, D], F32, kind="ExternalOutput").ap()

    with ExitStack() as es:
        sb = lambda n, shp, dt=F32: es.enter_context(nc.sbuf_tensor(n, shp, dt))
        S = mk_sems(nc, es)
        identb = sb("identb_s", [128, 128], BF16)
        ones1 = sb("ones1_s", [1, 128])
        stair = sb("stair_s", [128, SW], BF16)
        bqk = sb("bqk_s", [128, 16])
        bvB = sb("bvB", [128, D])
        boutB = bvB
        rows = sb("rows", [1, D])
        kcol = sb("kcol", [128, 4])
        scr = sb("scr", [128, 8])
        xt = sb("xt", [128, D])
        sq = sb("sq", [128, D])
        xnbf = sb("xnbf", [128, D], BF16)
        xnT = sb("xnT", [128, 8 * L], BF16)
        xqnT = sb("xqnT", [128, 8 * QH], BF16)
        attnoutT = sb("attnoutT", [128, 8 * QH], BF16)
        wstage = sb("wstage", [128, 8 * 128])
        wq_bf = sb("wq_bf", [128, 8 * 128], BF16)
        wk_bf = sb("wk_bf", [128, 8 * 128], BF16)
        wv_bf = sb("wv_bf", [128, 8 * 128], BF16)
        wo_bf = sb("wo_bf", [128, 8 * D], BF16)
        vaug = sb("vaug", [128, 32 * 130], BF16)
        QT = sb("QT", [128, QH], BF16)
        KT = sb("KT", [128, L], BF16)
        msc = sb("msc", [128, 512])
        attnT = sb("attnT", [128, 512], BF16)
        outt = sq
        osb = msc[0:64, :]
        rrow = None
        ps = es.enter_context(nc.psum_tensor("ps", [128, 512], F32))
        ps2 = es.enter_context(nc.psum_tensor("ps2", [128, 512], F32))
        psO = es.enter_context(nc.psum_tensor("psO", [128, 512], F32))
        psT = es.enter_context(nc.psum_tensor("psT", [128, 1024], BF16))
        rrow = xt[0:1, 0:512]

        E = Emit(nc, S)
        E.v("memset", kcol[:, 0:1], 1e-5)
        E.v("memset", kcol[:, 1:2], -30.0)
        E.dma(identb[:], identb_in)
        E.dma(ones1[:], ones_in)
        E.dma(stair[:], stair_in)
        E.dma(bqk[:], bqk_in)
        E.dma(rows[:, 0:D], bvrow_in)
        bcast_row(E, nc, ps, bvB, rows[:, 0:D], ones1[:])
        for hp in range(8):
            E.dma(wstage[:, 0:D], wo_in[hp * 128:(hp + 1) * 128, :])
            E.s("copy", wo_bf[:, hp * D:(hp + 1) * D], wstage[:, 0:D])
        E.v("memset", bass.AP(vaug, 64, [[32 * 130, 128], [130, 32], [65, 2], [1, 1]]), 1.0)

        for i in range(L // 128):
            dst = bass.AP(xnT, i * 128, [[8 * L, 128], [L, 8], [1, 128]])
            emit_ln_block(E, nc, x_in, i * 128, xt, sq, scr, xnbf, psT, dst, identb, kcol[:, 0:1])
        for i in range(QH // 128):
            dst = bass.AP(xqnT, i * 128, [[8 * QH, 128], [QH, 8], [1, 128]])
            emit_ln_block(E, nc, xq_in, i * 128, xt, sq, scr, xnbf, psT, dst, identb, kcol[:, 0:1])

        QG = QH // 512
        for hp in range(8):
            for (wbf, src) in ((wq_bf, wqp_in), (wk_bf, wkp_in), (wv_bf, wvp_in)):
                E.dma(bass.AP(wstage, 0, [[1024, 128], [128, 8], [1, 128]]),
                      src[hp].rearrange("(c p) o -> p c o", p=128))
                E.s("copy", wbf[:, 0:1024], wstage[:, 0:1024])
            for tt in range(L // 128):
                for c in range(8):
                    E.mm(ps[:, 0:128],
                         bass.AP(xnT, c * L + tt * 128, [[8 * L, 128], [1, 128]]),
                         wv_bf[:, c * 128:(c + 1) * 128], c == 0, c == 7)
                E.op("vector", nc.vector.tensor_tensor,
                     bass.AP(vaug, tt * 130, [[32 * 130, 128], [65, 2], [1, 64]]),
                     ps[:, 0:128], bvB[:, hp * 128:(hp + 1) * 128], ALU.add)
            for lg in range(L // 512):
                for c in range(8):
                    E.mm(ps[:, 0:512], wk_bf[:, c * 128:(c + 1) * 128],
                         bass.AP(xnT, c * L + lg * 512, [[8 * L, 128], [1, 512]]),
                         c == 0, c == 7)
                E.v("tensor_scalar", KT[:, lg * 512:(lg + 1) * 512], ps[:, 0:512],
                    bqk[:, 8 + hp:9 + hp], None, ALU.add)
            for lg in range(QH // 512):
                for c in range(8):
                    E.mm(ps[:, 0:512], wq_bf[:, c * 128:(c + 1) * 128],
                         bass.AP(xqnT, c * QH + lg * 512, [[8 * QH, 128], [1, 512]]),
                         c == 0, c == 7)
                E.v("tensor_scalar", QT[:, lg * 512:(lg + 1) * 512], ps[:, 0:512],
                    bqk[:, hp:hp + 1], None, ALU.add)
            for h in range(2):
                hb = h * 64
                for qg in range(QG):
                    qsl = slice(qg * 512, (qg + 1) * 512)
                    for kc in range(32):
                        E.mm(ps[:, 0:512],
                             KT[hb:hb + 64, kc * 128:(kc + 1) * 128],
                             QT[hb:hb + 64, qsl], True, True)
                        soff = qg * 512 - kc * 128 + 3968
                        E.v("tensor_tensor", msc[:], ps[:, 0:512],
                            stair[:, soff:soff + 512], ALU.add)
                        E.s("activation", attnT[:], msc[:], AF.Exp, kcol[:, 1:2], 0.125)
                        E.mm(psO[0:65, 0:512],
                             bass.AP(vaug, kc * 130 + h * 65, [[32 * 130, 128], [1, 65]]),
                             attnT[:], kc == 0, kc == 31, wait=True)
                    E.v("reciprocal", rrow[:], psO[64:65, 0:512])
                    E.mm(ps2[0:64, 0:512], ones1[0:1, 0:64], rrow[:], True, True)
                    E.v("tensor_copy", osb[:], psO[0:64, 0:512])
                    E.v("tensor_tensor",
                        attnoutT[hb:hb + 64, hp * QH + qg * 512:hp * QH + (qg + 1) * 512],
                        osb[:], ps2[0:64, 0:512], ALU.mult)
        E.dma(rows[:, 0:D], bout_in)
        bcast_row(E, nc, ps, boutB, rows[:, 0:D], ones1[:])
        for tt in range(QH // 128):
            for og in range(2):
                for hp in range(8):
                    E.mm(ps[:, 0:512],
                         attnoutT[:, hp * QH + tt * 128:hp * QH + (tt + 1) * 128],
                         wo_bf[:, hp * D + og * 512:hp * D + og * 512 + 512],
                         hp == 0, hp == 7)
                E.v("tensor_copy", outt[:, og * 512:(og + 1) * 512], ps[:, 0:512])
            E.v("tensor_add", outt[:], outt[:], boutB[:])
            E.dma(xt[:], xq_in[tt * 128:(tt + 1) * 128, :])
            E.v("tensor_add", outt[:], outt[:], xt[:])
            E.dma(r_out[tt * 128:(tt + 1) * 128, :], outt[:])

        blk = es.enter_context(nc.Block())
        E.replay(blk)
    return nc


def p10_inmaps(x_full, IN):
    import ml_dtypes
    identb = np.eye(128, dtype=np.float32).astype(ml_dtypes.bfloat16)
    Wq = IN["L1_attn_b_Wq"]
    Wk = IN["L1_attn_b_Wk"]
    Wv = IN["L1_attn_b_Wv"]
    pack = lambda W: np.ascontiguousarray(
        W.reshape(8, 2, D, 64).transpose(0, 2, 1, 3).reshape(8, D, 128))
    bqk = np.zeros((128, 16), np.float32)
    bqk[:, 0:8] = IN["L1_attn_b_bq"].reshape(8, 128).T
    bqk[:, 8:16] = IN["L1_attn_b_bk"].reshape(8, 128).T
    base = {
        "wqp": pack(Wq), "wkp": pack(Wk), "wvp": pack(Wv),
        "bqk": bqk,
        "bvrow": IN["L1_attn_b_bv"].reshape(1, D),
        "wo": IN["L1_attn_b_Wo"],
        "bout": IN["L1_attn_b_bo"].reshape(1, D),
        "identb": identb,
        "ones1": np.ones((1, 128), np.float32),
    }
    SW = 5504 + 512
    maps = []
    for c in range(NCORE):
        b, qh = c // 2, c % 2
        q0 = qh * 2048
        xb = x_full.reshape(4, 4096, D)[b]
        r = np.arange(128)[:, None]
        u = np.arange(SW)[None, :]
        stair = np.where(r <= (u - 3968 + q0), 0.0, -1e30).astype(np.float32)
        m = dict(base)
        m["x"] = np.ascontiguousarray(xb)
        m["xq"] = np.ascontiguousarray(xb[q0:q0 + 2048])
        m["stair"] = stair.astype(ml_dtypes.bfloat16)
        maps.append(m)
    return maps


_P10 = None


def run_p10(x_full, IN):
    global _P10
    if _P10 is None:
        _P10 = build_p10()
    res = run_bass_kernel_spmd(_P10, p10_inmaps(x_full, IN), core_ids=list(range(NCORE)))
    out = np.zeros((4, 4096, D), np.float32)
    for c in range(NCORE):
        b, qh = c // 2, c % 2
        out[b, qh * 2048:(qh + 1) * 2048] = res.results[c]["r"]
    return out.reshape(-1, D)


def build_p9a():
    nc = bass.Bass("TRN2", target_bir_lowering=False)
    WN = 2304
    OWN = 2048
    xw_in = nc.dram_tensor("xw", [WN + 1, D], F32, kind="ExternalInput").ap()
    wqkv_in = nc.dram_tensor("wqkv", [D, 3 * 1024], F32, kind="ExternalInput").ap()
    wo_in = nc.dram_tensor("wo", [1024, D], F32, kind="ExternalInput").ap()
    qs_in = nc.dram_tensor("qs", [128, 2], F32, kind="ExternalInput").ap()
    cosS_in = nc.dram_tensor("cosS", [128, WN], BF16, kind="ExternalInput").ap()
    sinS_in = nc.dram_tensor("sinS", [128, WN], BF16, kind="ExternalInput").ap()
    cosI_in = nc.dram_tensor("cosI", [128, WN], BF16, kind="ExternalInput").ap()
    sinI_in = nc.dram_tensor("sinI", [128, WN], BF16, kind="ExternalInput").ap()
    lmask_in = nc.dram_tensor("lmask", [4, 128, 512], F32, kind="ExternalInput").ap()
    rot_in = nc.dram_tensor("rot", [128, 128], BF16, kind="ExternalInput").ap()
    b2_in = nc.dram_tensor("b2sel", [128, 2], F32, kind="ExternalInput").ap()
    b2t_in = nc.dram_tensor("b2t", [2, 128], F32, kind="ExternalInput").ap()
    identb_in = nc.dram_tensor("identb", [128, 128], BF16, kind="ExternalInput").ap()
    r_out = nc.dram_tensor("r", [OWN, D], F32, kind="ExternalOutput").ap()

    with ExitStack() as es:
        sb = lambda n, shp, dt=F32: es.enter_context(nc.sbuf_tensor(n, shp, dt))
        S = mk_sems(nc, es)
        identb = sb("identb_s", [128, 128], BF16)
        rot = sb("rot_s", [128, 128], BF16)
        b2 = sb("b2_s", [128, 2])
        b2t = sb("b2t_s", [2, 128])
        qs = sb("qs_s", [128, 2])
        cosS = sb("cosS_s", [128, WN], BF16)
        sinS = sb("sinS_s", [128, WN], BF16)
        cosI = sb("cosI_s", [128, WN], BF16)
        sinI = sb("sinI_s", [128, WN], BF16)
        lmask = sb("lmask_s", [128, 4 * 512])
        scr = sb("scr", [128, 8])
        xt = sb("xt", [128, D])
        shbf = sb("shbf", [128, D], BF16)
        shxT = sb("shxT", [128, 8 * WN], BF16)
        V = sb("V", [128, 18 * 1024], BF16)
        attnoutT = sb("attnoutT", [128, 8 * OWN], BF16)
        wstage = sb("wstage", [128, 1024])
        wq_bf = sb("wq_bf", [128, 8 * 128], BF16)
        wk_bf = sb("wk_bf", [128, 8 * 128], BF16)
        wv_bf = sb("wv_bf", [128, 8 * 128], BF16)
        wo_bf = sb("wo_bf", [128, 8 * D], BF16)
        QT = sb("QT", [128, WN], BF16)
        KT = sb("KT", [128, WN], BF16)
        sqv = sb("sqv", [128, 512])
        qsb = sb("qsb", [128, 512])
        t1 = sb("t1", [128, 512])
        nsb = sb("nsb", [2, 512])
        msc = sb("msc", [128, 512])
        attn = sb("attn", [128, 512], BF16)
        attnT = sb("attnT", [128, 512], BF16)
        ps = es.enter_context(nc.psum_tensor("ps", [128, 512], F32))
        ps2 = es.enter_context(nc.psum_tensor("ps2", [128, 512], F32))
        psO = es.enter_context(nc.psum_tensor("psO", [128, 512], F32))
        psT = es.enter_context(nc.psum_tensor("psT", [128, 1024], BF16))

        E = Emit(nc, S)
        E.dma(identb[:], identb_in)
        E.dma(rot[:], rot_in)
        E.dma(b2[:], b2_in)
        E.dma(b2t[:], b2t_in)
        E.dma(qs[:], qs_in)
        E.dma(cosS[:], cosS_in)
        E.dma(sinS[:], sinS_in)
        E.dma(cosI[:], cosI_in)
        E.dma(sinI[:], sinI_in)
        E.dma(bass.AP(lmask, 0, [[4 * 512, 128], [512, 4], [1, 512]]),
              lmask_in.rearrange("m p j -> p m j"))
        for hp in range(8):
            E.dma(wstage[:, 0:D], wo_in[hp * 128:(hp + 1) * 128, :])
            E.s("copy", wo_bf[:, hp * D:(hp + 1) * D], wstage[:, 0:D])

        for i in range(WN // 128):
            E.dma(xt[:, 0:512], xw_in[1 + i * 128:1 + (i + 1) * 128, 0:512])
            E.dma(xt[:, 512:1024], xw_in[i * 128:(i + 1) * 128, 512:1024])
            E.s("copy", shbf[:], xt[:])
            for dc in range(8):
                E.t("transpose", psT[:, dc * 128:(dc + 1) * 128], shbf[:, dc * 128:(dc + 1) * 128],
                    identb[:], wait=(dc == 0))
            E.v("tensor_copy", bass.AP(shxT, i * 128, [[8 * WN, 128], [WN, 8], [1, 128]]), psT[:])

        for hp in range(8):
            for (wbf, sec) in ((wq_bf, 0), (wk_bf, 1), (wv_bf, 2)):
                col0 = sec * 1024 + hp * 128
                E.dma(bass.AP(wstage, 0, [[1024, 128], [128, 8], [1, 128]]),
                      wqkv_in.rearrange("(c p) o -> p c o", p=128)[:, :, col0:col0 + 128])
                E.s("copy", wbf[:, 0:1024], wstage[:, 0:1024])
            for tt in range(WN // 128):
                for c in range(8):
                    E.mm(ps[:, 0:128],
                         bass.AP(shxT, c * WN + tt * 128, [[8 * WN, 128], [1, 128]]),
                         wv_bf[:, c * 128:(c + 1) * 128], c == 0, c == 7)
                E.v("tensor_copy",
                    bass.AP(V, tt * 1024 + hp * 128, [[18 * 1024, 128], [1, 128]]),
                    ps[:, 0:128])
            for (wbf, QKT, scol, cosX, sinX) in ((wq_bf, QT, 0, cosS, sinS),
                                                 (wk_bf, KT, 1, cosI, sinI)):
                for lg in range((WN + 511) // 512):
                    j0 = lg * 512
                    w = min(512, WN - j0)
                    for c in range(8):
                        E.mm(ps[:, 0:w], wbf[:, c * 128:(c + 1) * 128],
                             bass.AP(shxT, c * WN + j0, [[8 * WN, 128], [1, w]]),
                             c == 0, c == 7)
                    E.s("activation", sqv[:, 0:w], ps[:, 0:w], AF.Square, 0.0, 1.0)
                    E.mm(ps2[0:2, 0:w], b2[:], sqv[:, 0:w], True, True)
                    E.s("activation", nsb[:, 0:w], ps2[0:2, 0:w], AF.Sqrt, 0.0, 1.0)
                    E.v("tensor_scalar_max", nsb[:, 0:w], nsb[:, 0:w], 1e-12)
                    E.v("reciprocal", nsb[:, 0:w], nsb[:, 0:w])
                    E.mm(ps2[:, 0:w], b2t[:], nsb[:, 0:w], True, True)
                    E.v("tensor_copy", qsb[:, 0:w], ps[:, 0:w])
                    E.v("tensor_mul", qsb[:, 0:w], qsb[:, 0:w], ps2[:, 0:w])
                    E.v("tensor_scalar", qsb[:, 0:w], qsb[:, 0:w], qs[:, scol:scol + 1], None, ALU.mult)
                    E.s("copy", attn[:, 0:w], qsb[:, 0:w])
                    E.mm(ps2[:, 0:w], rot[:], attn[:, 0:w], True, True)
                    E.v("tensor_tensor", t1[:, 0:w], qsb[:, 0:w], cosX[:, j0:j0 + w], ALU.mult)
                    E.v("tensor_tensor", qsb[:, 0:w], ps2[:, 0:w], sinX[:, j0:j0 + w], ALU.mult)
                    E.v("tensor_tensor", QKT[:, j0:j0 + w], t1[:, 0:w], qsb[:, 0:w], ALU.add)
            for h in range(2):
                hb = h * 64
                for wdw in range(8):
                    jb0 = wdw * 256
                    qb0 = 256 + wdw * 256
                    for qt in range(2):
                        q0w = qb0 + qt * 128
                        E.mm(ps[:, 0:512],
                             QT[hb:hb + 64, q0w:q0w + 128],
                             KT[hb:hb + 64, jb0:jb0 + 512], True, True)
                        mi = (0 if (wdw == 0) else 2) + qt
                        E.v("tensor_tensor", msc[:], ps[:, 0:512],
                            lmask[:, mi * 512:(mi + 1) * 512], ALU.add)
                        E.v("reduce_max", scr[:, 0:1], msc[:], AX.X)
                        E.v("tensor_scalar_mul", scr[:, 1:2], scr[:, 0:1], -8.0)
                        E.s("activation", msc[:], msc[:], AF.Exp, scr[:, 1:2], 8.0,
                            accum_out=scr[:, 2:3])
                        E.v("reciprocal", scr[:, 3:4], scr[:, 2:3])
                        E.v("tensor_scalar", attn[:], msc[:], scr[:, 3:4], None, ALU.mult)
                        for jc in range(4):
                            E.t("transpose", psT[:, jc * 128:(jc + 1) * 128],
                                attn[:, jc * 128:(jc + 1) * 128], identb[:], wait=(jc == 0))
                        E.v("tensor_copy", attnT[:], psT[:, 0:512])
                        for jc in range(4):
                            vtt = 2 * wdw + jc
                            E.mm(psO[0:64, 0:128],
                                 bass.AP(V, vtt * 1024 + hp * 128 + hb, [[18 * 1024, 128], [1, 64]]),
                                 attnT[:, jc * 128:(jc + 1) * 128], jc == 0, jc == 3)
                        E.v("tensor_copy",
                            attnoutT[hb:hb + 64,
                                     hp * OWN + wdw * 256 + qt * 128: hp * OWN + wdw * 256 + (qt + 1) * 128],
                            psO[0:64, 0:128])
        for tt in range(OWN // 128):
            for og in range(2):
                for hp in range(8):
                    E.mm(ps[:, 0:512],
                         attnoutT[:, hp * OWN + tt * 128:hp * OWN + (tt + 1) * 128],
                         wo_bf[:, hp * D + og * 512:hp * D + og * 512 + 512],
                         hp == 0, hp == 7)
                E.v("tensor_copy", xt[:, og * 512:(og + 1) * 512], ps[:, 0:512])
            E.dma(t1[:, 0:512], xw_in[257 + tt * 128:257 + (tt + 1) * 128, 0:512])
            E.dma(qsb[:, 0:512], xw_in[257 + tt * 128:257 + (tt + 1) * 128, 512:1024])
            E.v("tensor_add", xt[:, 0:512], xt[:, 0:512], t1[:, 0:512])
            E.v("tensor_add", xt[:, 512:1024], xt[:, 512:1024], qsb[:, 0:512])
            E.dma(r_out[tt * 128:(tt + 1) * 128, :], xt[:])

        blk = es.enter_context(nc.Block())
        E.replay(blk)
    return nc


def build_p9b():
    nc = bass.Bass("TRN2", target_bir_lowering=False)
    T = 2048
    OC = 22
    xw_in = nc.dram_tensor("xw", [T + 1, D], F32, kind="ExternalInput").ap()
    w1p_in = nc.dram_tensor("w1p", [D, 2 * 2816], F32, kind="ExternalInput").ap()
    b1p_in = nc.dram_tensor("b1p", [128, 2 * OC], F32, kind="ExternalInput").ap()
    w2p_in = nc.dram_tensor("w2p", [2816, D], F32, kind="ExternalInput").ap()
    b2row_in = nc.dram_tensor("b2row", [1, D], F32, kind="ExternalInput").ap()
    g_in = nc.dram_tensor("lng", [1, D], F32, kind="ExternalInput").ap()
    bln_in = nc.dram_tensor("lnb", [1, D], F32, kind="ExternalInput").ap()
    identb_in = nc.dram_tensor("identb", [128, 128], BF16, kind="ExternalInput").ap()
    identf_in = nc.dram_tensor("identf", [128, 128], F32, kind="ExternalInput").ap()
    ones_in = nc.dram_tensor("ones1", [1, 128], F32, kind="ExternalInput").ap()
    r_out = nc.dram_tensor("r", [T, D], F32, kind="ExternalOutput").ap()

    with ExitStack() as es:
        sb = lambda n, shp, dt=F32: es.enter_context(nc.sbuf_tensor(n, shp, dt))
        S = mk_sems(nc, es)
        identb = sb("identb_s", [128, 128], BF16)
        identf = sb("identf_s", [128, 128], F32)
        ones1 = sb("ones1_s", [1, 128])
        b1p = sb("b1p_s", [128, 2 * OC])
        kcol = sb("kcol", [128, 4])
        scr = sb("scr", [128, 8])
        rows = sb("rows", [1, D])
        gB = sb("gB", [128, D])
        blnB = sb("blnB", [128, D])
        b2B = sb("b2B", [128, D])
        xt = sb("xt", [128, D])
        sq = sb("sq", [128, D])
        xnbf = sb("xnbf", [128, D], BF16)
        xnT = sb("xnT", [128, 8 * T], BF16)
        outacc = sb("outacc", [128, 8 * T])
        w1a = sb("w1a", [128, 8 * 128], BF16)
        w1g = sb("w1g", [128, 8 * 128], BF16)
        w2c = sb("w2c", [128, D], BF16)
        wstage = sb("wstage", [128, 1024])
        asb = sb("asb", [128, 512])
        ggs = sb("ggs", [128, 512])
        h2T = sb("h2T", [128, 4 * 512], BF16)
        ps = es.enter_context(nc.psum_tensor("ps", [128, 512], F32))
        ps2 = es.enter_context(nc.psum_tensor("ps2", [128, 512], F32))
        psr = es.enter_context(nc.psum_tensor("psr", [128, 512], F32))
        psT = es.enter_context(nc.psum_tensor("psT", [128, 1024], BF16))
        psF = es.enter_context(nc.psum_tensor("psF", [128, 512], F32))

        E = Emit(nc, S)
        E.v("memset", kcol[:, 0:1], 1e-5)
        E.dma(identb[:], identb_in)
        E.dma(identf[:], identf_in)
        E.dma(ones1[:], ones_in)
        E.dma(b1p[:], b1p_in)
        E.dma(rows[:], g_in)
        bcast_row(E, nc, ps, gB, rows[:], ones1[:])
        E.dma(rows[:], bln_in)
        bcast_row(E, nc, ps, blnB, rows[:], ones1[:])
        E.dma(rows[:], b2row_in)
        bcast_row(E, nc, ps, b2B, rows[:], ones1[:])

        for i in range(T // 128):
            E.dma(xt[:, 0:512], xw_in[1 + i * 128:1 + (i + 1) * 128, 0:512])
            E.dma(xt[:, 512:1024], xw_in[i * 128:(i + 1) * 128, 512:1024])
            E.v("reduce_sum", scr[:, 0:1], xt[:], AX.X)
            E.s("mul", scr[:, 1:2], scr[:, 0:1], 1.0 / D)
            E.v("tensor_scalar", xt[:], xt[:], scr[:, 1:2], None, ALU.subtract)
            E.s("activation", sq[:], xt[:], AF.Square, 0.0, 1.0, accum_out=scr[:, 0:1])
            E.s("activation", scr[:, 2:3], scr[:, 0:1], AF.Ln, kcol[:, 0:1], 1.0 / D)
            E.s("activation", scr[:, 3:4], scr[:, 2:3], AF.Exp, 0.0, -0.5)
            E.v("tensor_scalar", xt[:], xt[:], scr[:, 3:4], None, ALU.mult)
            E.v("tensor_mul", xt[:], xt[:], gB[:])
            E.v("tensor_add", xt[:], xt[:], blnB[:])
            E.s("copy", xnbf[:], xt[:])
            for dc in range(8):
                E.t("transpose", psT[:, dc * 128:(dc + 1) * 128], xnbf[:, dc * 128:(dc + 1) * 128],
                    identb[:], wait=(dc == 0))
            E.v("tensor_copy", bass.AP(xnT, i * 128, [[8 * T, 128], [T, 8], [1, 128]]), psT[:])

        for oc in range(OC):
            E.dma(bass.AP(wstage, 0, [[1024, 128], [128, 8], [1, 128]]),
                  w1p_in.rearrange("(c p) o -> p c o", p=128)[:, :, oc * 128:(oc + 1) * 128])
            E.s("copy", w1a[:], wstage[:, 0:1024])
            E.dma(bass.AP(wstage, 0, [[1024, 128], [128, 8], [1, 128]]),
                  w1p_in.rearrange("(c p) o -> p c o", p=128)[:, :, 2816 + oc * 128:2816 + (oc + 1) * 128])
            E.s("copy", w1g[:], wstage[:, 0:1024])
            E.dma(wstage[:, 0:D], w2p_in[oc * 128:(oc + 1) * 128, :])
            E.s("copy", w2c[:], wstage[:, 0:D])
            for lg in range(T // 512):
                tsl = slice(lg * 512, (lg + 1) * 512)
                for c in range(8):
                    E.mm(ps[:, 0:512], w1a[:, c * 128:(c + 1) * 128],
                         bass.AP(xnT, c * T + lg * 512, [[8 * T, 128], [1, 512]]),
                         c == 0, c == 7)
                for c in range(8):
                    E.mm(ps2[:, 0:512], w1g[:, c * 128:(c + 1) * 128],
                         bass.AP(xnT, c * T + lg * 512, [[8 * T, 128], [1, 512]]),
                         c == 0, c == 7)
                E.v("tensor_scalar", asb[:], ps[:, 0:512], b1p[:, oc:oc + 1], None, ALU.add)
                E.s("activation", ggs[:], ps2[:, 0:512], AF.Gelu, b1p[:, OC + oc:OC + oc + 1], 1.0)
                E.v("tensor_tensor", h2T[:, lg * 512:(lg + 1) * 512], asb[:], ggs[:], ALU.mult)
            for lg in range(T // 512):
                for d2c in range(8):
                    E.mm(psr[:, 0:512], w2c[:, d2c * 128:(d2c + 1) * 128],
                         h2T[:, lg * 512:(lg + 1) * 512], True, True)
                    dst = outacc[:, d2c * T + lg * 512: d2c * T + (lg + 1) * 512]
                    if oc == 0:
                        E.v("tensor_copy", dst, psr[:, 0:512])
                    else:
                        E.v("tensor_add", dst, dst, psr[:, 0:512])

        for tt in range(T // 128):
            for half in range(2):
                for j in range(4):
                    d2c = half * 4 + j
                    E.t("transpose", psF[:, j * 128:(j + 1) * 128],
                        outacc[:, d2c * T + tt * 128: d2c * T + (tt + 1) * 128], identf[:],
                        wait=(j == 0))
                E.v("tensor_copy", xt[:, half * 512:(half + 1) * 512], psF[:, 0:512])
            E.v("tensor_add", xt[:], xt[:], b2B[:])
            E.dma(sq[:, 0:512], xw_in[1 + tt * 128:1 + (tt + 1) * 128, 0:512])
            E.dma(sq[:, 512:1024], xw_in[1 + tt * 128:1 + (tt + 1) * 128, 512:1024])
            E.v("tensor_add", xt[:], xt[:], sq[:])
            E.dma(r_out[tt * 128:(tt + 1) * 128, :], xt[:])

        blk = es.enter_context(nc.Block())
        E.replay(blk)
    return nc


_P9A = None
_P9B = None


def p9_consts():
    import ml_dtypes
    DIM_HEAD, WINDOW, n = 64, 256, 4096
    half = np.arange(0, DIM_HEAD, 2, dtype=np.float32)
    inv_freq = 1.0 / 10000.0 ** (half / DIM_HEAD)
    t = np.arange(-257 + 1 - 255, n, dtype=np.float32)
    identb = np.eye(128, dtype=np.float32).astype(ml_dtypes.bfloat16)
    identf = np.eye(128, dtype=np.float32)
    rotm = np.zeros((128, 128), np.float32)
    for hh in range(2):
        for m in range(64):
            if m < 32:
                rotm[hh * 64 + m + 32, hh * 64 + m] = -1.0
            else:
                rotm[hh * 64 + m - 32, hh * 64 + m] = 1.0
    rot = rotm.astype(ml_dtypes.bfloat16)
    b2 = np.zeros((128, 2), np.float32)
    b2[0:64, 0] = 1.0
    b2[64:128, 1] = 1.0
    b2t = np.ascontiguousarray(b2.T)
    masks = np.zeros((4, 128, 512), np.float32)
    for variant in range(2):
        for qt in range(2):
            i = (np.arange(128) + qt * 128)[:, None]
            j2 = np.arange(512)[None, :]
            prev_ok = (i <= j2) & (j2 < 256)
            own_ok = (j2 >= 256) & ((j2 - 256) <= i)
            ok = prev_ok | own_ok
            if variant == 0:
                ok = own_ok
            masks[variant * 2 + qt] = np.where(ok, 0.0, -np.float32(np.finfo(np.float32).max))
    return identb, identf, rot, b2, b2t, masks


def p9_rope_tables(tglob):
    import ml_dtypes
    DIM_HEAD, WINDOW, n = 64, 256, 4096
    half = np.arange(0, DIM_HEAD, 2, dtype=np.float32)
    inv_freq = (1.0 / 10000.0 ** (half / DIM_HEAD)).astype(np.float32)
    freqs = tglob[:, None].astype(np.float32) * inv_freq[None, :]
    freqs = np.concatenate([freqs, freqs], -1)
    base_scale = ((half + 0.4 * DIM_HEAD) / (1.4 * DIM_HEAD)).astype(np.float32)
    power = ((tglob - n // 2) / (WINDOW // 2)).astype(np.float32)
    scale = base_scale[None, :] ** power[:, None]
    scale = np.concatenate([scale, scale], -1)
    cos, sin = np.cos(freqs), np.sin(freqs)
    cS = (cos * scale).T
    sS = (sin * scale).T
    cI = (cos / scale).T
    sI = (sin / scale).T
    f = lambda a: np.ascontiguousarray(np.tile(a, (2, 1))).astype(ml_dtypes.bfloat16)
    return f(cS), f(sS), f(cI), f(sI)


def run_p9a(x_full, IN, layer):
    global _P9A
    if _P9A is None:
        _P9A = build_p9a()
    identb, identf, rot, b2, b2t, masks = p9_consts()
    qs2 = np.zeros((128, 2), np.float32)
    qs2[:, 0] = np.tile(IN["L1_ffn_q_scale"][layer], 2)
    qs2[:, 1] = np.tile(IN["L1_ffn_k_scale"][layer], 2)
    maps = []
    for c in range(NCORE):
        b, sh = c // 2, c % 2
        q0 = sh * 2048
        xb = x_full.reshape(4, 4096, D)[b]
        xw = np.zeros((2305, D), np.float32)
        lo = q0 - 257
        src_lo = max(lo, 0)
        xw[src_lo - lo:] = xb[src_lo:q0 + 2048]
        tglob = np.arange(q0 - 256, q0 + 2048, dtype=np.float32)
        cS, sS, cI, sI = p9_rope_tables(tglob)
        maps.append({
            "xw": xw,
            "wqkv": IN["L1_ffn_Wqkv"][layer],
            "wo": IN["L1_ffn_Wo"][layer],
            "qs": qs2,
            "cosS": cS, "sinS": sS, "cosI": cI, "sinI": sI,
            "lmask": masks if sh == 0 else np.concatenate([masks[2:], masks[2:]], axis=0),
            "rot": rot, "b2sel": b2, "b2t": b2t, "identb": identb,
        })
    res = run_bass_kernel_spmd(_P9A, maps, core_ids=list(range(NCORE)))
    out = np.zeros((4, 4096, D), np.float32)
    for c in range(NCORE):
        b, sh = c // 2, c % 2
        out[b, sh * 2048:(sh + 1) * 2048] = res.results[c]["r"]
    return out.reshape(-1, D)


def run_p9b(x_full, IN, layer):
    global _P9B
    if _P9B is None:
        _P9B = build_p9b()
    identb, identf, rot, b2, b2t, masks = p9_consts()
    W1 = IN["L1_ffn_W1"][layer]
    b1 = IN["L1_ffn_b1"][layer]
    W2 = IN["L1_ffn_W2"][layer]
    w1p = np.zeros((D, 2 * 2816), np.float32)
    w1p[:, 0:2730] = W1[:, 0:2730]
    w1p[:, 2816:2816 + 2730] = W1[:, 2730:]
    b1p_flat = np.zeros(2 * 2816, np.float32)
    b1p_flat[0:2730] = b1[0:2730]
    b1p_flat[2816:2816 + 2730] = b1[2730:]
    b1p = np.zeros((128, 44), np.float32)
    b1p[:, 0:22] = b1p_flat[0:2816].reshape(22, 128).T
    b1p[:, 22:44] = b1p_flat[2816:].reshape(22, 128).T
    w2p = np.zeros((2816, D), np.float32)
    w2p[0:2730] = W2
    base = {
        "w1p": w1p, "b1p": b1p, "w2p": w2p,
        "b2row": IN["L1_ffn_b2"][layer].reshape(1, D),
        "lng": IN["L1_ffn_ln_g"][layer].reshape(1, D),
        "lnb": IN["L1_ffn_ln_b"][layer].reshape(1, D),
        "identb": identb, "identf": identf,
        "ones1": np.ones((1, 128), np.float32),
    }
    maps = []
    for c in range(NCORE):
        b, sh = c // 2, c % 2
        q0 = sh * 2048
        xb = x_full.reshape(4, 4096, D)[b]
        xw = np.zeros((2049, D), np.float32)
        lo = q0 - 1
        src_lo = max(lo, 0)
        xw[src_lo - lo:] = xb[src_lo:q0 + 2048]
        m = dict(base)
        m["xw"] = xw
        maps.append(m)
    res = run_bass_kernel_spmd(_P9B, maps, core_ids=list(range(NCORE)))
    out = np.zeros((4, 4096, D), np.float32)
    for c in range(NCORE):
        b, sh = c // 2, c % 2
        out[b, sh * 2048:(sh + 1) * 2048] = res.results[c]["r"]
    return out.reshape(-1, D)


def build_pln():
    nc = bass.Bass("TRN2", target_bir_lowering=False)
    x_in = nc.dram_tensor("x", [TC, D], F32, kind="ExternalInput").ap()
    s_out = nc.dram_tensor("s", [TC, D], F32, kind="ExternalOutput").ap()
    with ExitStack() as es:
        sb = lambda n, shp, dt=F32: es.enter_context(nc.sbuf_tensor(n, shp, dt))
        S = mk_sems(nc, es)
        kcol = sb("kcol", [128, 4])
        scr = sb("scr", [128, 8])
        xt = sb("xt", [128, D])
        sq = sb("sq", [128, D])
        E = Emit(nc, S)
        E.v("memset", kcol[:, 0:1], 1e-5)
        for i in range(TC // 128):
            E.dma(xt[:], x_in[i * 128:(i + 1) * 128, :])
            E.v("reduce_sum", scr[:, 0:1], xt[:], AX.X)
            E.s("mul", scr[:, 1:2], scr[:, 0:1], 1.0 / D)
            E.v("tensor_scalar", xt[:], xt[:], scr[:, 1:2], None, ALU.subtract)
            E.s("activation", sq[:], xt[:], AF.Square, 0.0, 1.0, accum_out=scr[:, 0:1])
            E.s("activation", scr[:, 2:3], scr[:, 0:1], AF.Ln, kcol[:, 0:1], 1.0 / D)
            E.s("activation", scr[:, 3:4], scr[:, 2:3], AF.Exp, 0.0, -0.5)
            E.v("tensor_scalar", xt[:], xt[:], scr[:, 3:4], None, ALU.mult)
            E.dma(s_out[i * 128:(i + 1) * 128, :], xt[:])
        blk = es.enter_context(nc.Block())
        E.replay(blk)
    return nc


def build_padd():
    nc = bass.Bass("TRN2", target_bir_lowering=False)
    a_in = nc.dram_tensor("a", [TC, D], F32, kind="ExternalInput").ap()
    b_in = nc.dram_tensor("b", [TC, D], F32, kind="ExternalInput").ap()
    r_out = nc.dram_tensor("r", [TC, D], F32, kind="ExternalOutput").ap()
    with ExitStack() as es:
        sb = lambda n, shp, dt=F32: es.enter_context(nc.sbuf_tensor(n, shp, dt))
        S = mk_sems(nc, es)
        at = sb("at", [128, D])
        bt = sb("bt", [128, D])
        E = Emit(nc, S)
        for i in range(TC // 128):
            E.dma(at[:], a_in[i * 128:(i + 1) * 128, :])
            E.dma(bt[:], b_in[i * 128:(i + 1) * 128, :])
            E.v("tensor_add", at[:], at[:], bt[:])
            E.dma(r_out[i * 128:(i + 1) * 128, :], at[:])
        blk = es.enter_context(nc.Block())
        E.replay(blk)
    return nc


_PLN = None
_PADD = None


def run_pln(x_full):
    global _PLN
    if _PLN is None:
        _PLN = build_pln()
    maps = [{"x": np.ascontiguousarray(x_full[c * TC:(c + 1) * TC])} for c in range(NCORE)]
    res = run_bass_kernel_spmd(_PLN, maps, core_ids=list(range(NCORE)))
    return np.concatenate([r["s"] for r in res.results], axis=0)


def run_padd(a_full, b_full):
    global _PADD
    if _PADD is None:
        _PADD = build_padd()
    maps = [{"a": np.ascontiguousarray(a_full[c * TC:(c + 1) * TC]),
             "b": np.ascontiguousarray(b_full[c * TC:(c + 1) * TC])} for c in range(NCORE)]
    res = run_bass_kernel_spmd(_PADD, maps, core_ids=list(range(NCORE)))
    return np.concatenate([r["r"] for r in res.results], axis=0)


def kernel(**inputs):
    IN = inputs
    x = np.asarray(inputs["L0_attn_a_x"], np.float32)
    r = np.ascontiguousarray(x.reshape(-1, D))

    y0 = run_p1(x, IN["L0_attn_a_x_proj_w"], IN["L0_attn_a_dt_proj_w"],
                IN["L0_attn_a_dt_proj_b"], IN["L0_attn_a_A_log"], IN["L0_attn_a_D_param"])
    r = run_p2(y0.reshape(-1, D), r, IN["L0_attn_a_ln_a"], IN["L0_attn_a_ln_b"])
    r = run_moe(r, IN)
    r = run_p6(r, IN)
    r = run_moe(r, IN)

    yT = run_p7(r, IN)
    r = run_p8(yT, r, IN)
    for _rep in range(2):
        s = run_pln(r)
        for i in range(6):
            s = run_p9a(s, IN, i)
            s = run_p9b(s, IN, i)
        r = run_padd(r, s)
        if _rep == 0:
            r = run_p10(r, IN)
    return r.reshape(4, 4096, D)


# version 03 (final)
# speedup vs baseline: 1.5292x; 1.2215x over previous
import numpy as np
from contextlib import ExitStack
import concourse.bass as bass
import concourse.mybir as mybir
from concourse.bass_utils import run_bass_kernel_spmd

F32 = mybir.dt.float32
BF16 = mybir.dt.bfloat16
AF = mybir.ActivationFunctionType
ALU = mybir.AluOpType
AX = mybir.AxisListType

NCORE = 8
BT = 16384
TC = BT // NCORE
D = 1024


class Emit:
    ENGS = ("sync", "vector", "scalar", "tensor", "gpsimd")

    def __init__(self, nc, sems):
        self.nc = nc
        self.sem = sems["sync"] if isinstance(sems, dict) else sems
        self.sd = sems["gpsimd"] if isinstance(sems, dict) else None
        self.tot = 0
        self.ld = 0
        self.prog = []

    def op(self, eng, f, *a, dma=False, wait=True, **kw):
        w = 16 if dma else 1
        self.prog.append((eng, f, a, kw, self.tot if wait else -1, w, self.ld, False))
        self.tot += w

    def mark(self):
        return self.tot

    def load(self, dst, src, wait_at, **kw):
        self.prog.append(("sync", self.nc.sync.dma_start, (dst, src), kw,
                          wait_at, 16, self.ld, True))
        self.ld += 16

    def dma(self, dst, src, **kw):
        self.op("sync", self.nc.sync.dma_start, dst, src, dma=True, **kw)

    def dma2(self, dst, src, **kw):
        self.op("gpsimd", self.nc.gpsimd.dma_start, dst, src, dma=True, **kw)

    def v(self, name, *a, **kw):
        self.op("vector", getattr(self.nc.vector, name), *a, **kw)

    def s(self, name, *a, **kw):
        self.op("scalar", getattr(self.nc.scalar, name), *a, **kw)

    def t(self, name, *a, **kw):
        self.op("tensor", getattr(self.nc.tensor, name), *a, **kw)

    def g(self, name, *a, **kw):
        self.op("gpsimd", getattr(self.nc.gpsimd, name), *a, **kw)

    def mm(self, out, lhsT, rhs, start, stop, wait=None):
        if wait is None:
            wait = bool(start)
        self.op("tensor", self.nc.tensor.matmul, out, lhsT, rhs,
                wait=wait, start=start, stop=stop)

    def replay(self, block):
        for eng_name in self.ENGS:
            ops = [p for p in self.prog if p[0] == eng_name]
            if not ops:
                continue

            def body(eng, ops=ops):
                sd_waited = 0
                for (_, f, a, kw, wv, inc, ldv, is_load) in ops:
                    if is_load:
                        if wv > 0:
                            eng.wait_ge(self.sem, wv)
                        f(*a, **kw).then_inc(self.sd, inc)
                        continue
                    if ldv > sd_waited:
                        eng.wait_ge(self.sd, ldv)
                        sd_waited = ldv
                    if wv > 0:
                        eng.wait_ge(self.sem, wv)
                    f(*a, **kw).then_inc(self.sem, inc)
            getattr(block, eng_name)(body)

        def tail(eng):
            eng.wait_ge(self.sem, self.tot)
            if self.ld > 0:
                eng.wait_ge(self.sd, self.ld)
        block.gpsimd(tail)


def mk_sems(nc, es):
    return {e: es.enter_context(nc.semaphore("S_" + e)) for e in Emit.ENGS}


def bcast_row(E, nc, ps_pool, out_sb, row_sb, ones1):
    W = row_sb.ap[-1][1] if hasattr(row_sb, "ap") else None
    W = row_sb.ap[-1][1]
    off = 0
    while off < W:
        w = min(512, W - off)
        E.mm(ps_pool[:, 0:w], ones1, row_sb[:, off:off + w], True, True)
        E.v("tensor_copy", out_sb[:, off:off + w], ps_pool[:, 0:w])
        off += w


def build_p2():
    nc = bass.Bass("TRN2", target_bir_lowering=False)
    y_in = nc.dram_tensor("y", [TC, D], F32, kind="ExternalInput").ap()
    x_in = nc.dram_tensor("x", [TC, D], F32, kind="ExternalInput").ap()
    lna_in = nc.dram_tensor("lna", [1, D], F32, kind="ExternalInput").ap()
    lnb_in = nc.dram_tensor("lnb", [1, D], F32, kind="ExternalInput").ap()
    ones_in = nc.dram_tensor("ones1", [1, 128], F32, kind="ExternalInput").ap()
    r_out = nc.dram_tensor("r", [TC, D], F32, kind="ExternalOutput").ap()

    with ExitStack() as es:
        sb = lambda n, shp, dt=F32: es.enter_context(nc.sbuf_tensor(n, shp, dt))
        S = mk_sems(nc, es)
        yt = sb("yt", [128, D])
        xt = sb("xt", [128, D])
        sq = sb("sq", [128, D])
        yc = sb("yc", [128, D])
        lnaB = sb("lnaB", [128, D])
        lnbB = sb("lnbB", [128, D])
        rows = sb("rows", [1, 2 * D])
        ones1 = sb("ones1s", [1, 128])
        scr = sb("scr", [128, 4])
        ps = es.enter_context(nc.psum_tensor("ps", [128, 512], F32))
        E = Emit(nc, S)

        E.dma(rows[:, 0:D], lna_in)
        E.dma(rows[:, D:2 * D], lnb_in)
        E.dma(ones1[:], ones_in)
        bcast_row(E, nc, ps, lnaB, rows[:, 0:D], ones1[:])
        bcast_row(E, nc, ps, lnbB, rows[:, D:2 * D], ones1[:])

        ntile = TC // 128
        for i in range(ntile):
            E.dma(yt[:], y_in[i * 128:(i + 1) * 128, :])
            E.dma(xt[:], x_in[i * 128:(i + 1) * 128, :])
            E.v("reduce_sum", scr[:, 0:1], yt[:], AX.X)
            E.s("mul", scr[:, 1:2], scr[:, 0:1], 1.0 / D)
            E.v("tensor_scalar", yc[:], yt[:], scr[:, 1:2], None, ALU.subtract)
            E.s("activation", sq[:], yc[:], AF.Square, 0.0, 1.0, accum_out=scr[:, 0:1])
            E.s("activation", scr[:, 2:3], scr[:, 0:1], AF.Sqrt, 0.0, 1.0 / (D - 1))
            E.v("tensor_scalar_add", scr[:, 2:3], scr[:, 2:3], 1e-12)
            E.v("reciprocal", scr[:, 3:4], scr[:, 2:3])
            E.v("tensor_scalar", yc[:], yc[:], scr[:, 3:4], None, ALU.mult)
            E.v("tensor_mul", yc[:], yc[:], lnaB[:])
            E.v("tensor_add", yc[:], yc[:], lnbB[:])
            E.v("tensor_add", yc[:], yc[:], xt[:])
            E.dma(r_out[i * 128:(i + 1) * 128, :], yc[:])

        blk = es.enter_context(nc.Block())
        E.replay(blk)
    return nc


def p2_inmaps(y_full, x_full, lna, lnb):
    maps = []
    for c in range(NCORE):
        sl = slice(c * TC, (c + 1) * TC)
        maps.append({
            "y": np.ascontiguousarray(y_full[sl]),
            "x": np.ascontiguousarray(x_full[sl]),
            "lna": lna.reshape(1, D),
            "lnb": lnb.reshape(1, D),
            "ones1": np.ones((1, 128), np.float32),
        })
    return maps


_P2 = None


def run_p2(y_full, x_full, lna, lnb):
    global _P2
    if _P2 is None:
        _P2 = build_p2()
    res = run_bass_kernel_spmd(_P2, p2_inmaps(y_full, x_full, lna, lnb), core_ids=list(range(NCORE)))
    return np.concatenate([r["r"] for r in res.results], axis=0)


def emit_ln_block(E, nc, src_ap, row0, xt, sq, scr, xnbf, psT, xnT_dst, identb, epscol):
    E.dma(xt[:], src_ap[row0:row0 + 128, :])
    E.v("reduce_sum", scr[:, 0:1], xt[:], AX.X)
    E.s("mul", scr[:, 1:2], scr[:, 0:1], 1.0 / D)
    E.v("tensor_scalar", xt[:], xt[:], scr[:, 1:2], None, ALU.subtract)
    E.s("activation", sq[:], xt[:], AF.Square, 0.0, 1.0, accum_out=scr[:, 0:1])
    E.s("activation", scr[:, 2:3], scr[:, 0:1], AF.Ln, epscol, 1.0 / D)
    E.s("activation", scr[:, 3:4], scr[:, 2:3], AF.Exp, 0.0, -0.5)
    E.v("tensor_scalar", xnbf[:], xt[:], scr[:, 3:4], None, ALU.mult)
    for dc in range(8):
        E.t("transpose", psT[:, dc * 128:(dc + 1) * 128], xnbf[:, dc * 128:(dc + 1) * 128], identb[:],
            wait=(dc == 0))
    E.v("tensor_copy", xnT_dst, psT[:])


def build_p1():
    nc = bass.Bass("TRN2", target_bir_lowering=False)
    L = 4096
    LH = 2048
    x_in = nc.dram_tensor("x", [L, D], F32, kind="ExternalInput").ap()
    xpwT_in = nc.dram_tensor("xpwT", [D, 96], F32, kind="ExternalInput").ap()
    dtwT_in = nc.dram_tensor("dtwT", [64, 512], F32, kind="ExternalInput").ap()
    dtb_in = nc.dram_tensor("dtb", [512], F32, kind="ExternalInput").ap()
    alog_in = nc.dram_tensor("alog", [512, 16], F32, kind="ExternalInput").ap()
    dpar_in = nc.dram_tensor("dpar", [512], F32, kind="ExternalInput").ap()
    sel_in = nc.dram_tensor("sel", [32, 32, 128], F32, kind="ExternalInput").ap()
    identb_in = nc.dram_tensor("identb", [128, 128], BF16, kind="ExternalInput").ap()
    identf_in = nc.dram_tensor("identf", [128, 128], F32, kind="ExternalInput").ap()
    y_out = nc.dram_tensor("y", [L, 512], F32, kind="ExternalOutput").ap()

    with ExitStack() as es:
        sb = lambda n, shp, dt=F32: es.enter_context(nc.sbuf_tensor(n, shp, dt))
        S = mk_sems(nc, es)
        identb = sb("identb_s", [128, 128], BF16)
        identf = sb("identf_s", [128, 128], F32)
        sel = sb("sel_s", [32, 32 * 128])
        xpw_bf = sb("xpw_bf", [128, 8 * 96], BF16)
        dtw_bf = sb("dtw_bf", [64, 512], BF16)
        cols = sb("cols", [128, 16])
        A_sb = sb("A_sb", [128, 64])
        hcar = sb("hcar", [128, 64])
        kcol = sb("kcol", [128, 4])
        scr = sb("scr", [128, 8])
        xt = sb("xt", [128, D])
        sq = sb("sq", [128, D])
        xnbf = sb("xnbf", [128, D], BF16)
        xnT = sb("xnT", [128, 8 * LH], BF16)
        dr_bf = sb("dr_bf", [64, LH], BF16)
        BC32 = sb("BC32", [32, LH])
        deltaT = sb("deltaT", [128, 4 * LH])
        BmB = sb("BmB", [128, LH])
        CbB = sb("CbB", [128, LH])
        dAt = sb("dAt", [128, LH])
        BXt = sb("BXt", [128, LH])
        ht = sb("ht", [128, LH])
        ut = sb("ut", [128, LH])
        yT = sb("yT", [128, 4 * LH])
        ytile = sb("ytile", [128, 512])
        dbcT = sb("dbcT", [96, LH])
        ps = es.enter_context(nc.psum_tensor("ps", [128, 512], F32))
        psT = es.enter_context(nc.psum_tensor("psT", [128, 1024], BF16))
        psF = es.enter_context(nc.psum_tensor("psF", [128, 512], F32))

        E = Emit(nc, S)
        E.v("memset", kcol[:, 0:1], 1e-5)
        E.v("memset", kcol[:, 1:2], 1.0)
        E.dma(identb[:], identb_in)
        E.dma(identf[:], identf_in)
        E.dma(sel[:], sel_in.rearrange("k j p -> k (j p)"))
        E.dma(bass.AP(xt, 0, [[1024, 128], [96, 8], [1, 96]]),
              xpwT_in.rearrange("(c p) r -> p c r", p=128))
        E.s("copy", xpw_bf[:], xt[:, 0:8 * 96])
        E.dma(sq[0:64, 0:512], dtwT_in)
        E.s("copy", dtw_bf[:], sq[0:64, 0:512])
        E.dma(cols[:, 0:4], dtb_in.rearrange("(c p) -> p c", p=128),
              allow_slow_non_contiguous=True)
        E.dma(cols[:, 4:8], dpar_in.rearrange("(c p) -> p c", p=128),
              allow_slow_non_contiguous=True)
        E.dma(bass.AP(A_sb, 0, [[64, 128], [16, 4], [1, 16]]),
              alog_in.rearrange("(c p) n -> p c n", p=128))
        E.s("activation", A_sb[:], A_sb[:], AF.Exp, 0.0, 1.0)
        E.v("tensor_scalar_mul", A_sb[:], A_sb[:], -1.0)

        dh = None
        for lh in range(2):
            l0 = lh * LH
            for i in range(LH // 128):
                dst = bass.AP(xnT, i * 128, [[8 * LH, 128], [LH, 8], [1, 128]])
                emit_ln_block(E, nc, x_in, l0 + i * 128, xt, sq, scr, xnbf, psT, dst, identb,
                              kcol[:, 0:1])
            for lg in range(LH // 512):
                for c in range(8):
                    E.mm(ps[0:96, 0:512],
                         bass.AP(xpw_bf, c * 96, [[8 * 96, 128], [1, 96]]),
                         bass.AP(xnT, c * LH + lg * 512, [[8 * LH, 128], [1, 512]]),
                         c == 0, c == 7)
                E.v("tensor_copy", dbcT[:, lg * 512:(lg + 1) * 512], ps[0:96, 0:512])
            E.s("copy", dr_bf[:], dbcT[0:64, :])
            E.v("tensor_copy", BC32[:], dbcT[64:96, :])
            for dc in range(4):
                for lg in range(LH // 512):
                    E.mm(ps[:, 0:512], dtw_bf[:, dc * 128:(dc + 1) * 128],
                         dr_bf[:, lg * 512:(lg + 1) * 512], True, True)
                    dslice = deltaT[:, dc * LH + lg * 512: dc * LH + (lg + 1) * 512]
                    E.v("tensor_scalar", dAt[:, 0:512], ps[:, 0:512], cols[:, dc:dc + 1], None, ALU.add)
                    E.s("activation", BXt[:, 0:512], dAt[:, 0:512], AF.Abs, 0.0, 1.0)
                    E.s("activation", BXt[:, 0:512], BXt[:, 0:512], AF.Exp, 0.0, -1.0)
                    E.s("activation", BXt[:, 0:512], BXt[:, 0:512], AF.Ln, kcol[:, 1:2], 1.0)
                    E.v("tensor_scalar_max", dAt[:, 0:512], dAt[:, 0:512], 0.0)
                    E.v("tensor_add", dslice, dAt[:, 0:512], BXt[:, 0:512])
            for n in range(16):
                for (dst, j) in ((BmB, n), (CbB, 16 + n)):
                    for lg in range(LH // 512):
                        E.mm(ps[:, 0:512], bass.AP(sel, j * 128, [[32 * 128, 32], [1, 128]]),
                             BC32[:, lg * 512:(lg + 1) * 512], True, True)
                        E.v("tensor_copy", dst[:, lg * 512:(lg + 1) * 512], ps[:, 0:512])
                for dc in range(4):
                    dsl = deltaT[:, dc * LH:(dc + 1) * LH]
                    E.v("tensor_scalar", dAt[:], dsl, A_sb[:, dc * 16 + n:dc * 16 + n + 1], None, ALU.mult)
                    E.s("activation", dAt[:], dAt[:], AF.Exp, 0.0, 1.0)
                    xn_dc = bass.AP(xnT, (XNT_DC_BASE[dc]) * LH, [[8 * LH, 128], [1, LH]])
                    E.v("tensor_mul", ut[:], dsl, xn_dc)
                    E.v("tensor_mul", BXt[:], ut[:], BmB[:])
                    init = 0.0 if lh == 0 else hcar[:, n * 4 + dc:n * 4 + dc + 1]
                    E.v("tensor_tensor_scan", ht[:], dAt[:], BXt[:], init, ALU.mult, ALU.add)
                    E.v("tensor_copy", hcar[:, n * 4 + dc:n * 4 + dc + 1], ht[:, LH - 1:LH])
                    E.v("tensor_mul", ht[:], ht[:], CbB[:])
                    if n == 0:
                        E.v("tensor_copy", yT[:, dc * LH:(dc + 1) * LH], ht[:])
                    else:
                        E.v("tensor_add", yT[:, dc * LH:(dc + 1) * LH], yT[:, dc * LH:(dc + 1) * LH], ht[:])
            for dc in range(4):
                xn_dc = bass.AP(xnT, (XNT_DC_BASE[dc]) * LH, [[8 * LH, 128], [1, LH]])
                E.op("vector", nc.vector.scalar_tensor_tensor, yT[:, dc * LH:(dc + 1) * LH],
                     xn_dc, cols[:, 4 + dc:5 + dc], yT[:, dc * LH:(dc + 1) * LH], ALU.mult, ALU.add)
            for tt in range(LH // 128):
                for dc in range(4):
                    E.t("transpose", psF[:, dc * 128:(dc + 1) * 128],
                        yT[:, dc * LH + tt * 128: dc * LH + (tt + 1) * 128], identf[:],
                        wait=(dc == 0))
                E.v("tensor_copy", ytile[:], psF[:])
                E.dma(y_out[l0 + tt * 128: l0 + (tt + 1) * 128, :], ytile[:])

        blk = es.enter_context(nc.Block())
        E.replay(blk)
    return nc


XNT_DC_BASE = [0, 1, 2, 3]


def p1_inmaps(x_full, xpw, dtw, dtb, alog, dpar):
    import ml_dtypes
    sel = np.zeros((32, 32, 128), np.float32)
    for j in range(32):
        sel[j, j, :] = 1.0
    identb = np.eye(128, dtype=np.float32).astype(ml_dtypes.bfloat16)
    identf = np.eye(128, dtype=np.float32)
    maps = []
    for c in range(NCORE):
        b, dh = c // 2, c % 2
        dsl = slice(dh * 512, (dh + 1) * 512)
        xb = x_full[b]
        if dh == 1:
            xb = np.concatenate([xb[:, 512:], xb[:, :512]], axis=1)
        xpw_r = xpw if dh == 0 else np.concatenate([xpw[:, 512:], xpw[:, :512]], axis=1)
        maps.append({
            "x": np.ascontiguousarray(xb),
            "xpwT": np.ascontiguousarray(xpw_r.T),
            "dtwT": np.ascontiguousarray(dtw[dsl, :].T),
            "dtb": np.ascontiguousarray(dtb[dsl]),
            "alog": np.ascontiguousarray(alog[dsl]),
            "dpar": np.ascontiguousarray(dpar[dsl]),
            "sel": sel,
            "identb": identb,
            "identf": identf,
        })
    return maps


_P1 = None


def run_p1(x_full, xpw, dtw, dtb, alog, dpar):
    global _P1
    if _P1 is None:
        _P1 = build_p1()
    res = run_bass_kernel_spmd(_P1, p1_inmaps(x_full, xpw, dtw, dtb, alog, dpar),
                               core_ids=list(range(NCORE)))
    y = np.zeros((4, 4096, 1024), np.float32)
    for c in range(NCORE):
        b, dh = c // 2, c % 2
        y[b, :, dh * 512:(dh + 1) * 512] = res.results[c]["y"]
    return y


def build_moe():
    nc = bass.Bass("TRN2", target_bir_lowering=False)
    T = TC
    x_in = nc.dram_tensor("x", [T, D], F32, kind="ExternalInput").ap()
    rwT_in = nc.dram_tensor("rwT", [D, 16], F32, kind="ExternalInput").ap()
    eb_in = nc.dram_tensor("eb", [1, 16], F32, kind="ExternalInput").ap()
    gwT_in = nc.dram_tensor("gwT", [16, D, 512], F32, kind="ExternalInput").ap()
    uwT_in = nc.dram_tensor("uwT", [16, D, 512], F32, kind="ExternalInput").ap()
    dwT_in = nc.dram_tensor("dwT", [16, 512, D], F32, kind="ExternalInput").ap()
    sgwT_in = nc.dram_tensor("sgwT", [D, D], F32, kind="ExternalInput").ap()
    suwT_in = nc.dram_tensor("suwT", [D, D], F32, kind="ExternalInput").ap()
    sdwT_in = nc.dram_tensor("sdwT", [D, D], F32, kind="ExternalInput").ap()
    eps_in = nc.dram_tensor("epsB", [128, 16], F32, kind="ExternalInput").ap()
    sel16_in = nc.dram_tensor("sel16", [16, 16, 128], F32, kind="ExternalInput").ap()
    ones_in = nc.dram_tensor("ones1", [1, 128], F32, kind="ExternalInput").ap()
    identb_in = nc.dram_tensor("identb", [128, 128], BF16, kind="ExternalInput").ap()
    identf_in = nc.dram_tensor("identf", [128, 128], F32, kind="ExternalInput").ap()
    r_out = nc.dram_tensor("r", [T, D], F32, kind="ExternalOutput").ap()

    with ExitStack() as es:
        sb = lambda n, shp, dt=F32: es.enter_context(nc.sbuf_tensor(n, shp, dt))
        S = mk_sems(nc, es)
        identb = sb("identb_s", [128, 128], BF16)
        identf = sb("identf_s", [128, 128], F32)
        ones1 = sb("ones1_s", [1, 128])
        sel16 = sb("sel16_s", [16, 16 * 128], BF16)
        epsB = sb("epsB_s", [128, 16])
        ebB = sb("ebB", [128, 16])
        kcol = sb("kcol", [128, 4])
        scr = sb("scr", [128, 8])
        xt = sb("xt", [128, D])
        sq = sb("sq", [128, D])
        xnbf = sb("xnbf", [128, D], BF16)
        xnT = sb("xnT", [128, 8 * T], BF16)
        combT = sb("combT", [16, T], BF16)
        routedT = sb("routedT", [128, 8 * T])
        ssig = sb("ssig", [128, 16])
        sc_t = sb("sc_t", [128, 16])
        w32 = sb("w32", [128, 32])
        w6 = sb("w6", [128, 8])
        gs = sb("gs", [128, 4])
        smask = sb("smask", [128, 16])
        maskedp = sb("maskedp", [128, 16])
        work = sb("work", [128, 16])
        sel4 = sb("sel4", [128, 16])
        outt = sb("outt", [128, D])
        ps = es.enter_context(nc.psum_tensor("ps", [128, 512], F32))
        ps2 = es.enter_context(nc.psum_tensor("ps2", [128, 512], F32))
        psr = es.enter_context(nc.psum_tensor("psr", [128, 512], F32))
        psT = es.enter_context(nc.psum_tensor("psT", [128, 1024], BF16))
        psF = es.enter_context(nc.psum_tensor("psF", [128, 512], F32))
        psM = es.enter_context(nc.psum_tensor("psM", [128, 16], F32))

        E = Emit(nc, S)
        E.v("memset", kcol[:, 0:1], 1e-5)
        E.dma(identb[:], identb_in)
        E.dma(identf[:], identf_in)
        E.dma(ones1[:], ones_in)
        E.dma(routedT[0:16, 0:16 * 128], sel16_in.rearrange("k j p -> k (j p)"))
        E.s("copy", sel16[:], routedT[0:16, 0:16 * 128])
        E.dma(epsB[:], eps_in)
        rw = sb("rw", [128, 8 * 16])
        E.dma(bass.AP(rw, 0, [[128, 128], [16, 8], [1, 16]]),
              rwT_in.rearrange("(c p) e -> p c e", p=128))
        rwb = sb("rwb", [128, 8 * 16], BF16)
        E.s("copy", rwb[:], rw[:])
        ebrow = sb("ebrow", [1, 16])
        E.dma(ebrow[:], eb_in)
        E.mm(psM[:, 0:16], ones1[:], ebrow[:], True, True)
        E.v("tensor_copy", ebB[:], psM[:, 0:16])

        ntile = T // 128
        for i in range(ntile):
            dst = bass.AP(xnT, i * 128, [[8 * T, 128], [T, 8], [1, 128]])
            emit_ln_block(E, nc, x_in, i * 128, xt, sq, scr, xnbf, psT, dst, identb, kcol[:, 0:1])
            for c in range(8):
                E.mm(psM[:, 0:16], bass.AP(xnT, c * T + i * 128, [[8 * T, 128], [1, 128]]),
                     rwb[:, c * 16:(c + 1) * 16], c == 0, c == 7)
            E.s("activation", ssig[:], psM[:, 0:16], AF.Sigmoid, 0.0, 1.0)
            E.v("tensor_add", sc_t[:], ssig[:], ebB[:])
            pairs = [(0, 1), (0, 2), (0, 3), (1, 2), (1, 3), (2, 3)]
            for pi, (a, b) in enumerate(pairs):
                apa = bass.AP(sc_t, a, [[16, 128], [4, 4]])
                apb = bass.AP(sc_t, b, [[16, 128], [4, 4]])
                E.v("tensor_tensor", w32[:, pi * 4:(pi + 1) * 4], apa, apb, ALU.add)
            E.v("tensor_max", gs[:], w32[:, 0:4], w32[:, 4:8])
            E.v("tensor_max", gs[:], gs[:], w32[:, 8:12])
            E.v("tensor_max", gs[:], gs[:], w32[:, 12:16])
            E.v("tensor_max", gs[:], gs[:], w32[:, 16:20])
            E.v("tensor_max", gs[:], gs[:], w32[:, 20:24])
            for pi, (a, b) in enumerate(pairs):
                E.v("tensor_tensor", w6[:, pi:pi + 1], gs[:, a:a + 1], gs[:, b:b + 1], ALU.min)
            E.v("tensor_max", w6[:, 6:7], w6[:, 0:1], w6[:, 1:2])
            E.v("tensor_max", w6[:, 6:7], w6[:, 6:7], w6[:, 2:3])
            E.v("tensor_max", w6[:, 6:7], w6[:, 6:7], w6[:, 3:4])
            E.v("tensor_max", w6[:, 6:7], w6[:, 6:7], w6[:, 4:5])
            E.v("tensor_max", w6[:, 6:7], w6[:, 6:7], w6[:, 5:6])
            E.v("tensor_scalar", gs[:], gs[:], w6[:, 6:7], None, ALU.is_ge)
            E.v("tensor_copy", bass.AP(smask, 0, [[16, 128], [4, 4], [1, 4]]),
                bass.AP(gs, 0, [[4, 128], [1, 4], [0, 4]]))
            E.v("tensor_mul", maskedp[:], sc_t[:], smask[:])
            E.v("tensor_sub", maskedp[:], maskedp[:], epsB[:])
            E.v("tensor_copy", work[:], maskedp[:])
            for k in range(4):
                E.v("reduce_max", scr[:, 4:5], work[:], AX.X)
                if k < 3:
                    E.v("tensor_scalar", sel4[:], work[:], scr[:, 4:5], None, ALU.is_equal)
                    E.op("vector", nc.vector.scalar_tensor_tensor, work[:], sel4[:], -1e30,
                         work[:], ALU.mult, ALU.add)
            E.v("tensor_scalar", sel4[:], maskedp[:], scr[:, 4:5], None, ALU.is_ge)
            E.v("tensor_mul", sel4[:], sel4[:], ssig[:])
            E.v("reduce_sum", scr[:, 5:6], sel4[:], AX.X)
            E.v("tensor_scalar_add", scr[:, 5:6], scr[:, 5:6], 1e-20)
            E.v("reciprocal", scr[:, 6:7], scr[:, 5:6])
            E.v("tensor_scalar", sel4[:], sel4[:], scr[:, 6:7], 2.5, ALU.mult, op1=ALU.mult)
            E.t("transpose", psF[0:16, 0:128], sel4[:], identf[:])
            E.v("tensor_copy", combT[:, i * 128:(i + 1) * 128], psF[0:16, 0:128])

        markS = [0, 0]
        usec = [0]

        def stage_load(E, stage, wbf, src_re, csize, ccount):
            per = 2048 // csize
            par = usec[0] % 2
            usec[0] += 1
            base = par * 2048
            wa = markS[par]
            for c0 in range(0, ccount, per):
                n = min(per, ccount - c0)
                dst = bass.AP(stage, base, [[2 * 2048, 128], [csize, n], [1, csize]])
                E.load(dst, src_re[:, c0:c0 + n, :], wa)
                E.s("copy", wbf[:, c0 * csize:(c0 + n) * csize],
                    bass.AP(stage, base, [[2 * 2048, 128], [1, n * csize]]))
                wa = E.mark()
            markS[par] = E.mark()

        with nc.sbuf_tensor("stage", [128, 2 * 2048], F32) as stage, \
             nc.sbuf_tensor("gw_bf", [128, 8 * D], BF16) as gw_bf, \
             nc.sbuf_tensor("uw_bf", [128, 8 * D], BF16) as uw_bf, \
             nc.sbuf_tensor("dw_bf", [128, 8 * D], BF16) as dw_bf, \
             nc.sbuf_tensor("gsil", [128, 8 * 512], BF16) as gsil, \
             nc.sbuf_tensor("h2", [128, 8 * 512], BF16) as h2:
            for e in range(16):
                stage_load(E, stage, gw_bf,
                           gwT_in[e].rearrange("(c p) i -> p c i", p=128), 512, 8)
                stage_load(E, stage, uw_bf,
                           uwT_in[e].rearrange("(c p) i -> p c i", p=128), 512, 8)
                stage_load(E, stage, dw_bf,
                           dwT_in[e].rearrange("(c p) h -> p c h", p=128), D, 4)
                for lg in range(T // 512):
                    tsl = slice(lg * 512, (lg + 1) * 512)
                    for ic in range(4):
                        for c in range(8):
                            E.mm(ps[:, 0:512],
                                 bass.AP(gw_bf, c * 512 + ic * 128, [[8 * D, 128], [1, 128]]),
                                 bass.AP(xnT, c * T + lg * 512, [[8 * T, 128], [1, 512]]),
                                 c == 0, c == 7)
                        E.s("activation", gsil[:, ic * 512:(ic + 1) * 512], ps[:, 0:512], AF.Silu, 0.0, 1.0)
                        for c in range(8):
                            E.mm(ps2[:, 0:512],
                                 bass.AP(uw_bf, c * 512 + ic * 128, [[8 * D, 128], [1, 128]]),
                                 bass.AP(xnT, c * T + lg * 512, [[8 * T, 128], [1, 512]]),
                                 c == 0, c == 7)
                        E.v("tensor_tensor", h2[:, ic * 512:(ic + 1) * 512],
                            gsil[:, ic * 512:(ic + 1) * 512], ps2[:, 0:512], ALU.mult)
                    E.mm(ps2[:, 0:512], bass.AP(sel16, e * 128, [[16 * 128, 16], [1, 128]]),
                         combT[:, tsl], True, True)
                    for ic in range(4):
                        E.v("tensor_tensor", h2[:, ic * 512:(ic + 1) * 512],
                            h2[:, ic * 512:(ic + 1) * 512], ps2[:, 0:512], ALU.mult)
                    for d2c in range(8):
                        for ic in range(4):
                            E.mm(psr[:, 0:512],
                                 bass.AP(dw_bf, ic * D + d2c * 128, [[8 * D, 128], [1, 128]]),
                                 h2[:, ic * 512:(ic + 1) * 512], ic == 0, ic == 3)
                        dst = routedT[:, d2c * T + lg * 512: d2c * T + (lg + 1) * 512]
                        if e == 0:
                            E.v("tensor_copy", dst, psr[:, 0:512])
                        else:
                            E.v("tensor_add", dst, dst, psr[:, 0:512])

            stage_load(E, stage, gw_bf, sgwT_in.rearrange("(c p) i -> p c i", p=128), D, 8)
            stage_load(E, stage, uw_bf, suwT_in.rearrange("(c p) i -> p c i", p=128), D, 8)
            stage_load(E, stage, dw_bf, sdwT_in.rearrange("(c p) i -> p c i", p=128), D, 8)
            for lg in range(T // 512):
                for ic in range(8):
                    for c in range(8):
                        E.mm(ps[:, 0:512],
                             bass.AP(gw_bf, c * D + ic * 128, [[8 * D, 128], [1, 128]]),
                             bass.AP(xnT, c * T + lg * 512, [[8 * T, 128], [1, 512]]),
                             c == 0, c == 7)
                    E.s("activation", gsil[:, ic * 512:(ic + 1) * 512], ps[:, 0:512], AF.Silu, 0.0, 1.0)
                    for c in range(8):
                        E.mm(ps2[:, 0:512],
                             bass.AP(uw_bf, c * D + ic * 128, [[8 * D, 128], [1, 128]]),
                             bass.AP(xnT, c * T + lg * 512, [[8 * T, 128], [1, 512]]),
                             c == 0, c == 7)
                    E.v("tensor_tensor", h2[:, ic * 512:(ic + 1) * 512],
                        gsil[:, ic * 512:(ic + 1) * 512], ps2[:, 0:512], ALU.mult)
                for d2c in range(8):
                    for ic in range(8):
                        E.mm(psr[:, 0:512],
                             bass.AP(dw_bf, ic * D + d2c * 128, [[8 * D, 128], [1, 128]]),
                             h2[:, ic * 512:(ic + 1) * 512], ic == 0, ic == 7)
                    dst = routedT[:, d2c * T + lg * 512: d2c * T + (lg + 1) * 512]
                    E.v("tensor_add", dst, dst, psr[:, 0:512])

        for tt in range(ntile):
            for half in range(2):
                for j in range(4):
                    d2c = half * 4 + j
                    E.t("transpose", psF[:, j * 128:(j + 1) * 128],
                        routedT[:, d2c * T + tt * 128: d2c * T + (tt + 1) * 128], identf[:],
                        wait=(j == 0))
                E.v("tensor_copy", outt[:, half * 512:(half + 1) * 512], psF[:, 0:512])
            E.dma(xt[:], x_in[tt * 128:(tt + 1) * 128, :])
            E.v("tensor_add", outt[:], outt[:], xt[:])
            E.dma(r_out[tt * 128:(tt + 1) * 128, :], outt[:])

        blk = es.enter_context(nc.Block())
        E.replay(blk)
    return nc


def moe_inmaps(x_full, IN):
    import ml_dtypes
    sel16 = np.zeros((16, 16, 128), np.float32)
    for j in range(16):
        sel16[j, j, :] = 1.0
    epsB = np.tile((np.arange(16, dtype=np.float32) * 1e-6)[None, :], (128, 1))
    identb = np.eye(128, dtype=np.float32).astype(ml_dtypes.bfloat16)
    identf = np.eye(128, dtype=np.float32)
    base = {
        "rwT": np.ascontiguousarray(IN["L0_ffn_router_weight"].T),
        "eb": IN["L0_ffn_e_bias"].reshape(1, 16),
        "gwT": np.ascontiguousarray(IN["L0_ffn_gate_w"].transpose(0, 2, 1)),
        "uwT": np.ascontiguousarray(IN["L0_ffn_up_w"].transpose(0, 2, 1)),
        "dwT": np.ascontiguousarray(IN["L0_ffn_down_w"].transpose(0, 2, 1)),
        "sgwT": np.ascontiguousarray(IN["L0_ffn_sh_gate_w"].T),
        "suwT": np.ascontiguousarray(IN["L0_ffn_sh_up_w"].T),
        "sdwT": np.ascontiguousarray(IN["L0_ffn_sh_down_w"].T),
        "epsB": epsB,
        "sel16": sel16,
        "ones1": np.ones((1, 128), np.float32),
        "identb": identb,
        "identf": identf,
    }
    maps = []
    for c in range(NCORE):
        m = dict(base)
        m["x"] = np.ascontiguousarray(x_full[c * TC:(c + 1) * TC])
        maps.append(m)
    return maps


_PMOE = None


def run_moe(x_full, IN):
    global _PMOE
    if _PMOE is None:
        _PMOE = build_moe()
    res = run_bass_kernel_spmd(_PMOE, moe_inmaps(x_full, IN), core_ids=list(range(NCORE)))
    return np.concatenate([r["r"] for r in res.results], axis=0)


def build_p6():
    nc = bass.Bass("TRN2", target_bir_lowering=False)
    L = 4096
    QH = 2048
    x_in = nc.dram_tensor("x", [L, D], F32, kind="ExternalInput").ap()
    wqkv_in = nc.dram_tensor("wqkv", [D, 3 * D], F32, kind="ExternalInput").ap()
    bq128_in = nc.dram_tensor("bq128", [128, 24], F32, kind="ExternalInput").ap()
    bvrow_in = nc.dram_tensor("bvrow", [1, D], F32, kind="ExternalInput").ap()
    wo_in = nc.dram_tensor("wo", [D, D], F32, kind="ExternalInput").ap()
    bout_in = nc.dram_tensor("bout", [1, D], F32, kind="ExternalInput").ap()
    cos_in = nc.dram_tensor("cosb", [128, L], BF16, kind="ExternalInput").ap()
    sin_in = nc.dram_tensor("sinb", [128, L], BF16, kind="ExternalInput").ap()
    rot_in = nc.dram_tensor("rot", [128, 128], BF16, kind="ExternalInput").ap()
    identb_in = nc.dram_tensor("identb", [128, 128], BF16, kind="ExternalInput").ap()
    ones_in = nc.dram_tensor("ones1", [1, 128], F32, kind="ExternalInput").ap()
    r_out = nc.dram_tensor("r", [QH, D], F32, kind="ExternalOutput").ap()
    QOFF = 0

    with ExitStack() as es:
        sb = lambda n, shp, dt=F32: es.enter_context(nc.sbuf_tensor(n, shp, dt))
        S = mk_sems(nc, es)
        identb = sb("identb_s", [128, 128], BF16)
        rot = sb("rot_s", [128, 128], BF16)
        ones1 = sb("ones1_s", [1, 128])
        cosb = sb("cosb_s", [128, L], BF16)
        sinb = sb("sinb_s", [128, L], BF16)
        bq128 = sb("bq128_s", [128, 24])
        bvB = sb("bvB", [128, D])
        boutB = sb("boutB", [128, D])
        rows = sb("rows", [1, 2 * D])
        kcol = sb("kcol", [128, 4])
        scr = sb("scr", [128, 8])
        xt = sb("xt", [128, D])
        sq = sb("sq", [128, D])
        xnbf = sb("xnbf", [128, D], BF16)
        xnT = sb("xnT", [128, 8 * L], BF16)
        attnoutT = sb("attnoutT", [128, 8 * QH], BF16)
        wstage = sb("wstage", [128, 8 * 128])
        wq_bf = sb("wq_bf", [128, 8 * 128], BF16)
        wk_bf = sb("wk_bf", [128, 8 * 128], BF16)
        wv_bf = sb("wv_bf", [128, 8 * 128], BF16)
        wo_bf = sb("wo_bf", [128, 8 * D], BF16)
        vaug = sb("vaug", [128, 32 * 130], BF16)
        QT = sb("QT", [128, L], BF16)
        KT = sb("KT", [128, L], BF16)
        qf = sb("qf", [128, 512])
        t1 = sb("t1", [128, 512])
        t2 = sb("t2", [128, 512])
        qb = sb("qb", [128, 512], BF16)
        attnT = sb("attnT", [128, 512], BF16)
        osb = sb("osb", [64, 512])
        rrow = sb("rrow", [1, 512])
        outt = sb("outt", [128, D])
        ps = es.enter_context(nc.psum_tensor("ps", [128, 512], F32))
        ps2 = es.enter_context(nc.psum_tensor("ps2", [128, 512], F32))
        psO = es.enter_context(nc.psum_tensor("psO", [128, 512], F32))
        psT = es.enter_context(nc.psum_tensor("psT", [128, 1024], BF16))

        E = Emit(nc, S)
        E.v("memset", kcol[:, 0:1], 1e-5)
        E.v("memset", kcol[:, 1:2], -30.0)
        E.dma(identb[:], identb_in)
        E.dma(rot[:], rot_in)
        E.dma(ones1[:], ones_in)
        E.dma(cosb[:], cos_in)
        E.dma(sinb[:], sin_in)
        E.dma(bq128[:], bq128_in)
        E.dma(rows[:, 0:D], bvrow_in)
        E.dma(rows[:, D:2 * D], bout_in)
        bcast_row(E, nc, ps, bvB, rows[:, 0:D], ones1[:])
        bcast_row(E, nc, ps, boutB, rows[:, D:2 * D], ones1[:])
        for hp in range(8):
            E.dma(wstage[:, 0:D], wo_in[hp * 128:(hp + 1) * 128, :])
            E.s("copy", wo_bf[:, hp * D:(hp + 1) * D], wstage[:, 0:D])

        E.v("memset", bass.AP(vaug, 64, [[32 * 130, 128], [130, 32], [65, 2], [1, 1]]), 1.0)

        for i in range(L // 128):
            dst = bass.AP(xnT, i * 128, [[8 * L, 128], [L, 8], [1, 128]])
            emit_ln_block(E, nc, x_in, i * 128, xt, sq, scr, xnbf, psT, dst, identb, kcol[:, 0:1])

        QG = QH // 512
        for hp in range(8):
            for (wbf, sec) in ((wq_bf, 0), (wk_bf, 1), (wv_bf, 2)):
                col0 = sec * D + hp * 128
                E.dma(bass.AP(wstage, 0, [[1024, 128], [128, 8], [1, 128]]),
                      wqkv_in.rearrange("(c p) o -> p c o", p=128)[:, :, col0:col0 + 128])
                E.s("copy", wbf[:, 0:1024], wstage[:, 0:1024])
            for tt in range(L // 128):
                for c in range(8):
                    E.mm(ps[:, 0:128],
                         bass.AP(xnT, c * L + tt * 128, [[8 * L, 128], [1, 128]]),
                         wv_bf[:, c * 128:(c + 1) * 128], c == 0, c == 7)
                E.op("vector", nc.vector.tensor_tensor,
                     bass.AP(vaug, tt * 130, [[32 * 130, 128], [65, 2], [1, 64]]),
                     ps[:, 0:128], bvB[:, hp * 128:(hp + 1) * 128], ALU.add)
            for (wbf, QKT, bcol) in ((wq_bf, QT, hp), (wk_bf, KT, 8 + hp)):
                for lg in range(L // 512):
                    tsl = slice(lg * 512, (lg + 1) * 512)
                    for c in range(8):
                        E.mm(ps[:, 0:512], wbf[:, c * 128:(c + 1) * 128],
                             bass.AP(xnT, c * L + lg * 512, [[8 * L, 128], [1, 512]]),
                             c == 0, c == 7)
                    E.v("tensor_scalar", qf[:], ps[:, 0:512], bq128[:, bcol:bcol + 1], None, ALU.add)
                    E.s("copy", qb[:], qf[:])
                    E.mm(ps2[:, 0:512], rot[:], qb[:], True, True)
                    E.v("tensor_tensor", t1[:], qf[:], cosb[:, tsl], ALU.mult)
                    E.v("tensor_tensor", t2[:], ps2[:, 0:512], sinb[:, tsl], ALU.mult)
                    E.v("tensor_tensor", QKT[:, tsl], t1[:], t2[:], ALU.add)
            for h in range(2):
                hb = h * 64
                for qg in range(QG):
                    qsl = slice(QOFF + qg * 512, QOFF + (qg + 1) * 512)
                    for kc in range(32):
                        E.mm(ps[:, 0:512],
                             KT[hb:hb + 64, kc * 128:(kc + 1) * 128],
                             QT[hb:hb + 64, qsl], True, True)
                        E.s("activation", attnT[:], ps[:, 0:512], AF.Exp, kcol[:, 1:2], 0.125)
                        E.mm(psO[0:65, 0:512],
                             bass.AP(vaug, kc * 130 + h * 65, [[32 * 130, 128], [1, 65]]),
                             attnT[:], kc == 0, kc == 31, wait=True)
                    E.v("reciprocal", rrow[:], psO[64:65, 0:512])
                    E.mm(ps2[0:64, 0:512], ones1[0:1, 0:64], rrow[:], True, True)
                    E.v("tensor_copy", osb[:], psO[0:64, 0:512])
                    E.v("tensor_tensor",
                        attnoutT[hb:hb + 64, hp * QH + qg * 512:hp * QH + (qg + 1) * 512],
                        osb[:], ps2[0:64, 0:512], ALU.mult)
        for tt in range(QH // 128):
            for og in range(2):
                for hp in range(8):
                    E.mm(ps[:, 0:512],
                         attnoutT[:, hp * QH + tt * 128:hp * QH + (tt + 1) * 128],
                         wo_bf[:, hp * D + og * 512:hp * D + og * 512 + 512],
                         hp == 0, hp == 7)
                E.v("tensor_copy", outt[:, og * 512:(og + 1) * 512], ps[:, 0:512])
            E.v("tensor_add", outt[:], outt[:], boutB[:])
            E.dma(xt[:], x_in[QOFF + tt * 128:QOFF + (tt + 1) * 128, :])
            E.v("tensor_add", outt[:], outt[:], xt[:])
            E.dma(r_out[tt * 128:(tt + 1) * 128, :], outt[:])

        blk = es.enter_context(nc.Block())
        E.replay(blk)
    return nc


def p6_consts():
    import ml_dtypes
    Dh = 64
    inv_freq = 1.0 / 10000.0 ** (np.arange(0, Dh, 2, dtype=np.float32) / Dh)
    freqs = np.arange(4096, dtype=np.float32)[:, None] * inv_freq[None, :]
    emb = np.concatenate((freqs, freqs), axis=-1)
    cos = np.cos(emb).T
    sin = np.sin(emb).T
    cosb = np.tile(cos, (2, 1)).astype(ml_dtypes.bfloat16)
    sinb = np.tile(sin, (2, 1)).astype(ml_dtypes.bfloat16)
    rotm = np.zeros((128, 128), np.float32)
    for hh in range(2):
        for m in range(64):
            if m < 32:
                rotm[hh * 64 + m + 32, hh * 64 + m] = -1.0
            else:
                rotm[hh * 64 + m - 32, hh * 64 + m] = 1.0
    rot = rotm.astype(ml_dtypes.bfloat16)
    identb = np.eye(128, dtype=np.float32).astype(ml_dtypes.bfloat16)
    return cosb, sinb, rot, identb


def p6_inmaps(x_full, IN):
    cosb, sinb, rot, identb = p6_consts()
    wqkv = IN["L0_attn_b_W_qkv"]
    bqkv = IN["L0_attn_b_b_qkv"]
    base = {
        "wqkv": wqkv,
        "bq128": np.ascontiguousarray(bqkv[:2048].reshape(16, 128).T),
        "bvrow": bqkv[2048:].reshape(1, D),
        "wo": IN["L0_attn_b_W_out"],
        "bout": IN["L0_attn_b_b_out"].reshape(1, D),
        "rot": rot,
        "identb": identb,
        "ones1": np.ones((1, 128), np.float32),
    }
    bq = np.zeros((128, 24), np.float32)
    bq[:, 0:8] = bqkv[0:1024].reshape(8, 128).T
    bq[:, 8:16] = bqkv[1024:2048].reshape(8, 128).T
    base["bq128"] = bq
    maps = []
    for c in range(NCORE):
        b, qh = c // 2, c % 2
        xb = x_full.reshape(4, 4096, D)[b]
        m = dict(base)
        if qh == 1:
            m["x"] = np.ascontiguousarray(np.concatenate([xb[2048:], xb[:2048]], axis=0))
            m["cosb"] = np.ascontiguousarray(np.concatenate([cosb[:, 2048:], cosb[:, :2048]], axis=1))
            m["sinb"] = np.ascontiguousarray(np.concatenate([sinb[:, 2048:], sinb[:, :2048]], axis=1))
        else:
            m["x"] = np.ascontiguousarray(xb)
            m["cosb"] = cosb
            m["sinb"] = sinb
        maps.append(m)
    return maps


_P6 = None


def run_p6(x_full, IN):
    global _P6
    if _P6 is None:
        _P6 = build_p6()
    res = run_bass_kernel_spmd(_P6, p6_inmaps(x_full, IN), core_ids=list(range(NCORE)))
    out = np.zeros((4, 4096, D), np.float32)
    for c in range(NCORE):
        b, qh = c // 2, c % 2
        out[b, qh * 2048:(qh + 1) * 2048] = res.results[c]["r"]
    return out.reshape(-1, D)


def build_p7():
    nc = bass.Bass("TRN2", target_bir_lowering=False)
    L = 4096
    LH = 2048
    x_in = nc.dram_tensor("x", [L, D], F32, kind="ExternalInput").ap()
    wx_in = nc.dram_tensor("wx", [D, 96], F32, kind="ExternalInput").ap()
    wdt_in = nc.dram_tensor("wdt", [64, 512], F32, kind="ExternalInput").ap()
    dtb_in = nc.dram_tensor("dtb", [512], F32, kind="ExternalInput").ap()
    alog_in = nc.dram_tensor("alog", [512, 16], F32, kind="ExternalInput").ap()
    dpar_in = nc.dram_tensor("dpar", [512], F32, kind="ExternalInput").ap()
    sel_in = nc.dram_tensor("sel", [32, 32, 128], F32, kind="ExternalInput").ap()
    identb_in = nc.dram_tensor("identb", [128, 128], BF16, kind="ExternalInput").ap()
    yT_out = nc.dram_tensor("yT", [512, L], BF16, kind="ExternalOutput").ap()

    with ExitStack() as es:
        sb = lambda n, shp, dt=F32: es.enter_context(nc.sbuf_tensor(n, shp, dt))
        S = mk_sems(nc, es)
        identb = sb("identb_s", [128, 128], BF16)
        sel = sb("sel_s", [32, 32 * 128])
        wx_bf = sb("wx_bf", [128, 8 * 96], BF16)
        wdt_bf = sb("wdt_bf", [64, 512], BF16)
        cols = sb("cols", [128, 16])
        A_sb = sb("A_sb", [128, 64])
        numcar = sb("numcar", [128, 64])
        sumd = sb("sumd", [128, 12])
        kcol = sb("kcol", [128, 4])
        scr = sb("scr", [128, 8])
        deltaT = sb("deltaT", [128, 4 * L])
        xnhalfT = sb("xnhalfT", [128, 4 * L], BF16)
        BC32f = sb("BC32f", [32, L])
        ps = es.enter_context(nc.psum_tensor("ps", [128, 512], F32))
        psT = es.enter_context(nc.psum_tensor("psT", [128, 1024], BF16))

        E = Emit(nc, S)
        E.v("memset", kcol[:, 0:1], 1e-5)
        E.v("memset", kcol[:, 1:2], 1.0)
        E.v("memset", kcol[:, 2:3], 0.0)
        E.dma(identb[:], identb_in)
        E.dma(sel[:], sel_in.rearrange("k j p -> k (j p)"))
        E.dma(cols[:, 0:4], dtb_in.rearrange("(c p) -> p c", p=128), allow_slow_non_contiguous=True)
        E.dma(cols[:, 4:8], dpar_in.rearrange("(c p) -> p c", p=128), allow_slow_non_contiguous=True)
        E.dma(bass.AP(A_sb, 0, [[64, 128], [16, 4], [1, 16]]),
              alog_in.rearrange("(c p) n -> p c n", p=128))
        E.s("activation", A_sb[:], A_sb[:], AF.Exp, 0.0, 1.0)
        E.v("tensor_scalar_mul", A_sb[:], A_sb[:], -1.0)

        with nc.sbuf_tensor("xt", [128, D], F32) as xt, \
             nc.sbuf_tensor("sq", [128, D], F32) as sq, \
             nc.sbuf_tensor("xnbf", [128, D], BF16) as xnbf, \
             nc.sbuf_tensor("xnT", [128, 8 * LH], BF16) as xnT, \
             nc.sbuf_tensor("dbcT", [96, LH], F32) as dbcT, \
             nc.sbuf_tensor("dr_bf", [64, LH], BF16) as dr_bf, \
             nc.sbuf_tensor("wtmp", [128, 2 * 512], F32) as wtmp:
            E.dma(bass.AP(xt, 0, [[1024, 128], [96, 8], [1, 96]]),
                  wx_in.rearrange("(c p) r -> p c r", p=128))
            E.s("copy", wx_bf[:], xt[:, 0:8 * 96])
            E.dma(sq[0:64, 0:512], wdt_in)
            E.s("copy", wdt_bf[:], sq[0:64, 0:512])
            for lh in range(2):
                l0 = lh * LH
                for i in range(LH // 128):
                    dst = bass.AP(xnT, i * 128, [[8 * LH, 128], [LH, 8], [1, 128]])
                    emit_ln_block(E, nc, x_in, l0 + i * 128, xt, sq, scr, xnbf, psT, dst, identb,
                                  kcol[:, 0:1])
                    E.v("tensor_copy",
                        bass.AP(xnhalfT, l0 + i * 128, [[4 * L, 128], [L, 4], [1, 128]]),
                        psT[:, 0:512])
                for lg in range(LH // 512):
                    for c in range(8):
                        E.mm(ps[0:96, 0:512],
                             bass.AP(wx_bf, c * 96, [[8 * 96, 128], [1, 96]]),
                             bass.AP(xnT, c * LH + lg * 512, [[8 * LH, 128], [1, 512]]),
                             c == 0, c == 7)
                    E.v("tensor_copy", dbcT[:, lg * 512:(lg + 1) * 512], ps[0:96, 0:512])
                E.s("copy", dr_bf[:], dbcT[0:64, :])
                E.v("tensor_copy", BC32f[:, l0:l0 + LH], dbcT[64:96, :])
                for dc in range(4):
                    for lg in range(LH // 512):
                        E.mm(ps[:, 0:512], wdt_bf[:, dc * 128:(dc + 1) * 128],
                             dr_bf[:, lg * 512:(lg + 1) * 512], True, True)
                        dslice = deltaT[:, dc * L + l0 + lg * 512: dc * L + l0 + (lg + 1) * 512]
                        E.v("tensor_scalar", wtmp[:, 0:512], ps[:, 0:512], cols[:, dc:dc + 1], None, ALU.add)
                        E.s("activation", wtmp[:, 512:1024], wtmp[:, 0:512], AF.Abs, 0.0, 1.0)
                        E.s("activation", wtmp[:, 512:1024], wtmp[:, 512:1024], AF.Exp, 0.0, -1.0)
                        E.s("activation", wtmp[:, 512:1024], wtmp[:, 512:1024], AF.Ln, kcol[:, 1:2], 1.0)
                        E.v("tensor_scalar_max", wtmp[:, 0:512], wtmp[:, 0:512], 0.0)
                        E.v("tensor_add", dslice, wtmp[:, 0:512], wtmp[:, 512:1024])
                    E.v("reduce_sum", sumd[:, lh * 4 + dc:lh * 4 + dc + 1],
                        deltaT[:, dc * L + l0: dc * L + l0 + LH], AX.X)
            for dc in range(4):
                E.v("tensor_add", sumd[:, 8 + dc:9 + dc], sumd[:, dc:dc + 1],
                    sumd[:, 4 + dc:5 + dc])

        with nc.sbuf_tensor("BmB", [128, LH], BF16) as BmB, \
             nc.sbuf_tensor("CbB", [128, LH], BF16) as CbB, \
             nc.sbuf_tensor("w0", [128, LH], F32) as w0, \
             nc.sbuf_tensor("w1", [128, LH], F32) as w1, \
             nc.sbuf_tensor("w2", [128, LH], F32) as w2, \
             nc.sbuf_tensor("w3", [128, LH], F32) as w3, \
             nc.sbuf_tensor("yT_s", [128, 4 * LH], F32) as yT, \
             nc.sbuf_tensor("yTb", [128, LH], BF16) as yTb:
            zbc = bass.AP(kcol, 2, [[4, 128], [0, LH]])
            for lh in range(2):
                l0 = lh * LH
                for n in range(16):
                    for (dstB, j) in ((BmB, n), (CbB, 16 + n)):
                        for lg in range(LH // 512):
                            E.mm(ps[:, 0:512], bass.AP(sel, j * 128, [[32 * 128, 32], [1, 128]]),
                                 BC32f[:, l0 + lg * 512:l0 + (lg + 1) * 512], True, True)
                            E.v("tensor_copy", dstB[:, lg * 512:(lg + 1) * 512], ps[:, 0:512])
                    for dc in range(4):
                        dsl = deltaT[:, dc * L + l0: dc * L + l0 + LH]
                        acol = A_sb[:, dc * 16 + n:dc * 16 + n + 1]
                        E.v("tensor_scalar", w0[:], dsl, acol, None, ALU.mult)
                        if lh == 0:
                            E.v("tensor_tensor_scan", w1[:], w0[:], zbc, 0.0, ALU.add, ALU.add)
                        else:
                            E.v("tensor_scalar", scr[:, 4:5], sumd[:, dc:dc + 1], acol, None, ALU.mult)
                            E.v("tensor_tensor_scan", w1[:], w0[:], zbc, scr[:, 4:5], ALU.add, ALU.add)
                        E.v("tensor_scalar", scr[:, 5:6], sumd[:, 8 + dc:9 + dc], acol, None, ALU.mult)
                        E.s("activation", w2[:], w1[:], AF.Exp, scr[:, 5:6], -1.0)
                        xn_dc = bass.AP(xnhalfT, dc * L + l0, [[4 * L, 128], [1, LH]])
                        E.v("tensor_mul", w0[:], dsl, xn_dc)
                        E.v("tensor_mul", w0[:], w0[:], BmB[:])
                        E.v("tensor_mul", w0[:], w0[:], w2[:])
                        init = 0.0 if lh == 0 else numcar[:, n * 4 + dc:n * 4 + dc + 1]
                        E.v("tensor_tensor_scan", w3[:], w0[:], zbc, init, ALU.add, ALU.add)
                        E.v("tensor_copy", numcar[:, n * 4 + dc:n * 4 + dc + 1], w3[:, LH - 1:LH])
                        E.v("tensor_scalar_add", w2[:], w2[:], 1e-12)
                        E.v("reciprocal", w2[:], w2[:])
                        E.v("tensor_mul", w3[:], w3[:], w2[:])
                        E.v("tensor_mul", w3[:], w3[:], CbB[:])
                        if n == 0:
                            E.v("tensor_copy", yT[:, dc * LH:(dc + 1) * LH], w3[:])
                        else:
                            E.v("tensor_add", yT[:, dc * LH:(dc + 1) * LH],
                                yT[:, dc * LH:(dc + 1) * LH], w3[:])
                for dc in range(4):
                    xn_dc = bass.AP(xnhalfT, dc * L + l0, [[4 * L, 128], [1, LH]])
                    E.op("vector", nc.vector.scalar_tensor_tensor, yT[:, dc * LH:(dc + 1) * LH],
                         xn_dc, cols[:, 4 + dc:5 + dc], yT[:, dc * LH:(dc + 1) * LH], ALU.mult, ALU.add)
                    E.s("copy", yTb[:], yT[:, dc * LH:(dc + 1) * LH])
                    E.dma(yT_out[dc * 128:(dc + 1) * 128, l0:l0 + LH], yTb[:])

        blk = es.enter_context(nc.Block())
        E.replay(blk)
    return nc


def p7_inmaps(x_full, IN):
    import ml_dtypes
    sel = np.zeros((32, 32, 128), np.float32)
    for j in range(32):
        sel[j, j, :] = 1.0
    identb = np.eye(128, dtype=np.float32).astype(ml_dtypes.bfloat16)
    wx = IN["L1_attn_a_W_x"]
    wdt = IN["L1_attn_a_W_dt"]
    dtb = IN["L1_attn_a_b_dt"]
    alog = IN["L1_attn_a_A_log"]
    dpar = IN["L1_attn_a_D"]
    maps = []
    for c in range(NCORE):
        b, dh = c // 2, c % 2
        dsl = slice(dh * 512, (dh + 1) * 512)
        xb = x_full.reshape(4, 4096, D)[b]
        if dh == 1:
            xb = np.concatenate([xb[:, 512:], xb[:, :512]], axis=1)
            wx_r = np.concatenate([wx[512:], wx[:512]], axis=0)
        else:
            wx_r = wx
        maps.append({
            "x": np.ascontiguousarray(xb),
            "wx": np.ascontiguousarray(wx_r),
            "wdt": np.ascontiguousarray(wdt[:, dsl]),
            "dtb": np.ascontiguousarray(dtb[dsl]),
            "alog": np.ascontiguousarray(alog[dsl]),
            "dpar": np.ascontiguousarray(dpar[dsl]),
            "sel": sel,
            "identb": identb,
        })
    return maps


_P7 = None


def run_p7(x_full, IN):
    global _P7
    if _P7 is None:
        _P7 = build_p7()
    res = run_bass_kernel_spmd(_P7, p7_inmaps(x_full, IN), core_ids=list(range(NCORE)))
    import ml_dtypes
    yT = np.zeros((4, 1024, 4096), dtype=ml_dtypes.bfloat16)
    for c in range(NCORE):
        b, dh = c // 2, c % 2
        yT[b, dh * 512:(dh + 1) * 512] = res.results[c]["yT"]
    return yT


def build_p8():
    nc = bass.Bass("TRN2", target_bir_lowering=False)
    T = TC
    yT_in = nc.dram_tensor("yT", [D, T], BF16, kind="ExternalInput").ap()
    x_in = nc.dram_tensor("x", [T, D], F32, kind="ExternalInput").ap()
    wo_in = nc.dram_tensor("wo", [D, D], F32, kind="ExternalInput").ap()
    bout_in = nc.dram_tensor("bout", [1, D], F32, kind="ExternalInput").ap()
    ones_in = nc.dram_tensor("ones1", [1, 128], F32, kind="ExternalInput").ap()
    r_out = nc.dram_tensor("r", [T, D], F32, kind="ExternalOutput").ap()

    with ExitStack() as es:
        sb = lambda n, shp, dt=F32: es.enter_context(nc.sbuf_tensor(n, shp, dt))
        S = mk_sems(nc, es)
        ones1 = sb("ones1_s", [1, 128])
        rows = sb("rows", [1, D])
        boutB = sb("boutB", [128, D])
        wstage = sb("wstage", [128, D])
        wo_bf = sb("wo_bf", [128, 8 * D], BF16)
        ytiles = sb("ytiles", [128, 8 * 128], BF16)
        xt = sb("xt", [128, D])
        outt = sb("outt", [128, D])
        ps = es.enter_context(nc.psum_tensor("ps", [128, 512], F32))
        E = Emit(nc, S)
        E.dma(ones1[:], ones_in)
        E.dma(rows[:], bout_in)
        bcast_row(E, nc, ps, boutB, rows[:], ones1[:])
        for c in range(8):
            E.dma(wstage[:], wo_in[c * 128:(c + 1) * 128, :])
            E.s("copy", wo_bf[:, c * D:(c + 1) * D], wstage[:])
        for tt in range(T // 128):
            for c in range(8):
                E.dma(ytiles[:, c * 128:(c + 1) * 128],
                      yT_in[c * 128:(c + 1) * 128, tt * 128:(tt + 1) * 128])
            for og in range(2):
                for c in range(8):
                    E.mm(ps[:, 0:512], ytiles[:, c * 128:(c + 1) * 128],
                         wo_bf[:, c * D + og * 512:c * D + og * 512 + 512],
                         c == 0, c == 7)
                E.v("tensor_copy", outt[:, og * 512:(og + 1) * 512], ps[:, 0:512])
            E.v("tensor_add", outt[:], outt[:], boutB[:])
            E.dma(xt[:], x_in[tt * 128:(tt + 1) * 128, :])
            E.v("tensor_add", outt[:], outt[:], xt[:])
            E.dma(r_out[tt * 128:(tt + 1) * 128, :], outt[:])
        blk = es.enter_context(nc.Block())
        E.replay(blk)
    return nc


_P8 = None


def run_p8(yT_full, x_full, IN):
    global _P8
    if _P8 is None:
        _P8 = build_p8()
    maps = []
    for c in range(NCORE):
        b, half = c // 2, c % 2
        maps.append({
            "yT": np.ascontiguousarray(yT_full[b][:, half * 2048:(half + 1) * 2048]),
            "x": np.ascontiguousarray(x_full[c * TC:(c + 1) * TC]),
            "wo": IN["L1_attn_a_W_out"],
            "bout": IN["L1_attn_a_b_out"].reshape(1, D),
            "ones1": np.ones((1, 128), np.float32),
        })
    res = run_bass_kernel_spmd(_P8, maps, core_ids=list(range(NCORE)))
    return np.concatenate([r["r"] for r in res.results], axis=0)


def build_p10():
    nc = bass.Bass("TRN2", target_bir_lowering=False)
    L = 4096
    QH = 2048
    SW = 5504 + 512
    x_in = nc.dram_tensor("x", [L, D], F32, kind="ExternalInput").ap()
    xq_in = nc.dram_tensor("xq", [QH, D], F32, kind="ExternalInput").ap()
    wqp_in = nc.dram_tensor("wqp", [8, D, 128], F32, kind="ExternalInput").ap()
    wkp_in = nc.dram_tensor("wkp", [8, D, 128], F32, kind="ExternalInput").ap()
    wvp_in = nc.dram_tensor("wvp", [8, D, 128], F32, kind="ExternalInput").ap()
    bqk_in = nc.dram_tensor("bqk", [128, 16], F32, kind="ExternalInput").ap()
    bvrow_in = nc.dram_tensor("bvrow", [1, D], F32, kind="ExternalInput").ap()
    wo_in = nc.dram_tensor("wo", [D, D], F32, kind="ExternalInput").ap()
    bout_in = nc.dram_tensor("bout", [1, D], F32, kind="ExternalInput").ap()
    stair_in = nc.dram_tensor("stair", [128, SW], BF16, kind="ExternalInput").ap()
    identb_in = nc.dram_tensor("identb", [128, 128], BF16, kind="ExternalInput").ap()
    ones_in = nc.dram_tensor("ones1", [1, 128], F32, kind="ExternalInput").ap()
    r_out = nc.dram_tensor("r", [QH, D], F32, kind="ExternalOutput").ap()

    with ExitStack() as es:
        sb = lambda n, shp, dt=F32: es.enter_context(nc.sbuf_tensor(n, shp, dt))
        S = mk_sems(nc, es)
        identb = sb("identb_s", [128, 128], BF16)
        ones1 = sb("ones1_s", [1, 128])
        stair = sb("stair_s", [128, SW], BF16)
        bqk = sb("bqk_s", [128, 16])
        bvB = sb("bvB", [128, D])
        boutB = bvB
        rows = sb("rows", [1, D])
        kcol = sb("kcol", [128, 4])
        scr = sb("scr", [128, 8])
        xt = sb("xt", [128, D])
        sq = sb("sq", [128, D])
        xnbf = sb("xnbf", [128, D], BF16)
        xnT = sb("xnT", [128, 8 * L], BF16)
        xqnT = sb("xqnT", [128, 8 * QH], BF16)
        attnoutT = sb("attnoutT", [128, 8 * QH], BF16)
        wstage = sb("wstage", [128, 8 * 128])
        wq_bf = sb("wq_bf", [128, 8 * 128], BF16)
        wk_bf = sb("wk_bf", [128, 8 * 128], BF16)
        wv_bf = sb("wv_bf", [128, 8 * 128], BF16)
        wo_bf = sb("wo_bf", [128, 8 * D], BF16)
        vaug = sb("vaug", [128, 32 * 130], BF16)
        QT = sb("QT", [128, QH], BF16)
        KT = sb("KT", [128, L], BF16)
        msc = sb("msc", [128, 512])
        attnT = sb("attnT", [128, 512], BF16)
        outt = sq
        osb = msc[0:64, :]
        rrow = None
        ps = es.enter_context(nc.psum_tensor("ps", [128, 512], F32))
        ps2 = es.enter_context(nc.psum_tensor("ps2", [128, 512], F32))
        psO = es.enter_context(nc.psum_tensor("psO", [128, 512], F32))
        psT = es.enter_context(nc.psum_tensor("psT", [128, 1024], BF16))
        rrow = xt[0:1, 0:512]

        E = Emit(nc, S)
        E.v("memset", kcol[:, 0:1], 1e-5)
        E.v("memset", kcol[:, 1:2], -30.0)
        E.dma(identb[:], identb_in)
        E.dma(ones1[:], ones_in)
        E.dma(stair[:], stair_in)
        E.dma(bqk[:], bqk_in)
        E.dma(rows[:, 0:D], bvrow_in)
        bcast_row(E, nc, ps, bvB, rows[:, 0:D], ones1[:])
        for hp in range(8):
            E.dma(wstage[:, 0:D], wo_in[hp * 128:(hp + 1) * 128, :])
            E.s("copy", wo_bf[:, hp * D:(hp + 1) * D], wstage[:, 0:D])
        E.v("memset", bass.AP(vaug, 64, [[32 * 130, 128], [130, 32], [65, 2], [1, 1]]), 1.0)

        for i in range(L // 128):
            dst = bass.AP(xnT, i * 128, [[8 * L, 128], [L, 8], [1, 128]])
            emit_ln_block(E, nc, x_in, i * 128, xt, sq, scr, xnbf, psT, dst, identb, kcol[:, 0:1])
        for i in range(QH // 128):
            dst = bass.AP(xqnT, i * 128, [[8 * QH, 128], [QH, 8], [1, 128]])
            emit_ln_block(E, nc, xq_in, i * 128, xt, sq, scr, xnbf, psT, dst, identb, kcol[:, 0:1])

        QG = QH // 512
        for hp in range(8):
            for (wbf, src) in ((wq_bf, wqp_in), (wk_bf, wkp_in), (wv_bf, wvp_in)):
                E.dma(bass.AP(wstage, 0, [[1024, 128], [128, 8], [1, 128]]),
                      src[hp].rearrange("(c p) o -> p c o", p=128))
                E.s("copy", wbf[:, 0:1024], wstage[:, 0:1024])
            for tt in range(L // 128):
                for c in range(8):
                    E.mm(ps[:, 0:128],
                         bass.AP(xnT, c * L + tt * 128, [[8 * L, 128], [1, 128]]),
                         wv_bf[:, c * 128:(c + 1) * 128], c == 0, c == 7)
                E.op("vector", nc.vector.tensor_tensor,
                     bass.AP(vaug, tt * 130, [[32 * 130, 128], [65, 2], [1, 64]]),
                     ps[:, 0:128], bvB[:, hp * 128:(hp + 1) * 128], ALU.add)
            for lg in range(L // 512):
                for c in range(8):
                    E.mm(ps[:, 0:512], wk_bf[:, c * 128:(c + 1) * 128],
                         bass.AP(xnT, c * L + lg * 512, [[8 * L, 128], [1, 512]]),
                         c == 0, c == 7)
                E.v("tensor_scalar", KT[:, lg * 512:(lg + 1) * 512], ps[:, 0:512],
                    bqk[:, 8 + hp:9 + hp], None, ALU.add)
            for lg in range(QH // 512):
                for c in range(8):
                    E.mm(ps[:, 0:512], wq_bf[:, c * 128:(c + 1) * 128],
                         bass.AP(xqnT, c * QH + lg * 512, [[8 * QH, 128], [1, 512]]),
                         c == 0, c == 7)
                E.v("tensor_scalar", QT[:, lg * 512:(lg + 1) * 512], ps[:, 0:512],
                    bqk[:, hp:hp + 1], None, ALU.add)
            for h in range(2):
                hb = h * 64
                for qg in range(QG):
                    qsl = slice(qg * 512, (qg + 1) * 512)
                    for kc in range(32):
                        E.mm(ps[:, 0:512],
                             KT[hb:hb + 64, kc * 128:(kc + 1) * 128],
                             QT[hb:hb + 64, qsl], True, True)
                        soff = qg * 512 - kc * 128 + 3968
                        E.v("tensor_tensor", msc[:], ps[:, 0:512],
                            stair[:, soff:soff + 512], ALU.add)
                        E.s("activation", attnT[:], msc[:], AF.Exp, kcol[:, 1:2], 0.125)
                        E.mm(psO[0:65, 0:512],
                             bass.AP(vaug, kc * 130 + h * 65, [[32 * 130, 128], [1, 65]]),
                             attnT[:], kc == 0, kc == 31, wait=True)
                    E.v("reciprocal", rrow[:], psO[64:65, 0:512])
                    E.mm(ps2[0:64, 0:512], ones1[0:1, 0:64], rrow[:], True, True)
                    E.v("tensor_copy", osb[:], psO[0:64, 0:512])
                    E.v("tensor_tensor",
                        attnoutT[hb:hb + 64, hp * QH + qg * 512:hp * QH + (qg + 1) * 512],
                        osb[:], ps2[0:64, 0:512], ALU.mult)
        E.dma(rows[:, 0:D], bout_in)
        bcast_row(E, nc, ps, boutB, rows[:, 0:D], ones1[:])
        for tt in range(QH // 128):
            for og in range(2):
                for hp in range(8):
                    E.mm(ps[:, 0:512],
                         attnoutT[:, hp * QH + tt * 128:hp * QH + (tt + 1) * 128],
                         wo_bf[:, hp * D + og * 512:hp * D + og * 512 + 512],
                         hp == 0, hp == 7)
                E.v("tensor_copy", outt[:, og * 512:(og + 1) * 512], ps[:, 0:512])
            E.v("tensor_add", outt[:], outt[:], boutB[:])
            E.dma(xt[:], xq_in[tt * 128:(tt + 1) * 128, :])
            E.v("tensor_add", outt[:], outt[:], xt[:])
            E.dma(r_out[tt * 128:(tt + 1) * 128, :], outt[:])

        blk = es.enter_context(nc.Block())
        E.replay(blk)
    return nc


def p10_inmaps(x_full, IN):
    import ml_dtypes
    identb = np.eye(128, dtype=np.float32).astype(ml_dtypes.bfloat16)
    Wq = IN["L1_attn_b_Wq"]
    Wk = IN["L1_attn_b_Wk"]
    Wv = IN["L1_attn_b_Wv"]
    pack = lambda W: np.ascontiguousarray(
        W.reshape(8, 2, D, 64).transpose(0, 2, 1, 3).reshape(8, D, 128))
    bqk = np.zeros((128, 16), np.float32)
    bqk[:, 0:8] = IN["L1_attn_b_bq"].reshape(8, 128).T
    bqk[:, 8:16] = IN["L1_attn_b_bk"].reshape(8, 128).T
    base = {
        "wqp": pack(Wq), "wkp": pack(Wk), "wvp": pack(Wv),
        "bqk": bqk,
        "bvrow": IN["L1_attn_b_bv"].reshape(1, D),
        "wo": IN["L1_attn_b_Wo"],
        "bout": IN["L1_attn_b_bo"].reshape(1, D),
        "identb": identb,
        "ones1": np.ones((1, 128), np.float32),
    }
    SW = 5504 + 512
    maps = []
    for c in range(NCORE):
        b, qh = c // 2, c % 2
        q0 = qh * 2048
        xb = x_full.reshape(4, 4096, D)[b]
        r = np.arange(128)[:, None]
        u = np.arange(SW)[None, :]
        stair = np.where(r <= (u - 3968 + q0), 0.0, -1e30).astype(np.float32)
        m = dict(base)
        m["x"] = np.ascontiguousarray(xb)
        m["xq"] = np.ascontiguousarray(xb[q0:q0 + 2048])
        m["stair"] = stair.astype(ml_dtypes.bfloat16)
        maps.append(m)
    return maps


_P10 = None


def run_p10(x_full, IN):
    global _P10
    if _P10 is None:
        _P10 = build_p10()
    res = run_bass_kernel_spmd(_P10, p10_inmaps(x_full, IN), core_ids=list(range(NCORE)))
    out = np.zeros((4, 4096, D), np.float32)
    for c in range(NCORE):
        b, qh = c // 2, c % 2
        out[b, qh * 2048:(qh + 1) * 2048] = res.results[c]["r"]
    return out.reshape(-1, D)


def build_p9a():
    nc = bass.Bass("TRN2", target_bir_lowering=False)
    WN = 2304
    OWN = 2048
    xw_in = nc.dram_tensor("xw", [WN + 1, D], F32, kind="ExternalInput").ap()
    wqkv_in = nc.dram_tensor("wqkv", [D, 3 * 1024], F32, kind="ExternalInput").ap()
    wo_in = nc.dram_tensor("wo", [1024, D], F32, kind="ExternalInput").ap()
    qs_in = nc.dram_tensor("qs", [128, 2], F32, kind="ExternalInput").ap()
    cosS_in = nc.dram_tensor("cosS", [128, WN], BF16, kind="ExternalInput").ap()
    sinS_in = nc.dram_tensor("sinS", [128, WN], BF16, kind="ExternalInput").ap()
    cosI_in = nc.dram_tensor("cosI", [128, WN], BF16, kind="ExternalInput").ap()
    sinI_in = nc.dram_tensor("sinI", [128, WN], BF16, kind="ExternalInput").ap()
    lmask_in = nc.dram_tensor("lmask", [4, 128, 512], F32, kind="ExternalInput").ap()
    rot_in = nc.dram_tensor("rot", [128, 128], BF16, kind="ExternalInput").ap()
    b2_in = nc.dram_tensor("b2sel", [128, 2], F32, kind="ExternalInput").ap()
    b2t_in = nc.dram_tensor("b2t", [2, 128], F32, kind="ExternalInput").ap()
    identb_in = nc.dram_tensor("identb", [128, 128], BF16, kind="ExternalInput").ap()
    r_out = nc.dram_tensor("r", [OWN, D], F32, kind="ExternalOutput").ap()

    with ExitStack() as es:
        sb = lambda n, shp, dt=F32: es.enter_context(nc.sbuf_tensor(n, shp, dt))
        S = mk_sems(nc, es)
        identb = sb("identb_s", [128, 128], BF16)
        rot = sb("rot_s", [128, 128], BF16)
        b2 = sb("b2_s", [128, 2])
        b2t = sb("b2t_s", [2, 128])
        qs = sb("qs_s", [128, 2])
        cosS = sb("cosS_s", [128, WN], BF16)
        sinS = sb("sinS_s", [128, WN], BF16)
        cosI = sb("cosI_s", [128, WN], BF16)
        sinI = sb("sinI_s", [128, WN], BF16)
        lmask = sb("lmask_s", [128, 4 * 512])
        scr = sb("scr", [128, 8])
        xt = sb("xt", [128, D])
        shbf = sb("shbf", [128, D], BF16)
        shxT = sb("shxT", [128, 8 * WN], BF16)
        V = sb("V", [128, 18 * 1024], BF16)
        attnoutT = sb("attnoutT", [128, 8 * OWN], BF16)
        wstage = sb("wstage", [128, 6 * 1024])
        wq_bf = sb("wq_bf", [128, 8 * 128], BF16)
        wk_bf = sb("wk_bf", [128, 8 * 128], BF16)
        wv_bf = sb("wv_bf", [128, 8 * 128], BF16)
        wo_bf = sb("wo_bf", [128, 8 * D], BF16)
        QT = sb("QT", [128, WN], BF16)
        KT = sb("KT", [128, WN], BF16)
        sqv = sb("sqv", [128, 512])
        qsb = sb("qsb", [128, 512])
        t1 = sb("t1", [128, 512])
        nsb = sb("nsb", [2, 512])
        msc = sb("msc", [128, 512])
        attn = sb("attn", [128, 512], BF16)
        attnT = sb("attnT", [128, 512], BF16)
        ps = es.enter_context(nc.psum_tensor("ps", [128, 512], F32))
        ps2 = es.enter_context(nc.psum_tensor("ps2", [128, 512], F32))
        psO = es.enter_context(nc.psum_tensor("psO", [128, 512], F32))
        psT = es.enter_context(nc.psum_tensor("psT", [128, 1024], BF16))

        E = Emit(nc, S)
        E.dma(identb[:], identb_in)
        E.dma(rot[:], rot_in)
        E.dma(b2[:], b2_in)
        E.dma(b2t[:], b2t_in)
        E.dma(qs[:], qs_in)
        E.dma(cosS[:], cosS_in)
        E.dma(sinS[:], sinS_in)
        E.dma(cosI[:], cosI_in)
        E.dma(sinI[:], sinI_in)
        E.dma(bass.AP(lmask, 0, [[4 * 512, 128], [512, 4], [1, 512]]),
              lmask_in.rearrange("m p j -> p m j"))
        for hp in range(8):
            E.dma(wstage[:, 0:D], wo_in[hp * 128:(hp + 1) * 128, :])
            E.s("copy", wo_bf[:, hp * D:(hp + 1) * D], wstage[:, 0:D])

        for i in range(WN // 128):
            E.dma(xt[:, 0:512], xw_in[1 + i * 128:1 + (i + 1) * 128, 0:512])
            E.dma(xt[:, 512:1024], xw_in[i * 128:(i + 1) * 128, 512:1024])
            E.s("copy", shbf[:], xt[:])
            for dc in range(8):
                E.t("transpose", psT[:, dc * 128:(dc + 1) * 128], shbf[:, dc * 128:(dc + 1) * 128],
                    identb[:], wait=(dc == 0))
            E.v("tensor_copy", bass.AP(shxT, i * 128, [[8 * WN, 128], [WN, 8], [1, 128]]), psT[:])

        markW = [0, 0]
        for hp in range(8):
            par = hp % 2
            base = par * 3 * 1024
            wa = markW[par]
            for k, sec in enumerate((0, 1, 2)):
                col0 = sec * 1024 + hp * 128
                E.load(bass.AP(wstage, base + k * 1024, [[6 * 1024, 128], [128, 8], [1, 128]]),
                       wqkv_in.rearrange("(c p) o -> p c o", p=128)[:, :, col0:col0 + 128], wa)
            for k, wbf in enumerate((wq_bf, wk_bf, wv_bf)):
                E.s("copy", wbf[:, 0:1024],
                    bass.AP(wstage, base + k * 1024, [[6 * 1024, 128], [1, 1024]]))
            markW[par] = E.mark()
            for tt in range(WN // 128):
                for c in range(8):
                    E.mm(ps[:, 0:128],
                         bass.AP(shxT, c * WN + tt * 128, [[8 * WN, 128], [1, 128]]),
                         wv_bf[:, c * 128:(c + 1) * 128], c == 0, c == 7)
                E.v("tensor_copy",
                    bass.AP(V, tt * 1024 + hp * 128, [[18 * 1024, 128], [1, 128]]),
                    ps[:, 0:128])
            for (wbf, QKT, scol, cosX, sinX) in ((wq_bf, QT, 0, cosS, sinS),
                                                 (wk_bf, KT, 1, cosI, sinI)):
                for lg in range((WN + 511) // 512):
                    j0 = lg * 512
                    w = min(512, WN - j0)
                    for c in range(8):
                        E.mm(ps[:, 0:w], wbf[:, c * 128:(c + 1) * 128],
                             bass.AP(shxT, c * WN + j0, [[8 * WN, 128], [1, w]]),
                             c == 0, c == 7)
                    E.s("activation", sqv[:, 0:w], ps[:, 0:w], AF.Square, 0.0, 1.0)
                    E.mm(ps2[0:2, 0:w], b2[:], sqv[:, 0:w], True, True)
                    E.s("activation", nsb[:, 0:w], ps2[0:2, 0:w], AF.Sqrt, 0.0, 1.0)
                    E.v("tensor_scalar_max", nsb[:, 0:w], nsb[:, 0:w], 1e-12)
                    E.v("reciprocal", nsb[:, 0:w], nsb[:, 0:w])
                    E.mm(ps2[:, 0:w], b2t[:], nsb[:, 0:w], True, True)
                    E.v("tensor_copy", qsb[:, 0:w], ps[:, 0:w])
                    E.v("tensor_mul", qsb[:, 0:w], qsb[:, 0:w], ps2[:, 0:w])
                    E.v("tensor_scalar", qsb[:, 0:w], qsb[:, 0:w], qs[:, scol:scol + 1], None, ALU.mult)
                    E.s("copy", attn[:, 0:w], qsb[:, 0:w])
                    E.mm(ps2[:, 0:w], rot[:], attn[:, 0:w], True, True)
                    E.v("tensor_tensor", t1[:, 0:w], qsb[:, 0:w], cosX[:, j0:j0 + w], ALU.mult)
                    E.v("tensor_tensor", qsb[:, 0:w], ps2[:, 0:w], sinX[:, j0:j0 + w], ALU.mult)
                    E.v("tensor_tensor", QKT[:, j0:j0 + w], t1[:, 0:w], qsb[:, 0:w], ALU.add)
            for h in range(2):
                hb = h * 64
                for wdw in range(8):
                    jb0 = wdw * 256
                    qb0 = 256 + wdw * 256
                    for qt in range(2):
                        q0w = qb0 + qt * 128
                        E.mm(ps[:, 0:512],
                             QT[hb:hb + 64, q0w:q0w + 128],
                             KT[hb:hb + 64, jb0:jb0 + 512], True, True)
                        mi = (0 if (wdw == 0) else 2) + qt
                        E.v("tensor_tensor", msc[:], ps[:, 0:512],
                            lmask[:, mi * 512:(mi + 1) * 512], ALU.add)
                        E.v("reduce_max", scr[:, 0:1], msc[:], AX.X)
                        E.v("tensor_scalar_mul", scr[:, 1:2], scr[:, 0:1], -8.0)
                        E.s("activation", msc[:], msc[:], AF.Exp, scr[:, 1:2], 8.0,
                            accum_out=scr[:, 2:3])
                        E.v("reciprocal", scr[:, 3:4], scr[:, 2:3])
                        E.v("tensor_scalar", attn[:], msc[:], scr[:, 3:4], None, ALU.mult)
                        for jc in range(4):
                            E.t("transpose", psT[:, jc * 128:(jc + 1) * 128],
                                attn[:, jc * 128:(jc + 1) * 128], identb[:], wait=(jc == 0))
                        E.v("tensor_copy", attnT[:], psT[:, 0:512])
                        for jc in range(4):
                            vtt = 2 * wdw + jc
                            E.mm(psO[0:64, 0:128],
                                 bass.AP(V, vtt * 1024 + hp * 128 + hb, [[18 * 1024, 128], [1, 64]]),
                                 attnT[:, jc * 128:(jc + 1) * 128], jc == 0, jc == 3)
                        E.v("tensor_copy",
                            attnoutT[hb:hb + 64,
                                     hp * OWN + wdw * 256 + qt * 128: hp * OWN + wdw * 256 + (qt + 1) * 128],
                            psO[0:64, 0:128])
        for tt in range(OWN // 128):
            for og in range(2):
                for hp in range(8):
                    E.mm(ps[:, 0:512],
                         attnoutT[:, hp * OWN + tt * 128:hp * OWN + (tt + 1) * 128],
                         wo_bf[:, hp * D + og * 512:hp * D + og * 512 + 512],
                         hp == 0, hp == 7)
                E.v("tensor_copy", xt[:, og * 512:(og + 1) * 512], ps[:, 0:512])
            E.dma(t1[:, 0:512], xw_in[257 + tt * 128:257 + (tt + 1) * 128, 0:512])
            E.dma(qsb[:, 0:512], xw_in[257 + tt * 128:257 + (tt + 1) * 128, 512:1024])
            E.v("tensor_add", xt[:, 0:512], xt[:, 0:512], t1[:, 0:512])
            E.v("tensor_add", xt[:, 512:1024], xt[:, 512:1024], qsb[:, 0:512])
            E.dma(r_out[tt * 128:(tt + 1) * 128, :], xt[:])

        blk = es.enter_context(nc.Block())
        E.replay(blk)
    return nc


def build_p9b():
    nc = bass.Bass("TRN2", target_bir_lowering=False)
    T = 2048
    OC = 22
    xw_in = nc.dram_tensor("xw", [T + 1, D], F32, kind="ExternalInput").ap()
    w1p_in = nc.dram_tensor("w1p", [D, 2 * 2816], F32, kind="ExternalInput").ap()
    b1p_in = nc.dram_tensor("b1p", [128, 2 * OC], F32, kind="ExternalInput").ap()
    w2p_in = nc.dram_tensor("w2p", [2816, D], F32, kind="ExternalInput").ap()
    b2row_in = nc.dram_tensor("b2row", [1, D], F32, kind="ExternalInput").ap()
    g_in = nc.dram_tensor("lng", [1, D], F32, kind="ExternalInput").ap()
    bln_in = nc.dram_tensor("lnb", [1, D], F32, kind="ExternalInput").ap()
    identb_in = nc.dram_tensor("identb", [128, 128], BF16, kind="ExternalInput").ap()
    identf_in = nc.dram_tensor("identf", [128, 128], F32, kind="ExternalInput").ap()
    ones_in = nc.dram_tensor("ones1", [1, 128], F32, kind="ExternalInput").ap()
    r_out = nc.dram_tensor("r", [T, D], F32, kind="ExternalOutput").ap()

    with ExitStack() as es:
        sb = lambda n, shp, dt=F32: es.enter_context(nc.sbuf_tensor(n, shp, dt))
        S = mk_sems(nc, es)
        identb = sb("identb_s", [128, 128], BF16)
        identf = sb("identf_s", [128, 128], F32)
        ones1 = sb("ones1_s", [1, 128])
        b1p = sb("b1p_s", [128, 2 * OC])
        kcol = sb("kcol", [128, 4])
        scr = sb("scr", [128, 8])
        rows = sb("rows", [1, D])
        gB = sb("gB", [128, D])
        blnB = sb("blnB", [128, D])
        b2B = sb("b2B", [128, D])
        xt = sb("xt", [128, D])
        sq = sb("sq", [128, D])
        xnbf = sb("xnbf", [128, D], BF16)
        xnT = sb("xnT", [128, 8 * T], BF16)
        outacc = sb("outacc", [128, 8 * T])
        w1a = sb("w1a", [128, 8 * 128], BF16)
        w1g = sb("w1g", [128, 8 * 128], BF16)
        w2c = sb("w2c", [128, D], BF16)
        wstage = sb("wstage", [128, 6 * 1024])
        asb = sb("asb", [128, 512])
        ggs = sb("ggs", [128, 512])
        h2T = sb("h2T", [128, 4 * 512], BF16)
        ps = es.enter_context(nc.psum_tensor("ps", [128, 512], F32))
        ps2 = es.enter_context(nc.psum_tensor("ps2", [128, 512], F32))
        psr = es.enter_context(nc.psum_tensor("psr", [128, 512], F32))
        psT = es.enter_context(nc.psum_tensor("psT", [128, 1024], BF16))
        psF = es.enter_context(nc.psum_tensor("psF", [128, 512], F32))

        E = Emit(nc, S)
        E.v("memset", kcol[:, 0:1], 1e-5)
        E.dma(identb[:], identb_in)
        E.dma(identf[:], identf_in)
        E.dma(ones1[:], ones_in)
        E.dma(b1p[:], b1p_in)
        E.dma(rows[:], g_in)
        bcast_row(E, nc, ps, gB, rows[:], ones1[:])
        E.dma(rows[:], bln_in)
        bcast_row(E, nc, ps, blnB, rows[:], ones1[:])
        E.dma(rows[:], b2row_in)
        bcast_row(E, nc, ps, b2B, rows[:], ones1[:])

        for i in range(T // 128):
            E.dma(xt[:, 0:512], xw_in[1 + i * 128:1 + (i + 1) * 128, 0:512])
            E.dma(xt[:, 512:1024], xw_in[i * 128:(i + 1) * 128, 512:1024])
            E.v("reduce_sum", scr[:, 0:1], xt[:], AX.X)
            E.s("mul", scr[:, 1:2], scr[:, 0:1], 1.0 / D)
            E.v("tensor_scalar", xt[:], xt[:], scr[:, 1:2], None, ALU.subtract)
            E.s("activation", sq[:], xt[:], AF.Square, 0.0, 1.0, accum_out=scr[:, 0:1])
            E.s("activation", scr[:, 2:3], scr[:, 0:1], AF.Ln, kcol[:, 0:1], 1.0 / D)
            E.s("activation", scr[:, 3:4], scr[:, 2:3], AF.Exp, 0.0, -0.5)
            E.v("tensor_scalar", xt[:], xt[:], scr[:, 3:4], None, ALU.mult)
            E.v("tensor_mul", xt[:], xt[:], gB[:])
            E.v("tensor_add", xt[:], xt[:], blnB[:])
            E.s("copy", xnbf[:], xt[:])
            for dc in range(8):
                E.t("transpose", psT[:, dc * 128:(dc + 1) * 128], xnbf[:, dc * 128:(dc + 1) * 128],
                    identb[:], wait=(dc == 0))
            E.v("tensor_copy", bass.AP(xnT, i * 128, [[8 * T, 128], [T, 8], [1, 128]]), psT[:])

        markW = [0, 0]
        for oc in range(OC):
            par = oc % 2
            base = par * 3 * 1024
            wa = markW[par]
            srcs = [
                w1p_in.rearrange("(c p) o -> p c o", p=128)[:, :, oc * 128:(oc + 1) * 128],
                w1p_in.rearrange("(c p) o -> p c o", p=128)[:, :, 2816 + oc * 128:2816 + (oc + 1) * 128],
            ]
            for k, src in enumerate(srcs):
                E.load(bass.AP(wstage, base + k * 1024, [[6 * 1024, 128], [128, 8], [1, 128]]), src, wa)
            E.load(bass.AP(wstage, base + 2 * 1024, [[6 * 1024, 128], [1, D]]),
                   w2p_in[oc * 128:(oc + 1) * 128, :], wa)
            E.s("copy", w1a[:], bass.AP(wstage, base, [[6 * 1024, 128], [1, 1024]]))
            E.s("copy", w1g[:], bass.AP(wstage, base + 1024, [[6 * 1024, 128], [1, 1024]]))
            E.s("copy", w2c[:], bass.AP(wstage, base + 2 * 1024, [[6 * 1024, 128], [1, 1024]]))
            markW[par] = E.mark()
            for lg in range(T // 512):
                tsl = slice(lg * 512, (lg + 1) * 512)
                for c in range(8):
                    E.mm(ps[:, 0:512], w1a[:, c * 128:(c + 1) * 128],
                         bass.AP(xnT, c * T + lg * 512, [[8 * T, 128], [1, 512]]),
                         c == 0, c == 7)
                for c in range(8):
                    E.mm(ps2[:, 0:512], w1g[:, c * 128:(c + 1) * 128],
                         bass.AP(xnT, c * T + lg * 512, [[8 * T, 128], [1, 512]]),
                         c == 0, c == 7)
                E.v("tensor_scalar", asb[:], ps[:, 0:512], b1p[:, oc:oc + 1], None, ALU.add)
                E.s("activation", ggs[:], ps2[:, 0:512], AF.Gelu, b1p[:, OC + oc:OC + oc + 1], 1.0)
                E.v("tensor_tensor", h2T[:, lg * 512:(lg + 1) * 512], asb[:], ggs[:], ALU.mult)
            for lg in range(T // 512):
                for d2c in range(8):
                    E.mm(psr[:, 0:512], w2c[:, d2c * 128:(d2c + 1) * 128],
                         h2T[:, lg * 512:(lg + 1) * 512], True, True)
                    dst = outacc[:, d2c * T + lg * 512: d2c * T + (lg + 1) * 512]
                    if oc == 0:
                        E.v("tensor_copy", dst, psr[:, 0:512])
                    else:
                        E.v("tensor_add", dst, dst, psr[:, 0:512])

        for tt in range(T // 128):
            for half in range(2):
                for j in range(4):
                    d2c = half * 4 + j
                    E.t("transpose", psF[:, j * 128:(j + 1) * 128],
                        outacc[:, d2c * T + tt * 128: d2c * T + (tt + 1) * 128], identf[:],
                        wait=(j == 0))
                E.v("tensor_copy", xt[:, half * 512:(half + 1) * 512], psF[:, 0:512])
            E.v("tensor_add", xt[:], xt[:], b2B[:])
            E.dma(sq[:, 0:512], xw_in[1 + tt * 128:1 + (tt + 1) * 128, 0:512])
            E.dma(sq[:, 512:1024], xw_in[1 + tt * 128:1 + (tt + 1) * 128, 512:1024])
            E.v("tensor_add", xt[:], xt[:], sq[:])
            E.dma(r_out[tt * 128:(tt + 1) * 128, :], xt[:])

        blk = es.enter_context(nc.Block())
        E.replay(blk)
    return nc


_P9A = None
_P9B = None


def p9_consts():
    import ml_dtypes
    DIM_HEAD, WINDOW, n = 64, 256, 4096
    half = np.arange(0, DIM_HEAD, 2, dtype=np.float32)
    inv_freq = 1.0 / 10000.0 ** (half / DIM_HEAD)
    t = np.arange(-257 + 1 - 255, n, dtype=np.float32)
    identb = np.eye(128, dtype=np.float32).astype(ml_dtypes.bfloat16)
    identf = np.eye(128, dtype=np.float32)
    rotm = np.zeros((128, 128), np.float32)
    for hh in range(2):
        for m in range(64):
            if m < 32:
                rotm[hh * 64 + m + 32, hh * 64 + m] = -1.0
            else:
                rotm[hh * 64 + m - 32, hh * 64 + m] = 1.0
    rot = rotm.astype(ml_dtypes.bfloat16)
    b2 = np.zeros((128, 2), np.float32)
    b2[0:64, 0] = 1.0
    b2[64:128, 1] = 1.0
    b2t = np.ascontiguousarray(b2.T)
    masks = np.zeros((4, 128, 512), np.float32)
    for variant in range(2):
        for qt in range(2):
            i = (np.arange(128) + qt * 128)[:, None]
            j2 = np.arange(512)[None, :]
            prev_ok = (i <= j2) & (j2 < 256)
            own_ok = (j2 >= 256) & ((j2 - 256) <= i)
            ok = prev_ok | own_ok
            if variant == 0:
                ok = own_ok
            masks[variant * 2 + qt] = np.where(ok, 0.0, -np.float32(np.finfo(np.float32).max))
    return identb, identf, rot, b2, b2t, masks


def p9_rope_tables(tglob):
    import ml_dtypes
    DIM_HEAD, WINDOW, n = 64, 256, 4096
    half = np.arange(0, DIM_HEAD, 2, dtype=np.float32)
    inv_freq = (1.0 / 10000.0 ** (half / DIM_HEAD)).astype(np.float32)
    freqs = tglob[:, None].astype(np.float32) * inv_freq[None, :]
    freqs = np.concatenate([freqs, freqs], -1)
    base_scale = ((half + 0.4 * DIM_HEAD) / (1.4 * DIM_HEAD)).astype(np.float32)
    power = ((tglob - n // 2) / (WINDOW // 2)).astype(np.float32)
    scale = base_scale[None, :] ** power[:, None]
    scale = np.concatenate([scale, scale], -1)
    cos, sin = np.cos(freqs), np.sin(freqs)
    cS = (cos * scale).T
    sS = (sin * scale).T
    cI = (cos / scale).T
    sI = (sin / scale).T
    f = lambda a: np.ascontiguousarray(np.tile(a, (2, 1))).astype(ml_dtypes.bfloat16)
    return f(cS), f(sS), f(cI), f(sI)


def run_p9a(x_full, IN, layer):
    global _P9A
    if _P9A is None:
        _P9A = build_p9a()
    identb, identf, rot, b2, b2t, masks = p9_consts()
    qs2 = np.zeros((128, 2), np.float32)
    qs2[:, 0] = np.tile(IN["L1_ffn_q_scale"][layer], 2)
    qs2[:, 1] = np.tile(IN["L1_ffn_k_scale"][layer], 2)
    maps = []
    for c in range(NCORE):
        b, sh = c // 2, c % 2
        q0 = sh * 2048
        xb = x_full.reshape(4, 4096, D)[b]
        xw = np.zeros((2305, D), np.float32)
        lo = q0 - 257
        src_lo = max(lo, 0)
        xw[src_lo - lo:] = xb[src_lo:q0 + 2048]
        tglob = np.arange(q0 - 256, q0 + 2048, dtype=np.float32)
        cS, sS, cI, sI = p9_rope_tables(tglob)
        maps.append({
            "xw": xw,
            "wqkv": IN["L1_ffn_Wqkv"][layer],
            "wo": IN["L1_ffn_Wo"][layer],
            "qs": qs2,
            "cosS": cS, "sinS": sS, "cosI": cI, "sinI": sI,
            "lmask": masks if sh == 0 else np.concatenate([masks[2:], masks[2:]], axis=0),
            "rot": rot, "b2sel": b2, "b2t": b2t, "identb": identb,
        })
    res = run_bass_kernel_spmd(_P9A, maps, core_ids=list(range(NCORE)))
    out = np.zeros((4, 4096, D), np.float32)
    for c in range(NCORE):
        b, sh = c // 2, c % 2
        out[b, sh * 2048:(sh + 1) * 2048] = res.results[c]["r"]
    return out.reshape(-1, D)


def run_p9b(x_full, IN, layer):
    global _P9B
    if _P9B is None:
        _P9B = build_p9b()
    identb, identf, rot, b2, b2t, masks = p9_consts()
    W1 = IN["L1_ffn_W1"][layer]
    b1 = IN["L1_ffn_b1"][layer]
    W2 = IN["L1_ffn_W2"][layer]
    w1p = np.zeros((D, 2 * 2816), np.float32)
    w1p[:, 0:2730] = W1[:, 0:2730]
    w1p[:, 2816:2816 + 2730] = W1[:, 2730:]
    b1p_flat = np.zeros(2 * 2816, np.float32)
    b1p_flat[0:2730] = b1[0:2730]
    b1p_flat[2816:2816 + 2730] = b1[2730:]
    b1p = np.zeros((128, 44), np.float32)
    b1p[:, 0:22] = b1p_flat[0:2816].reshape(22, 128).T
    b1p[:, 22:44] = b1p_flat[2816:].reshape(22, 128).T
    w2p = np.zeros((2816, D), np.float32)
    w2p[0:2730] = W2
    base = {
        "w1p": w1p, "b1p": b1p, "w2p": w2p,
        "b2row": IN["L1_ffn_b2"][layer].reshape(1, D),
        "lng": IN["L1_ffn_ln_g"][layer].reshape(1, D),
        "lnb": IN["L1_ffn_ln_b"][layer].reshape(1, D),
        "identb": identb, "identf": identf,
        "ones1": np.ones((1, 128), np.float32),
    }
    maps = []
    for c in range(NCORE):
        b, sh = c // 2, c % 2
        q0 = sh * 2048
        xb = x_full.reshape(4, 4096, D)[b]
        xw = np.zeros((2049, D), np.float32)
        lo = q0 - 1
        src_lo = max(lo, 0)
        xw[src_lo - lo:] = xb[src_lo:q0 + 2048]
        m = dict(base)
        m["xw"] = xw
        maps.append(m)
    res = run_bass_kernel_spmd(_P9B, maps, core_ids=list(range(NCORE)))
    out = np.zeros((4, 4096, D), np.float32)
    for c in range(NCORE):
        b, sh = c // 2, c % 2
        out[b, sh * 2048:(sh + 1) * 2048] = res.results[c]["r"]
    return out.reshape(-1, D)


def build_pln():
    nc = bass.Bass("TRN2", target_bir_lowering=False)
    x_in = nc.dram_tensor("x", [TC, D], F32, kind="ExternalInput").ap()
    s_out = nc.dram_tensor("s", [TC, D], F32, kind="ExternalOutput").ap()
    with ExitStack() as es:
        sb = lambda n, shp, dt=F32: es.enter_context(nc.sbuf_tensor(n, shp, dt))
        S = mk_sems(nc, es)
        kcol = sb("kcol", [128, 4])
        scr = sb("scr", [128, 8])
        xt = sb("xt", [128, D])
        sq = sb("sq", [128, D])
        E = Emit(nc, S)
        E.v("memset", kcol[:, 0:1], 1e-5)
        for i in range(TC // 128):
            E.dma(xt[:], x_in[i * 128:(i + 1) * 128, :])
            E.v("reduce_sum", scr[:, 0:1], xt[:], AX.X)
            E.s("mul", scr[:, 1:2], scr[:, 0:1], 1.0 / D)
            E.v("tensor_scalar", xt[:], xt[:], scr[:, 1:2], None, ALU.subtract)
            E.s("activation", sq[:], xt[:], AF.Square, 0.0, 1.0, accum_out=scr[:, 0:1])
            E.s("activation", scr[:, 2:3], scr[:, 0:1], AF.Ln, kcol[:, 0:1], 1.0 / D)
            E.s("activation", scr[:, 3:4], scr[:, 2:3], AF.Exp, 0.0, -0.5)
            E.v("tensor_scalar", xt[:], xt[:], scr[:, 3:4], None, ALU.mult)
            E.dma(s_out[i * 128:(i + 1) * 128, :], xt[:])
        blk = es.enter_context(nc.Block())
        E.replay(blk)
    return nc


def build_padd():
    nc = bass.Bass("TRN2", target_bir_lowering=False)
    a_in = nc.dram_tensor("a", [TC, D], F32, kind="ExternalInput").ap()
    b_in = nc.dram_tensor("b", [TC, D], F32, kind="ExternalInput").ap()
    r_out = nc.dram_tensor("r", [TC, D], F32, kind="ExternalOutput").ap()
    with ExitStack() as es:
        sb = lambda n, shp, dt=F32: es.enter_context(nc.sbuf_tensor(n, shp, dt))
        S = mk_sems(nc, es)
        at = sb("at", [128, D])
        bt = sb("bt", [128, D])
        E = Emit(nc, S)
        for i in range(TC // 128):
            E.dma(at[:], a_in[i * 128:(i + 1) * 128, :])
            E.dma(bt[:], b_in[i * 128:(i + 1) * 128, :])
            E.v("tensor_add", at[:], at[:], bt[:])
            E.dma(r_out[i * 128:(i + 1) * 128, :], at[:])
        blk = es.enter_context(nc.Block())
        E.replay(blk)
    return nc


_PLN = None
_PADD = None


def run_pln(x_full):
    global _PLN
    if _PLN is None:
        _PLN = build_pln()
    maps = [{"x": np.ascontiguousarray(x_full[c * TC:(c + 1) * TC])} for c in range(NCORE)]
    res = run_bass_kernel_spmd(_PLN, maps, core_ids=list(range(NCORE)))
    return np.concatenate([r["s"] for r in res.results], axis=0)


def run_padd(a_full, b_full):
    global _PADD
    if _PADD is None:
        _PADD = build_padd()
    maps = [{"a": np.ascontiguousarray(a_full[c * TC:(c + 1) * TC]),
             "b": np.ascontiguousarray(b_full[c * TC:(c + 1) * TC])} for c in range(NCORE)]
    res = run_bass_kernel_spmd(_PADD, maps, core_ids=list(range(NCORE)))
    return np.concatenate([r["r"] for r in res.results], axis=0)


def kernel(**inputs):
    IN = inputs
    x = np.asarray(inputs["L0_attn_a_x"], np.float32)
    r = np.ascontiguousarray(x.reshape(-1, D))

    y0 = run_p1(x, IN["L0_attn_a_x_proj_w"], IN["L0_attn_a_dt_proj_w"],
                IN["L0_attn_a_dt_proj_b"], IN["L0_attn_a_A_log"], IN["L0_attn_a_D_param"])
    r = run_p2(y0.reshape(-1, D), r, IN["L0_attn_a_ln_a"], IN["L0_attn_a_ln_b"])
    r = run_moe(r, IN)
    r = run_p6(r, IN)
    r = run_moe(r, IN)

    yT = run_p7(r, IN)
    r = run_p8(yT, r, IN)
    for _rep in range(2):
        s = run_pln(r)
        for i in range(6):
            s = run_p9a(s, IN, i)
            s = run_p9b(s, IN, i)
        r = run_padd(r, s)
        if _rep == 0:
            r = run_p10(r, IN)
    return r.reshape(4, 4096, D)
